# version 29
# baseline (speedup 1.0000x reference)
import numpy as np
import ml_dtypes
from contextlib import ExitStack

import concourse.bass as bass
import concourse.mybir as mybir
from concourse.bass_utils import run_bass_kernel_spmd
from concourse.bass_core_rust import IndirectOffsetOnAxis

F32 = mybir.dt.float32
BF16 = mybir.dt.bfloat16
U32 = mybir.dt.uint32
AF = mybir.ActivationFunctionType
ALU = mybir.AluOpType
AX = mybir.AxisListType

NCORES = 8
TOK = 2048
D = 2048
H, DH = 16, 128
NCH = 16
KC = 16
PCAP = 384
NSLOT = 8 * PCAP
NPREV = 3
BIG = 1.0e6
EPS = 1e-5

ENG = {"pe": "tensor", "act": "scalar", "dve": "vector", "io": "sync", "gp": "gpsimd"}
DEF_SEM = {"pe": "spe", "act": "sac", "dve": "sdv", "io": None, "gp": None}


class Rec:
    __slots__ = ("writers", "readers", "name")

    def __init__(self, name, seed=()):
        self.name = name
        self.writers = list(seed)
        self.readers = []

    def toks(self):
        return self.writers + self.readers


class KB:

    def __init__(self):
        self.nc = bass.Bass("TRN2", target_bir_lowering=False, debug=False)
        self.sems = {}
        self.waited = {}
        for nm in ("spe", "sac", "sdv"):
            self.sems[nm] = [self.nc.alloc_semaphore(name=nm), 0]

    def lane(self, nm):
        if nm not in self.sems:
            self.sems[nm] = [self.nc.alloc_semaphore(name=nm), 0]
        return nm

    def _wait(self, engname, toks):
        eng = getattr(self.nc, ENG[engname])
        best = {}
        for (nm, val) in toks:
            if val > best.get(nm, 0):
                best[nm] = val
        for nm, val in best.items():
            key = (engname, nm)
            if self.waited.get(key, 0) >= val:
                continue
            eng.wait_ge(self.sems[nm][0], val)
            self.waited[key] = val

    def emit(self, engname, fns, ins=(), outs=(), sem=None, last_only=False):
        toks = []
        for r in ins:
            toks += r.writers
        for r in outs:
            toks += r.toks()
        self._wait(engname, toks)
        eng = getattr(self.nc, ENG[engname])
        instr = None
        for fn in fns:
            instr = fn(eng)
        if sem is None:
            sem = DEF_SEM[engname]
            assert sem is not None, f"DMA op on {engname} must specify a lane"
        else:
            self.lane(sem)
        k = 1 if sem in ("spe", "sac", "sdv", "cc") else 16
        rec = self.sems[sem]
        instr.then_inc(rec[0], k)
        rec[1] += k
        tok = (sem, rec[1])
        for r in ins:
            r.readers.append(tok)
        for r in outs:
            r.writers = [tok]
            r.readers = []
        return tok

    def op(self, engname, fn, ins=(), outs=(), sem=None):
        return self.emit(engname, [fn], ins=ins, outs=outs, sem=sem)


def build_program():
    kb = KB()
    nc = kb.nc
    ctx = ExitStack()
    en = ctx.enter_context
    seeds = []

    ap = lambda t: t.ap()
    x_own = ap(nc.dram_tensor("x_own", [TOK, D], F32, kind="ExternalInput"))
    Wq_d = ap(nc.dram_tensor("wq", [128, KC * D], F32, kind="ExternalInput"))
    Wv_d = ap(nc.dram_tensor("wv", [128, KC * D], F32, kind="ExternalInput"))
    Wg_d = ap(nc.dram_tensor("wg", [128, KC * D], F32, kind="ExternalInput"))
    Wo_d = ap(nc.dram_tensor("wo", [128, KC * D], F32, kind="ExternalInput"))
    w1_d = ap(nc.dram_tensor("w1", [8, 128, KC * D], F32, kind="ExternalInput"))
    w2_d = ap(nc.dram_tensor("w2", [8, 128, KC * D], F32, kind="ExternalInput"))
    gwT_d = ap(nc.dram_tensor("gwt", [128, KC * 8], F32, kind="ExternalInput"))
    beta_d = ap(nc.dram_tensor("beta", [1, H], F32, kind="ExternalInput"))
    lng_d = ap(nc.dram_tensor("lng", [1, D], F32, kind="ExternalInput"))
    lnb_d = ap(nc.dram_tensor("lnb", [1, D], F32, kind="ExternalInput"))
    ident_d = ap(nc.dram_tensor("ident", [128, 128], BF16, kind="ExternalInput"))
    tri16_d = ap(nc.dram_tensor("tri16", [128, H * 128], F32, kind="ExternalInput"))
    mask16_d = ap(nc.dram_tensor("mask16", [128, H * 128], BF16, kind="ExternalInput"))
    slt_d = ap(nc.dram_tensor("slt", [128, 128], BF16, kind="ExternalInput"))
    iota_d = ap(nc.dram_tensor("iota", [128, 1], F32, kind="ExternalInput"))
    iota8_d = ap(nc.dram_tensor("iota8", [1, 8], F32, kind="ExternalInput"))
    tplus1_d = ap(nc.dram_tensor("tplus1", [1, H * 128], F32, kind="ExternalInput"))
    e8_d = ap(nc.dram_tensor("e8", [8, 1], F32, kind="ExternalInput"))
    m8_d = ap(nc.dram_tensor("m8", [8, 1], F32, kind="ExternalInput"))
    y_out = ap(nc.dram_tensor("y_out", [TOK, D], F32, kind="ExternalOutput"))
    u_dram_t = nc.dram_tensor("u_dram", [1, D], F32, kind="Internal")
    u_all_t = nc.dram_tensor("u_all", [8, D], F32, kind="Internal")
    u_dram = u_dram_t.ap()
    u_all = u_all_t.ap()

    s_xlnT = ap(nc.dram_tensor("s_xlnt", [NCH, 128, D], BF16, kind="Internal"))
    s_v = ap(nc.dram_tensor("s_v", [NCH, 128, D], BF16, kind="Internal"))
    s_g = ap(nc.dram_tensor("s_g", [NCH, 128, D], BF16, kind="Internal"))
    s_x1 = ap(nc.dram_tensor("s_x1", [NCH, 128, D], BF16, kind="Internal"))
    s_disp = ap(nc.dram_tensor("s_disp", [NSLOT, D], BF16, kind="Internal"))

    s_eout = ap(nc.dram_tensor("s_eout", [NSLOT, D], BF16, kind="Internal"))

    r_sxlnT = [Rec(f"sxlnT{c}") for c in range(NCH)]
    r_sv = [Rec(f"sv{c}") for c in range(NCH)]
    r_sg = [Rec(f"sg{c}") for c in range(NCH)]
    r_sx1 = [Rec(f"sx1{c}") for c in range(NCH)]
    r_disp = Rec("s_disp")
    rsc = Rec("sc_chain")
    rgc = Rec("gc_chain")
    r_eout = Rec("s_eout")

    IOA = IndirectOffsetOnAxis

    def SA(stack, name, shape, dt):
        t = stack.enter_context(nc.sbuf_tensor(name, shape, dt))
        return t, Rec(name, seed=seeds)

    attn_ctx = ExitStack()
    WA, rWA = SA(ctx, "WA", [128, KC * D], BF16)
    WB, rWB = SA(ctx, "WB", [128, KC * D], BF16)
    rWAq = [Rec(f"WAq{i}") for i in range(4)]
    rWBq = [Rec(f"WBq{i}") for i in range(4)]
    ident, rident = SA(ctx, "identsb", [128, 128], BF16)
    slt, rslt = SA(ctx, "sltsb", [128, 128], BF16)
    gwT, rgwT = SA(ctx, "gwT", [128, KC * 8], BF16)
    iota, riota = SA(ctx, "iotasb", [128, 1], F32)
    iota8bc, riota8 = SA(ctx, "iota8bc", [128, 8], F32)
    ones8, rones8 = SA(ctx, "ones8", [128, 8], F32)
    onesr_f, ronesr = SA(ctx, "onesr_f", [1, 128], F32)
    onesc_bf, ronesc = SA(ctx, "onesc_bf", [128, 1], BF16)
    lnl_bc, rlnl = SA(ctx, "lnl_bc", [128, H], F32)
    wdec_all, rwdec = SA(ctx, "wdec_all", [128, NCH * H], BF16)
    u_row, ru_row = SA(ctx, "u_row", [1, D], BF16)
    e8sb, re8 = SA(ctx, "e8sb", [8, 1], F32)
    m8sb, rm8 = SA(ctx, "m8sb", [8, 1], F32)
    w8f, rw8f = SA(ctx, "w8f", [8, H], F32)
    w8b, rw8b = SA(ctx, "w8b", [8, H], BF16)
    wtok, rwtok = SA(ctx, "wtok", [128, NCH], F32)
    slots, rslots = SA(ctx, "slots", [128, NCH], U32)
    runcnt, rruncnt = SA(ctx, "runcnt", [1, 8], F32)
    cnt8, rcnt8 = SA(ctx, "cnt8", [1, 8], F32)
    xbuf0, rx0 = SA(ctx, "xbuf0", [128, D], BF16)
    c_s1, rs1 = SA(ctx, "c_s1", [128, 1], F32)
    c_s2, rs2 = SA(ctx, "c_s2", [128, 1], F32)
    c_mu, rmu = SA(ctx, "c_mu", [128, 1], F32)
    c_msq, rmsq = SA(ctx, "c_msq", [128, 1], F32)
    c_var, rvar = SA(ctx, "c_var", [128, 1], F32)
    c_std, rstd_ = SA(ctx, "c_std", [128, 1], F32)
    c_rstd, rrstd = SA(ctx, "c_rstd", [128, 1], F32)
    c_nmr, rnmr = SA(ctx, "c_nmr", [128, 1], F32)
    lg8a, rlg8a = SA(ctx, "lg8a", [128, 8], F32)
    lg8b, rlg8b = SA(ctx, "lg8b", [128, 8], F32)
    t8a, rt8a = SA(ctx, "t8a", [128, 8], F32)
    t8b, rt8b = SA(ctx, "t8b", [128, 8], F32)
    oh8, roh8 = SA(ctx, "oh8", [128, 8], F32)
    oh8b, roh8b = SA(ctx, "oh8b", [128, 8], BF16)
    pexp8, rpexp8 = SA(ctx, "pexp8", [128, 8], F32)
    c_lmax, rlmax = SA(ctx, "c_lmax", [128, 1], F32)
    c_nlmax, rnlmax = SA(ctx, "c_nlmax", [128, 1], F32)
    c_sexp, rsexp = SA(ctx, "c_sexp", [128, 1], F32)
    c_rsum, rrsum = SA(ctx, "c_rsum", [128, 1], F32)
    c_ptop, rptop = SA(ctx, "c_ptop", [128, 1], F32)
    c_w, rcw = SA(ctx, "c_w", [128, 1], F32)
    c_rk1, rrk1 = SA(ctx, "c_rk1", [128, 1], F32)
    c_rk2, rrk2 = SA(ctx, "c_rk2", [128, 1], F32)
    c_rank, rrank = SA(ctx, "c_rank", [128, 1], F32)
    c_keep, rkeep = SA(ctx, "c_keep", [128, 1], F32)
    c_et, rcet = SA(ctx, "c_et", [128, 1], F32)
    c_sf, rcsf = SA(ctx, "c_sf", [128, 1], F32)
    c_sk, rcsk = SA(ctx, "c_sk", [128, 1], F32)
    c_bigk, rbigk = SA(ctx, "c_bigk", [128, 1], F32)
    c_sm, rcsm = SA(ctx, "c_sm", [128, 1], F32)
    c_ecol, recol = SA(ctx, "c_ecol", [128, 1], F32)
    c_mubf, rmubf = SA(ctx, "c_mubf", [128, 1], BF16)

    At_all, rAt = SA(attn_ctx, "At_all", [128, H * 128], BF16)
    P_row, rProw = SA(attn_ctx, "P_row", [1, H * 128], BF16)
    lng_bc, rlng = SA(attn_ctx, "lng_bc", [128, D], BF16)
    lnb_bc, rlnb = SA(attn_ctx, "lnb_bc", [128, D], BF16)
    c_stage, rcst = SA(attn_ctx, "c_stage", [1, H * 128], BF16)
    rcst4 = [Rec(f"cst{q4}") for q4 in range(4)]

    ps_tr = en(nc.psum_tensor("ps_tr", [128, 512], F32))
    rtrp = Rec("ps_tr")
    ps_sm = en(nc.psum_tensor("ps_sm", [128, 512], F32))
    rsm = Rec("ps_sm")
    ps_a0 = en(nc.psum_tensor("ps_a0", [128, 512], F32))
    ra0 = Rec("ps_a0")
    ps_a1 = en(nc.psum_tensor("ps_a1", [128, 512], F32))
    ra1 = Rec("ps_a1")
    ps_a2 = en(nc.psum_tensor("ps_a2", [128, 512], F32))
    ra2 = Rec("ps_a2")
    ps_a3 = en(nc.psum_tensor("ps_a3", [128, 512], F32))
    ra3 = Rec("ps_a3")
    ps_a4 = en(nc.psum_tensor("ps_a4", [128, 512], F32))
    ra4 = Rec("ps_a4")
    ps_a5 = en(nc.psum_tensor("ps_a5", [128, 512], F32))
    ra5 = Rec("ps_a5")

    setup_ctx = ExitStack()
    tri16, rtri16 = SA(setup_ctx, "tri16sb", [128, H * 128], F32)
    mask16, rmask16 = SA(setup_ctx, "mask16sb", [128, H * 128], BF16)
    lnlbig, rlnlbig = SA(setup_ctx, "lnlbig", [128, H * 128], F32)
    beta_bc, rbeta = SA(setup_ctx, "beta_bc", [128, H], F32)
    tmp16a, rtmp16a = SA(setup_ctx, "tmp16a", [128, H], F32)
    tmp16b, rtmp16b = SA(setup_ctx, "tmp16b", [128, H], F32)
    tp1, rtp1 = SA(setup_ctx, "tp1", [1, H * 128], F32)
    lnlexp, rlnlexp = SA(setup_ctx, "lnlexp", [1, H * 128], F32)
    prow_f, rprowf = SA(setup_ctx, "prow_f", [1, H * 128], F32)

    kb.op("io", lambda e: e.dma_start(out=ident[:], in_=ident_d), outs=[rident, rsc], sem="sc")
    kb.op("io", lambda e: e.dma_start(out=tri16[:], in_=tri16_d), outs=[rtri16, rsc], sem="sc")
    kb.op("io", lambda e: e.dma_start(out=mask16[:], in_=mask16_d), outs=[rmask16, rsc], sem="sc")
    kb.op("io", lambda e: e.dma_start(out=slt[:], in_=slt_d), outs=[rslt, rsc], sem="sc")
    kb.op("io", lambda e: e.dma_start(out=iota[:], in_=iota_d), outs=[riota, rsc], sem="sc")
    kb.op("io", lambda e: e.dma_start(out=iota8bc[:], in_=iota8_d.broadcast_to([128, 8])),
          outs=[riota8, rsc], sem="sc")
    kb.op("io", lambda e: e.dma_start(out=tp1[:], in_=tplus1_d), outs=[rtp1, rsc], sem="sc")
    kb.op("io", lambda e: e.dma_start(out=beta_bc[:], in_=beta_d.broadcast_to([128, H])),
          outs=[rbeta, rsc], sem="sc")
    kb.op("gp", lambda e: e.dma_start(out=lng_bc[:], in_=lng_d.broadcast_to([128, D])),
          outs=[rlng, rgc], sem="gc")
    kb.op("gp", lambda e: e.dma_start(out=lnb_bc[:], in_=lnb_d.broadcast_to([128, D])),
          outs=[rlnb, rgc], sem="gc")
    kb.op("gp", lambda e: e.dma_start(out=gwT[:], in_=gwT_d), outs=[rgwT, rgc], sem="gc")
    for qt in range(4):
        kb.op("gp", lambda e, qt=qt: e.dma_start(
            out=WA[:, qt * 4 * D:(qt + 1) * 4 * D],
            in_=Wv_d[:, qt * 4 * D:(qt + 1) * 4 * D]),
            outs=[rWAq[qt]], sem="wa")
        kb.op("gp", lambda e, qt=qt: e.dma_start(
            out=WB[:, qt * 4 * D:(qt + 1) * 4 * D],
            in_=Wg_d[:, qt * 4 * D:(qt + 1) * 4 * D]),
            outs=[rWBq[qt]], sem="wb")

    kb.op("dve", lambda e: e.memset(ones8[:], 1.0), outs=[rones8])
    kb.op("dve", lambda e: e.memset(runcnt[:], 0.0), outs=[rruncnt])
    kb.op("dve", lambda e: e.memset(onesr_f[:], 1.0), outs=[ronesr])
    kb.op("dve", lambda e: e.memset(onesc_bf[:], 1.0), outs=[ronesc])

    kb.op("act", lambda e: e.activation(out=tmp16a[:], in_=beta_bc[:], func=AF.Abs),
          ins=[rbeta], outs=[rtmp16a])
    kb.op("dve", lambda e: e.tensor_scalar_mul(tmp16a[:], tmp16a[:], -1.0),
          ins=[rtmp16a], outs=[rtmp16a])
    kb.op("act", lambda e: e.activation(out=tmp16a[:], in_=tmp16a[:], func=AF.Exp),
          ins=[rtmp16a], outs=[rtmp16a])
    kb.op("act", lambda e: e.activation(out=tmp16a[:], in_=tmp16a[:], func=AF.Ln, bias=1.0),
          ins=[rtmp16a], outs=[rtmp16a])
    kb.op("dve", lambda e: e.tensor_scalar(out=tmp16b[:], in0=beta_bc[:], scalar1=-1.0,
                                           scalar2=0.0, op0=ALU.mult, op1=ALU.max),
          ins=[rbeta], outs=[rtmp16b])
    kb.op("dve", lambda e: e.tensor_add(out=tmp16b[:], in0=tmp16b[:], in1=tmp16a[:]),
          ins=[rtmp16a, rtmp16b], outs=[rtmp16b])
    kb.op("dve", lambda e: e.tensor_scalar_mul(lnl_bc[:], tmp16b[:], -1.0),
          ins=[rtmp16b], outs=[rlnl])

    kb.op("dve", lambda e: e.tensor_copy(
        out=lnlexp[:].rearrange("o (h t) -> o h t", h=H),
        in_=lnl_bc[0:1, :].unsqueeze(2).broadcast_to([1, H, 128])),
        ins=[rlnl], outs=[rlnlexp])
    for cg in range(4):
        pscg = [ps_a0, ps_a1][cg % 2]
        rcg = [ra0, ra1][cg % 2]
        kb.emit("pe", [lambda e, cg=cg, pscg=pscg: e.matmul(
            pscg[:, 0:512], onesr_f[0:1, :], lnlexp[0:1, cg * 512:(cg + 1) * 512],
            start=True, stop=True)], ins=[ronesr, rlnlexp], outs=[rcg])
        kb.op("dve", lambda e, cg=cg, pscg=pscg: e.tensor_copy(
            out=lnlbig[:, cg * 512:(cg + 1) * 512], in_=pscg[:, 0:512]),
            ins=[rcg], outs=[rlnlbig])
    kb.op("dve", lambda e: e.tensor_mul(out=lnlbig[:], in0=lnlbig[:], in1=tri16[:]),
          ins=[rtri16, rlnlbig], outs=[rlnlbig])
    kb.op("act", lambda e: e.activation(out=lnlbig[:], in_=lnlbig[:], func=AF.Exp),
          ins=[rlnlbig], outs=[rlnlbig])
    kb.op("dve", lambda e: e.tensor_tensor(out=At_all[:], in0=lnlbig[:], in1=mask16[:],
                                           op=ALU.mult),
          ins=[rlnlbig, rmask16], outs=[rAt])

    kb.op("dve", lambda e: e.tensor_mul(out=prow_f[:], in0=tp1[:], in1=lnlexp[:]),
          ins=[rtp1, rlnlexp], outs=[rprowf])
    kb.op("act", lambda e: e.activation(out=prow_f[:], in_=prow_f[:], func=AF.Exp),
          ins=[rprowf], outs=[rprowf])
    kb.op("act", lambda e: e.copy(out=P_row[:], in_=prow_f[:]), ins=[rprowf], outs=[rProw])

    kb.op("io", lambda e: e.dma_start(out=e8sb[:], in_=e8_d), outs=[re8, rsc], sem="sc")
    kb.op("io", lambda e: e.dma_start(out=m8sb[:], in_=m8_d), outs=[rm8, rsc], sem="sc")
    kb.op("dve", lambda e: e.memset(u_row[:], 0.0), outs=[ru_row])
    for c in range(NCH):
        kb.op("dve", lambda e, c=c: e.tensor_scalar(
            out=c_s1[:], in0=iota[:], scalar1=-1.0, scalar2=float(2047 - 128 * c),
            op0=ALU.mult, op1=ALU.add),
            ins=[riota], outs=[rs1])
        kb.op("dve", lambda e: e.tensor_scalar(
            out=tmp16a[:], in0=lnl_bc[:], scalar1=c_s1[:, 0:1], scalar2=None,
            op0=ALU.mult),
            ins=[rlnl, rs1], outs=[rtmp16a])
        kb.op("act", lambda e: e.activation(out=tmp16a[:], in_=tmp16a[:], func=AF.Exp),
              ins=[rtmp16a], outs=[rtmp16a])
        kb.op("act", lambda e, c=c: e.copy(out=wdec_all[:, c * H:(c + 1) * H],
                                           in_=tmp16a[:]),
              ins=[rtmp16a], outs=[rwdec])

    seeds = [t for r in (rtri16, rmask16, rlnlbig, rbeta, rtmp16a, rtmp16b, rtp1,
                         rlnlexp, rprowf) for t in r.toks()]
    setup_ctx.close()

    a12_ctx = ExitStack()
    xbuf1, rx1 = SA(a12_ctx, "xbuf1", [128, D], BF16)
    xln_t = SA(a12_ctx, "xln0", [128, D], BF16)
    xlns = [xln_t, xln_t]
    xlnTs = [SA(a12_ctx, "xlnT0", [128, D], BF16), SA(a12_ctx, "xlnT1", [128, D], BF16)]
    v_t = SA(a12_ctx, "v0", [128, D], BF16)
    vsb = [v_t, v_t]
    g_t = SA(a12_ctx, "g0", [128, D], BF16)
    gsb = [g_t, g_t]
    q_f32, rq = SA(a12_ctx, "q_f32", [128, D], F32)
    S_sb, rS = SA(a12_ctx, "S_sb", [128, D], BF16)
    ysb = [SA(a12_ctx, "y0", [128, D], BF16), SA(a12_ctx, "y1", [128, D], BF16)]
    z_bf, rz = SA(a12_ctx, "z_bf", [128, D], BF16)
    q_bf, rqb = None, None
    d_s1, qs1 = SA(a12_ctx, "d_s1", [128, 1], F32)
    d_s2, qs2 = SA(a12_ctx, "d_s2", [128, 1], F32)
    d_mu, qmu = SA(a12_ctx, "d_mu", [128, 1], F32)
    d_msq, qmsq = SA(a12_ctx, "d_msq", [128, 1], F32)
    d_var, qvar = SA(a12_ctx, "d_var", [128, 1], F32)
    d_std, qstd = SA(a12_ctx, "d_std", [128, 1], F32)
    d_rstd, qrstd = SA(a12_ctx, "d_rstd", [128, 1], F32)
    d_nmr, qnmr = SA(a12_ctx, "d_nmr", [128, 1], F32)

    class BCView:
        def __init__(self, t, off=0):
            self.t = t
            self.off = off
        def __getitem__(self, key):
            if key == slice(None):
                return self.t[:, self.off:self.off + D // 2].bitcast(BF16)
            _, cs = key
            return self.t[:, self.off + cs.start // 2: self.off + cs.stop // 2].bitcast(BF16)

    q_bf, rqb = BCView(q_f32, 1024), rq

    xb = [(xbuf0, rx0), (xbuf1, rx1)]
    EV = ["act", "dve", "act", "dve"]

    def pcopy(engname, dst_ap_fn, ps, rps, dst_rec):
        if engname == "act":
            kb.op("act", lambda e: e.copy(out=dst_ap_fn(), in_=ps[:, 0:512]),
                  ins=[rps], outs=[dst_rec])
        else:
            kb.op("dve", lambda e: e.tensor_copy(out=dst_ap_fn(), in_=ps[:, 0:512]),
                  ins=[rps], outs=[dst_rec])

    def ln_stats(src_rec, src_tile, sq_tile, sq_rec, cols):
        (s1_, t1), (s2_, t2), (mu_, tmu), (msq_, tmsq), (var_, tvar), (std_, tstd), \
            (rstdc, trstd), (nmrc, tnmr) = cols
        kb.op("dve", lambda e: e.reduce_sum(out=s1_[:], in_=src_tile[:], axis=AX.X),
              ins=[src_rec], outs=[t1])
        kb.op("act", lambda e: e.activation(out=sq_tile[:], in_=src_tile[:], func=AF.Square,
                                            accum_out=s2_[:]),
              ins=[src_rec], outs=[sq_rec, t2])
        kb.op("dve", lambda e: e.tensor_scalar_mul(mu_[:], s1_[:], 1.0 / D),
              ins=[t1], outs=[tmu])
        kb.op("dve", lambda e: e.tensor_scalar(out=msq_[:], in0=mu_[:],
                                               scalar1=mu_[:, 0:1], scalar2=EPS,
                                               op0=ALU.mult, op1=ALU.subtract),
              ins=[tmu], outs=[tmsq])
        kb.op("dve", lambda e: e.tensor_scalar(out=var_[:], in0=s2_[:], scalar1=1.0 / D,
                                               scalar2=msq_[:, 0:1], op0=ALU.mult,
                                               op1=ALU.subtract),
              ins=[t2, tmsq], outs=[tvar])
        kb.op("act", lambda e: e.activation(out=std_[:], in_=var_[:], func=AF.Sqrt),
              ins=[tvar], outs=[tstd])
        kb.op("dve", lambda e: e.reciprocal(out=rstdc[:], in_=std_[:]),
              ins=[tstd], outs=[trstd])
        kb.op("dve", lambda e: e.tensor_scalar(out=nmrc[:], in0=mu_[:],
                                               scalar1=rstdc[:, 0:1], scalar2=-1.0,
                                               op0=ALU.mult, op1=ALU.mult),
              ins=[tmu, trstd], outs=[tnmr])
        return (rstdc, trstd), (nmrc, tnmr)

    colsA = [(c_s1, rs1), (c_s2, rs2), (c_mu, rmu), (c_msq, rmsq), (c_var, rvar),
             (c_std, rstd_), (c_rstd, rrstd), (c_nmr, rnmr)]
    colsB = [(d_s1, qs1), (d_s2, qs2), (d_mu, qmu), (d_msq, qmsq), (d_var, qvar),
             (d_std, qstd), (d_rstd, qrstd), (d_nmr, qnmr)]

    def transpose16(src_tile, src_rec, dst_tile, dst_rec):
        for rnd in range(4):
            fns = []
            for j in range(4):
                kcx = rnd * 4 + j
                fns.append(lambda e, kcx=kcx, j=j: e.matmul(
                    ps_tr[:, j * 128:(j + 1) * 128],
                    src_tile[:, kcx * 128:(kcx + 1) * 128], ident[:],
                    start=True, stop=True))
            kb.emit("pe", fns, ins=[src_rec, rident], outs=[rtrp])
            if rnd % 2 == 0:
                kb.op("act", lambda e, rnd=rnd: e.copy(
                    out=dst_tile[:, rnd * 512:(rnd + 1) * 512], in_=ps_tr[:]),
                    ins=[rtrp], outs=[dst_rec])
            else:
                kb.op("dve", lambda e, rnd=rnd: e.tensor_copy(
                    out=dst_tile[:, rnd * 512:(rnd + 1) * 512], in_=ps_tr[:]),
                    ins=[rtrp], outs=[dst_rec])

    pss4 = [(ps_a0, ra0), (ps_a1, ra1), (ps_a0, ra0), (ps_a1, ra1)]

    def big_mm(lhsT_tile, lhsT_rec, w_tile, w_rec, evac, four_banks=True,
               banks=(0, 1)):
        if four_banks:
            for qt in range(4):
                fns = []
                for kcq in range(4):
                    kcx = qt * 4 + kcq
                    for cg in range(4):
                        ps, rps = pss4[cg]
                        fns.append(lambda e, kcx=kcx, cg=cg, ps=ps: e.matmul(
                            ps[:, 0:512],
                            lhsT_tile[:, kcx * 128:(kcx + 1) * 128],
                            w_tile[:, kcx * D + cg * 512: kcx * D + (cg + 1) * 512],
                            start=(kcx == 0), stop=(kcx == KC - 1)))
                kb.emit("pe", fns, ins=[lhsT_rec, w_rec[qt]],
                        outs=[pss4[0][1], pss4[1][1], pss4[2][1], pss4[3][1]])
            for cg in range(4):
                ps, rps = pss4[cg]
                evac(cg, ps, rps)
        else:
            for pair in range(2):
                cga, cgb = 2 * pair, 2 * pair + 1
                psa, rpsa = pss4[banks[0]]
                psb_, rpsb = pss4[banks[1]]
                for qt in range(4):
                    fns = []
                    for kcq in range(4):
                        kcx = qt * 4 + kcq
                        for cg, ps in ((cga, psa), (cgb, psb_)):
                            fns.append(lambda e, kcx=kcx, cg=cg, ps=ps: e.matmul(
                                ps[:, 0:512],
                                lhsT_tile[:, kcx * 128:(kcx + 1) * 128],
                                w_tile[:, kcx * D + cg * 512: kcx * D + (cg + 1) * 512],
                                start=(kcx == 0), stop=(kcx == KC - 1)))
                    kb.emit("pe", fns, ins=[lhsT_rec, w_rec[qt]], outs=[rpsa, rpsb])
                evac(cga, psa, rpsa)
                evac(cgb, psb_, rpsb)

    def a1_step(c):
        xt, rxt = xb[c % 2]
        xlnp, rxlnp = xlns[c % 2]
        xlnTp, rxlnTp = xlnTs[c % 2]
        vp, rvp = vsb[c % 2]
        gp_, rgp = gsb[c % 2]
        kb.op("gp", lambda e, c=c, xt=xt: e.dma_start(
            out=xt[:], in_=x_own[c * 128:(c + 1) * 128, :]), outs=[rxt],
            sem="x0")
        (rst, trst), (nmr_, tnmr) = ln_stats(rxt, xt, z_bf, rz, colsA)
        kb.op("dve", lambda e, xt=xt, xlnp=xlnp: e.tensor_scalar(
            out=xlnp[:], in0=xt[:], scalar1=rst[:, 0:1], scalar2=nmr_[:, 0:1],
            op0=ALU.mult, op1=ALU.add),
            ins=[rxt, trst, tnmr], outs=[rxlnp])
        transpose16(xlnp, rxlnp, xlnTp, rxlnTp)
        kb.op("io", lambda e, c=c, xlnTp=xlnTp: e.dma_start(out=s_xlnT[c], in_=xlnTp[:]),
              ins=[rxlnTp], outs=[r_sxlnT[c]], sem="sxt")

        def evac_v(cg, ps, rps):
            pcopy(EV[cg], lambda cg=cg, vp=vp: vp[:, cg * 512:(cg + 1) * 512], ps, rps, rvp)
        big_mm(xlnTp, rxlnTp, WA, rWAq, evac_v, four_banks=False)
        kb.op("io", lambda e, c=c, vp=vp: e.dma_start(out=s_v[c], in_=vp[:]),
              ins=[rvp], outs=[r_sv[c]], sem="sv")

        for q4 in range(4):
            bank, rbank = psU[q4]
            fns = [lambda e, h=h, vp=vp, bank=bank: e.matmul(
                bank[0:1, (h % 4) * 128:(h % 4) * 128 + 128],
                wdec_all[:, c * H + h: c * H + h + 1],
                vp[:, h * 128:(h + 1) * 128],
                start=True, stop=True) for h in range(4 * q4, 4 * q4 + 4)]
            kb.emit("pe", fns, ins=[rwdec, rvp], outs=[rbank])
            kb.op("dve", lambda e, q4=q4, bank=bank: e.tensor_tensor(
                out=u_row[0:1, q4 * 512:(q4 + 1) * 512], in0=bank[0:1, 0:512],
                in1=u_row[0:1, q4 * 512:(q4 + 1) * 512], op=ALU.add),
                ins=[rbank], outs=[ru_row])

        def evac_g(cg, ps, rps):
            kb.op("act", lambda e, cg=cg, ps=ps, gp_=gp_: e.activation(
                out=gp_[:, cg * 512:(cg + 1) * 512], in_=ps[:, 0:512], func=AF.Silu),
                ins=[rps], outs=[rgp])
        big_mm(xlnTp, rxlnTp, WB, rWBq, evac_g, four_banks=False)
        kb.op("io", lambda e, c=c, gp_=gp_: e.dma_start(out=s_g[c], in_=gp_[:]),
              ins=[rgp], outs=[r_sg[c]], sem="sg")

    pss4[:] = [(ps_a0, ra0), (ps_a1, ra1), (ps_a2, ra2), (ps_a3, ra3)]
    pssS = [(ps_a4, ra4), (ps_a5, ra5), (ps_a4, ra4), (ps_a5, ra5)]
    psU = [(ps_a2, ra2), (ps_a3, ra3), (ps_a4, ra4), (ps_a5, ra5)]
    zT = BCView(q_f32, 0)
    rzT = Rec("zT", seed=rq.toks())

    for c in range(NCH):
        a1_step(c)

    for qt in range(4):
        kb.op("gp", lambda e, qt=qt: e.dma_start(
            out=WA[:, qt * 4 * D:(qt + 1) * 4 * D],
            in_=Wq_d[:, qt * 4 * D:(qt + 1) * 4 * D]),
            outs=[rWAq[qt]], sem="wa")
        kb.op("gp", lambda e, qt=qt: e.dma_start(
            out=WB[:, qt * 4 * D:(qt + 1) * 4 * D],
            in_=Wo_d[:, qt * 4 * D:(qt + 1) * 4 * D]),
            outs=[rWBq[qt]], sem="wb")

    r_ud = Rec("u_dram")
    r_cc = Rec("u_all")
    kb.op("gp", lambda e: e.dma_start(out=u_dram, in_=u_row[0:1, :]),
          ins=[ru_row], outs=[r_ud], sem="ud")
    kb.op("gp", lambda e: e.collective_compute(
        "AllGather", ALU.bypass, replica_groups=[list(range(NCORES))],
        ins=[u_dram_t.ap().opt()], outs=[u_all_t.ap().opt()]),
        ins=[r_ud], outs=[r_cc], sem="cc")
    u8_t, r_u8 = xlnTs[0]
    kb.op("gp", lambda e: e.dma_start(out=u8_t[0:8, :], in_=u_all),
          ins=[r_cc], outs=[r_u8], sem="u8")
    kb.op("dve", lambda e: e.tensor_scalar(
        out=w8f[:], in0=lnl_bc[0:8, :], scalar1=e8sb[0:8, 0:1], scalar2=None,
        op0=ALU.mult),
        ins=[rlnl, re8], outs=[rw8f])
    kb.op("act", lambda e: e.activation(out=w8f[:], in_=w8f[:], func=AF.Exp),
          ins=[rw8f], outs=[rw8f])
    kb.op("dve", lambda e: e.tensor_scalar(
        out=w8f[:], in0=w8f[:], scalar1=m8sb[0:8, 0:1], scalar2=None, op0=ALU.mult),
        ins=[rw8f, rm8], outs=[rw8f])
    kb.op("act", lambda e: e.copy(out=w8b[:], in_=w8f[:]), ins=[rw8f], outs=[rw8b])
    for q4 in range(4):
        psb, rpbh = pssS[q4]
        fns = [lambda e, h=h, psb=psb: e.matmul(
            psb[0:1, (h % 4) * 128:(h % 4) * 128 + 128],
            w8b[0:8, h:h + 1],
            u8_t[0:8, h * 128:(h + 1) * 128],
            start=True, stop=True) for h in range(4 * q4, 4 * q4 + 4)]
        kb.emit("pe", fns, ins=[rw8b, r_u8], outs=[rpbh])
        kb.op("act", lambda e, q4=q4, psb=psb: e.copy(
            out=c_stage[0:1, q4 * 512:(q4 + 1) * 512], in_=psb[0:1, :]),
            ins=[rpbh], outs=[rcst4[q4]])

    for cc2 in range(NCH):
        r_sxlnT[cc2].writers = [("sxt", kb.sems["sxt"][1])]
        r_sv[cc2].writers = [("sv", kb.sems["sv"][1])]
        r_sg[cc2].writers = [("sg", kb.sems["sg"][1])]

    def a2_s1a(c):
        xlnTp, rxlnTp = xlnTs[c % 2]
        vp, rvp = vsb[c % 2]
        yp, ryp = ysb[c % 2]
        kb.op("io", lambda e, c=c, xlnTp=xlnTp: e.dma_start(out=xlnTp[:], in_=s_xlnT[c]),
              ins=[r_sxlnT[c]], outs=[rxlnTp], sem="lxt")
        kb.op("io", lambda e, c=c, vp=vp: e.dma_start(out=vp[:], in_=s_v[c]),
              ins=[r_sv[c]], outs=[rvp], sem="lv")

        def evac_q(cg, ps, rps):
            pcopy(EV[cg], lambda cg=cg: q_bf[:, cg * 512:(cg + 1) * 512], ps, rps, rqb)
        big_mm(xlnTp, rxlnTp, WA, rWAq, evac_q, four_banks=False, banks=(0, 1))

        for q4 in range(4):
            psb, rpbh = pssS[q4]
            fns = []
            for h in range(4 * q4, 4 * q4 + 4):
                fns.append(lambda e, h=h, vp=vp, psb=psb: e.matmul(
                    psb[:, (h % 4) * 128:(h % 4) * 128 + 128],
                    At_all[:, h * 128:(h + 1) * 128],
                    vp[:, h * 128:(h + 1) * 128],
                    start=True, stop=False))
                fns.append(lambda e, h=h, psb=psb: e.matmul(
                    psb[:, (h % 4) * 128:(h % 4) * 128 + 128],
                    P_row[0:1, h * 128:(h + 1) * 128],
                    c_stage[0:1, h * 128:(h + 1) * 128],
                    start=False, stop=True))
            kb.emit("pe", fns, ins=[rAt, rvp, rProw, rcst4[q4]], outs=[rpbh])
            if q4 % 2 == 0:
                kb.op("act", lambda e, q4=q4, psb=psb: e.copy(
                    out=S_sb[:, q4 * 512:(q4 + 1) * 512], in_=psb[:, 0:512]),
                    ins=[rpbh], outs=[rS])
            else:
                kb.op("dve", lambda e, q4=q4, psb=psb: e.tensor_copy(
                    out=S_sb[:, q4 * 512:(q4 + 1) * 512], in_=psb[:, 0:512]),
                    ins=[rpbh], outs=[rS])
            kb.op("io", lambda e, q4=q4: e.dma_start(
                out=c_stage[0:1, q4 * 512:(q4 + 1) * 512],
                in_=S_sb[127:128, q4 * 512:(q4 + 1) * 512]),
                ins=[rS], outs=[rcst4[q4]], sem="row")
        kb.op("dve", lambda e, yp=yp: e.tensor_mul(out=yp[:], in0=q_bf[:], in1=S_sb[:]),
              ins=[rqb, rS], outs=[ryp])

    def a2_s1b(c):
        xt, rxt = xb[c % 2]
        yp, ryp = ysb[c % 2]
        gp_, rgp = gsb[c % 2]
        kb.op("io", lambda e, c=c, gp_=gp_: e.dma_start(out=gp_[:], in_=s_g[c]),
              ins=[r_sg[c]], outs=[rgp], sem="lg")
        kb.op("gp", lambda e, c=c, xt=xt: e.dma_start(
            out=xt[:], in_=x_own[c * 128:(c + 1) * 128, :]), outs=[rxt], sem="x0")
        (rst, trst), (nmr_, tnmr) = ln_stats(ryp, yp, z_bf, rz, colsA)
        kb.op("act", lambda e, rst=rst, nmr_=nmr_, yp=yp: e.activation(
            out=z_bf[:], in_=yp[:], func=AF.Identity,
            scale=rst[:, 0:1], bias=nmr_[:, 0:1]),
            ins=[ryp, trst, tnmr], outs=[rz])
        kb.op("dve", lambda e, yp=yp: e.tensor_mul(out=yp[:], in0=z_bf[:], in1=lng_bc[:]),
              ins=[rz, rlng], outs=[ryp])
        kb.op("dve", lambda e, yp=yp: e.tensor_add(out=z_bf[:], in0=yp[:], in1=lnb_bc[:]),
              ins=[ryp, rlnb], outs=[rz])
        kb.op("dve", lambda e, gp_=gp_, yp=yp: e.tensor_mul(out=yp[:], in0=z_bf[:], in1=gp_[:]),
              ins=[rz, rgp], outs=[ryp])
        transpose16(yp, ryp, zT, rzT)

        def evac_o(cg, ps, rps):
            kb.op("dve", lambda e, cg=cg, ps=ps, xt=xt: e.tensor_tensor(
                out=xt[:, cg * 512:(cg + 1) * 512], in0=ps[:, 0:512],
                in1=xt[:, cg * 512:(cg + 1) * 512], op=ALU.add),
                ins=[rps, rxt], outs=[rxt])
        big_mm(zT, rzT, WB, rWBq, evac_o, four_banks=False, banks=(2, 3))
        kb.op("io", lambda e, c=c, xt=xt: e.dma_start(out=s_x1[c], in_=xt[:]),
              ins=[rxt], outs=[r_sx1[c]], sem="sx1")

    def a2_s2a(c):
        xt, rxt = xb[c % 2]
        xlnp, rxlnp = xlns[c % 2]
        xlnTp, rxlnTp = xlnTs[c % 2]
        (rst, trst), (nmr_, tnmr) = ln_stats(rxt, xt, z_bf, rz, colsB)
        kb.op("dve", lambda e, rst=rst, nmr_=nmr_, xlnp=xlnp, xt=xt: e.tensor_scalar(
            out=xlnp[:], in0=xt[:], scalar1=rst[:, 0:1], scalar2=nmr_[:, 0:1],
            op0=ALU.mult, op1=ALU.add),
            ins=[rxt, trst, tnmr], outs=[rxlnp])
        transpose16(xlnp, rxlnp, xlnTp, rxlnTp)
        fns = []
        for kcx in range(KC):
            fns.append(lambda e, kcx=kcx, xlnTp=xlnTp: e.matmul(
                ps_sm[:, 0:8], xlnTp[:, kcx * 128:(kcx + 1) * 128],
                gwT[:, kcx * 8:(kcx + 1) * 8], start=(kcx == 0), stop=(kcx == KC - 1)))
        kb.emit("pe", fns, ins=[rxlnTp, rgwT], outs=[rsm])
        lgp, rlgp = (lg8a, rlg8a) if c % 2 == 0 else (lg8b, rlg8b)
        kb.op("dve", lambda e, lgp=lgp: e.tensor_copy(out=lgp[:], in_=ps_sm[:, 0:8]),
              ins=[rsm], outs=[rlgp])
    def a2_s2b(c):
        xlnp, rxlnp = xlns[c % 2]
        lgp, rlgp = (lg8a, rlg8a) if c % 2 == 0 else (lg8b, rlg8b)
        kb.op("dve", lambda e: e.reduce_max(out=c_lmax[:], in_=lgp[:], axis=AX.X),
              ins=[rlgp], outs=[rlmax])
        kb.op("dve", lambda e: e.tensor_scalar_mul(c_nlmax[:], c_lmax[:], -1.0),
              ins=[rlmax], outs=[rnlmax])
        kb.op("act", lambda e: e.activation(out=pexp8[:], in_=lgp[:], func=AF.Exp,
                                            bias=c_nlmax[:, 0:1], scale=1.0,
                                            accum_out=c_sexp[:]),
              ins=[rlgp, rnlmax], outs=[rpexp8, rsexp])
        kb.op("dve", lambda e: e.reciprocal(out=c_rsum[:], in_=c_sexp[:]),
              ins=[rsexp], outs=[rrsum])
        kb.op("dve", lambda e: e.tensor_scalar(out=oh8[:], in0=lgp[:],
                                               scalar1=c_lmax[:, 0:1], scalar2=None,
                                               op0=ALU.is_ge),
              ins=[rlgp, rlmax], outs=[roh8])
        kb.op("dve", lambda e: e.tensor_tensor_scan(out=t8a[:], data0=ones8[:], data1=oh8[:],
                                                    initial=0.0, op0=ALU.mult, op1=ALU.add),
              ins=[rones8, roh8], outs=[rt8a])
        kb.op("dve", lambda e: e.tensor_sub(out=t8a[:], in0=t8a[:], in1=oh8[:]),
              ins=[rt8a, roh8], outs=[rt8a])
        kb.op("dve", lambda e: e.tensor_single_scalar(out=t8a[:], in_=t8a[:], scalar=0.0,
                                                      op=ALU.is_equal),
              ins=[rt8a], outs=[rt8a])
        kb.op("dve", lambda e: e.tensor_mul(out=oh8[:], in0=oh8[:], in1=t8a[:]),
              ins=[roh8, rt8a], outs=[roh8])
        kb.op("act", lambda e: e.copy(out=oh8b[:], in_=oh8[:]), ins=[roh8], outs=[roh8b])
        kb.op("dve", lambda e: e.tensor_mul(out=t8b[:], in0=oh8[:], in1=pexp8[:]),
              ins=[roh8, rpexp8], outs=[rt8b])
        kb.op("dve", lambda e: e.reduce_sum(out=c_ptop[:], in_=t8b[:], axis=AX.X),
              ins=[rt8b], outs=[rptop])
        kb.op("dve", lambda e: e.tensor_mul(out=c_w[:], in0=c_ptop[:], in1=c_rsum[:]),
              ins=[rptop, rrsum], outs=[rcw])
        kb.emit("pe", [lambda e: e.matmul(ps_sm[:, 8:16], slt[:], oh8b[:],
                                          start=True, stop=True)],
                ins=[rslt, roh8b], outs=[rsm])
        kb.emit("pe", [lambda e: e.matmul(ps_sm[:, 16:24], onesr_f[0:1, :], runcnt[0:1, :],
                                          start=True, stop=True)],
                ins=[ronesr, rruncnt], outs=[rsm])
        kb.op("dve", lambda e: e.tensor_mul(out=t8a[:], in0=oh8[:], in1=ps_sm[:, 8:16]),
              ins=[roh8, rsm], outs=[rt8a])
        kb.op("dve", lambda e: e.reduce_sum(out=c_rk1[:], in_=t8a[:], axis=AX.X),
              ins=[rt8a], outs=[rrk1])
        kb.op("dve", lambda e: e.tensor_mul(out=t8b[:], in0=oh8[:], in1=ps_sm[:, 16:24]),
              ins=[roh8, rsm], outs=[rt8b])
        kb.op("dve", lambda e: e.reduce_sum(out=c_rk2[:], in_=t8b[:], axis=AX.X),
              ins=[rt8b], outs=[rrk2])
        kb.op("dve", lambda e: e.tensor_add(out=c_rank[:], in0=c_rk1[:], in1=c_rk2[:]),
              ins=[rrk1, rrk2], outs=[rrank])
        kb.op("dve", lambda e: e.tensor_mul(out=t8a[:], in0=oh8[:], in1=iota8bc[:]),
              ins=[roh8, riota8], outs=[rt8a])
        kb.op("dve", lambda e: e.reduce_sum(out=c_et[:], in_=t8a[:], axis=AX.X),
              ins=[rt8a], outs=[rcet])
        kb.op("dve", lambda e: e.tensor_single_scalar(out=c_keep[:], in_=c_rank[:],
                                                      scalar=float(PCAP), op=ALU.is_lt),
              ins=[rrank], outs=[rkeep])
        kb.op("dve", lambda e, c=c: e.tensor_scalar(out=wtok[:, c:c + 1], in0=c_w[:],
                                                    scalar1=c_keep[:, 0:1], scalar2=None,
                                                    op0=ALU.mult),
              ins=[rcw, rkeep], outs=[rwtok])
        kb.op("dve", lambda e: e.tensor_scalar(out=c_sf[:], in0=c_et[:], scalar1=float(PCAP),
                                               scalar2=c_rank[:, 0:1], op0=ALU.mult,
                                               op1=ALU.add),
              ins=[rcet, rrank], outs=[rcsf])
        kb.op("dve", lambda e: e.tensor_scalar(out=c_sk[:], in0=c_sf[:],
                                               scalar1=c_keep[:, 0:1], scalar2=None,
                                               op0=ALU.mult),
              ins=[rcsf, rkeep], outs=[rcsk])
        kb.op("dve", lambda e: e.tensor_scalar(out=c_bigk[:], in0=c_keep[:], scalar1=-BIG,
                                               scalar2=BIG, op0=ALU.mult, op1=ALU.add),
              ins=[rkeep], outs=[rbigk])
        kb.op("dve", lambda e: e.tensor_add(out=c_sm[:], in0=c_sk[:], in1=c_bigk[:]),
              ins=[rcsk, rbigk], outs=[rcsm])
        kb.op("dve", lambda e, c=c: e.tensor_copy(out=slots[:, c:c + 1], in_=c_sm[:]),
              ins=[rcsm], outs=[rslots])
        kb.emit("pe", [lambda e: e.matmul(ps_sm[0:1, 24:32], onesc_bf[:], oh8b[:],
                                          start=True, stop=True)],
                ins=[ronesc, roh8b], outs=[rsm])
        kb.op("dve", lambda e: e.tensor_copy(out=cnt8[:], in_=ps_sm[0:1, 24:32]),
              ins=[rsm], outs=[rcnt8])
        kb.op("dve", lambda e: e.tensor_add(out=runcnt[:], in0=runcnt[:], in1=cnt8[:]),
              ins=[rcnt8, rruncnt], outs=[rruncnt])
        tok = kb.op("gp", lambda e, c=c, xlnp=xlnp: e.indirect_dma_start(
            out=s_disp, out_offset=IOA(ap=slots[:, c:c + 1], axis=0),
            in_=xlnp[:], in_offset=None,
            bounds_check=NSLOT - 1, oob_is_err=False),
            ins=[rxlnp, rslots], sem="ind")
        r_disp.writers.append(tok)

    a2_s1a(0)
    a2_s1a(1)
    a2_s1b(0)
    for c in range(2, NCH):
        a2_s1a(c)
        a2_s2a(c - 2)
        a2_s1b(c - 1)
        a2_s2b(c - 2)
    a2_s2a(NCH - 2)
    a2_s1b(NCH - 1)
    a2_s2b(NCH - 2)
    a2_s2a(NCH - 1)
    a2_s2b(NCH - 1)

    seeds = [t for rr in (xlns + xlnTs + vsb + gsb + ysb) for t in rr[1].toks()] + \
        [t for r in (rq, rx1, rS, rz, rzT, rAt, rProw, rlng, rlnb, rcst, *rcst4)
         for t in r.toks()]
    a12_ctx.close()
    attn_ctx.close()

    a3_ctx = ExitStack()
    DeTs = [SA(a3_ctx, "DeT0", [128, KC * PCAP], BF16),
            SA(a3_ctx, "DeT1", [128, KC * PCAP], BF16)]
    rDeTq = [[Rec(f"DeT{b}q{qt}") for qt in range(4)] for b in range(2)]
    hmTs = [SA(a3_ctx, "hmT0", [128, KC * PCAP], BF16),
            SA(a3_ctx, "hmT1", [128, KC * PCAP], BF16)]
    eo_tm, reotm = SA(a3_ctx, "eo_tm", [128, D], BF16)
    ffnb = [SA(a3_ctx, "ffn0", [128, D], BF16), SA(a3_ctx, "ffn1", [128, D], BF16)]
    xb4 = [SA(a3_ctx, "x4a", [128, D], BF16), SA(a3_ctx, "x4b", [128, D], BF16)]

    class FView:
        def __init__(self, t):
            self.t = t
        def __getitem__(self, key):
            assert key == slice(None)
            return self.t[:, 0:2 * D].bitcast(F32)

    def a3_deT_loads(ei):
        DeT_bf, _ = DeTs[ei % 2]
        for kcx in range(KC):
            kb.op("io", lambda e, ei=ei, kcx=kcx, DeT_bf=DeT_bf: e.dma_start(
                out=DeT_bf[:, kcx * PCAP:(kcx + 1) * PCAP],
                in_=s_disp[ei * PCAP:(ei + 1) * PCAP, kcx * 128:(kcx + 1) * 128],
                transpose=True),
                ins=[r_disp], outs=[rDeTq[ei % 2][kcx // 4]], sem="de")

    def a3_mm1(ei):
        DeT_bf, _ = DeTs[ei % 2]
        hmT, rhmT = hmTs[ei % 2]
        for ic in range(KC):
            ps, rps = pss4[ic % 4]
            for qt in range(4):
                fns = []
                for kcq in range(4):
                    kcx = qt * 4 + kcq
                    fns.append(lambda e, ic=ic, kcx=kcx, ps=ps, DeT_bf=DeT_bf: e.matmul(
                        ps[:, 0:PCAP],
                        WA[:, kcx * D + ic * 128: kcx * D + (ic + 1) * 128],
                        DeT_bf[:, kcx * PCAP:(kcx + 1) * PCAP],
                        start=(kcx == 0), stop=(kcx == KC - 1)))
                kb.emit("pe", fns, ins=[rDeTq[ei % 2][qt], rWAq[qt]], outs=[rps])
            kb.op("act", lambda e, ic=ic, ps=ps, hmT=hmT: e.activation(
                out=hmT[:, ic * PCAP:(ic + 1) * PCAP],
                in_=ps[:, 0:PCAP], func=AF.Gelu),
                ins=[rps], outs=[rhmT])

    def a3_mm2(ei):
        hmT, rhmT = hmTs[ei % 2]
        for sg in range(3):
            for qt in range(4):
                fns = []
                for kcq in range(4):
                    kcx = qt * 4 + kcq
                    for cg in range(4):
                        ps, rps = pss4[cg]
                        fns.append(lambda e, sg=sg, kcx=kcx, cg=cg, ps=ps, hmT=hmT: e.matmul(
                            ps[:, 0:512],
                            hmT[:, kcx * PCAP + sg * 128: kcx * PCAP + (sg + 1) * 128],
                            WB[:, kcx * D + cg * 512: kcx * D + (cg + 1) * 512],
                            start=(kcx == 0), stop=(kcx == KC - 1)))
                kb.emit("pe", fns, ins=[rhmT, rWBq[qt]],
                        outs=[pss4[0][1], pss4[1][1], pss4[2][1], pss4[3][1]])
            for cg in range(4):
                ps, rps = pss4[cg]
                kb.op("dve", lambda e, cg=cg, ps=ps: e.tensor_copy(
                    out=eo_tm[:, cg * 512:(cg + 1) * 512], in_=ps[:, 0:512]),
                    ins=[rps], outs=[reotm])
            kb.op("act", lambda e, ei=ei, sg=sg: e.dma_start(
                out=s_eout[ei * PCAP + sg * 128: ei * PCAP + (sg + 1) * 128, :],
                in_=eo_tm[:]),
                ins=[reotm], outs=[r_eout], sem="seo")

    for qt in range(4):
        kb.op("gp", lambda e, qt=qt: e.dma_start(
            out=WA[:, qt * 4 * D:(qt + 1) * 4 * D],
            in_=w1_d[0][:, qt * 4 * D:(qt + 1) * 4 * D]),
            outs=[rWAq[qt]], sem="wa")
        kb.op("gp", lambda e, qt=qt: e.dma_start(
            out=WB[:, qt * 4 * D:(qt + 1) * 4 * D],
            in_=w2_d[0][:, qt * 4 * D:(qt + 1) * 4 * D]),
            outs=[rWBq[qt]], sem="wb")
    a3_deT_loads(0)
    for ei in range(8):
        a3_mm1(ei)
        if ei < 7:
            a3_deT_loads(ei + 1)
            for qt in range(4):
                kb.op("gp", lambda e, ei=ei, qt=qt: e.dma_start(
                    out=WA[:, qt * 4 * D:(qt + 1) * 4 * D],
                    in_=w1_d[ei + 1][:, qt * 4 * D:(qt + 1) * 4 * D]),
                    outs=[rWAq[qt]], sem="wa")
        if ei >= 1:
            a3_mm2(ei - 1)
            for qt in range(4):
                kb.op("gp", lambda e, ei=ei, qt=qt: e.dma_start(
                    out=WB[:, qt * 4 * D:(qt + 1) * 4 * D],
                    in_=w2_d[ei][:, qt * 4 * D:(qt + 1) * 4 * D]),
                    outs=[rWBq[qt]], sem="wb")
    a3_mm2(7)

    ofb = [(FView(hmTs[0][0]), Rec("of0", seed=hmTs[0][1].toks())),
           (FView(hmTs[1][0]), Rec("of1", seed=hmTs[1][1].toks()))]
    kb.op("dve", lambda e: e.memset(ffnb[0][0][:], 0.0), outs=[ffnb[0][1]])
    kb.op("dve", lambda e: e.memset(ffnb[1][0][:], 0.0), outs=[ffnb[1][1]])
    for cc2 in range(NCH):
        r_sx1[cc2].writers = [("sx1", kb.sems["sx1"][1])]
    r_eout.writers = [("seo", kb.sems["seo"][1])]
    for c in range(NCH):
        xt, rxt = xb4[c % 2]
        of, rof = ofb[c % 2]
        ffn_bf, rffn = ffnb[c % 2]
        kb.op("io", lambda e, c=c, xt=xt: e.dma_start(out=xt[:], in_=s_x1[c]),
              ins=[r_sx1[c]], outs=[rxt], sem="x0")
        kb.op("gp", lambda e, c=c, ffn_bf=ffn_bf: e.indirect_dma_start(
            out=ffn_bf[:], out_offset=None,
            in_=s_eout, in_offset=IOA(ap=slots[:, c:c + 1], axis=0),
            bounds_check=NSLOT - 1, oob_is_err=False),
            ins=[r_eout, rslots], outs=[rffn], sem="ga")
        kb.op("dve", lambda e, c=c, xt=xt, of=of, ffn_bf=ffn_bf: e.scalar_tensor_tensor(
            out=of[:], in0=ffn_bf[:], scalar=wtok[:, c:c + 1], in1=xt[:],
            op0=ALU.mult, op1=ALU.add),
            ins=[rffn, rwtok, rxt], outs=[rof])
        kb.op("io", lambda e, c=c, of=of: e.dma_start(
            out=y_out[c * 128:(c + 1) * 128, :], in_=of[:]),
            ins=[rof], outs=[], sem=("out0" if c % 2 == 0 else "out1"))
    seeds = [t for rr in (DeTs + hmTs) for t in rr[1].toks()] + \
        [t for r in (reotm,) for t in r.toks()]
    a3_ctx.close()

    nc.sync.wait_ge(kb.sems["out0"][0], kb.sems["out0"][1])
    nc.sync.wait_ge(kb.sems["out1"][0], kb.sems["out1"][1])
    ctx.close()
    return nc


_PROGRAM_CACHE = {}


def _enable_ldw_opt():
    try:
        from concourse.compiler_utils import get_compiler_flags, set_compiler_flags
        flags = [f.replace("--enable-ldw-opt=false", "--enable-ldw-opt=true")
                 for f in get_compiler_flags()]
        set_compiler_flags(flags)
    except Exception:
        pass


def _get_program():
    if "nc" not in _PROGRAM_CACHE:
        _PROGRAM_CACHE["nc"] = build_program()
    return _PROGRAM_CACHE["nc"]


def _host_w_layout(w):
    n = w.shape[1]
    return np.ascontiguousarray(
        w.reshape(KC, 128, n).transpose(1, 0, 2).reshape(128, KC * n))


def kernel(L0_attn_x, L0_attn_Wq, L0_attn_Wv, L0_attn_Wg, L0_attn_Wo, L0_attn_beta,
           L0_attn_ln_g, L0_attn_ln_b, L0_ffn_gate_weight, L0_ffn_w1, L0_ffn_w2):
    B, Sq, Dm = L0_attn_x.shape
    T = B * Sq
    xf = np.ascontiguousarray(np.asarray(L0_attn_x, dtype=np.float32).reshape(T, Dm))

    wq = _host_w_layout(np.asarray(L0_attn_Wq, dtype=np.float32))
    wv = _host_w_layout(np.asarray(L0_attn_Wv, dtype=np.float32))
    wg = _host_w_layout(np.asarray(L0_attn_Wg, dtype=np.float32))
    wo = _host_w_layout(np.asarray(L0_attn_Wo, dtype=np.float32))
    w1 = np.stack([_host_w_layout(np.asarray(L0_ffn_w1[e], dtype=np.float32))
                   for e in range(8)])
    w2 = np.stack([_host_w_layout(np.asarray(L0_ffn_w2[e], dtype=np.float32))
                   for e in range(8)])
    gw = np.asarray(L0_ffn_gate_weight, dtype=np.float32)
    gwT = np.ascontiguousarray(
        np.ascontiguousarray(gw.T).reshape(KC, 128, 8).transpose(1, 0, 2).reshape(128, KC * 8))

    beta = np.asarray(L0_attn_beta, dtype=np.float32).reshape(1, H)
    lng = np.asarray(L0_attn_ln_g, dtype=np.float32).reshape(1, Dm)
    lnb = np.asarray(L0_attn_ln_b, dtype=np.float32).reshape(1, Dm)

    ident = np.eye(128, dtype=np.float32).astype(ml_dtypes.bfloat16)
    sidx = np.arange(128)[:, None]
    tidx = np.arange(128)[None, :]
    tri1 = np.where(sidx <= tidx, (tidx - sidx).astype(np.float32), 0.0).astype(np.float32)
    tri16 = np.tile(tri1, (1, H))
    mask16 = np.tile((sidx <= tidx).astype(np.float32), (1, H)).astype(ml_dtypes.bfloat16)
    slt = (sidx < tidx).astype(np.float32).astype(ml_dtypes.bfloat16)
    iota = np.arange(128, dtype=np.float32).reshape(128, 1)
    iota8 = np.arange(8, dtype=np.float32).reshape(1, 8)
    tplus1 = np.tile(np.arange(1, 129, dtype=np.float32), H).reshape(1, H * 128)

    nc = _get_program()
    in_maps = []
    cpb = NCORES // B
    for k in range(NCORES):
        e8 = np.zeros((8, 1), dtype=np.float32)
        m8 = np.zeros((8, 1), dtype=np.float32)
        for j in range(NCORES):
            if j // cpb == k // cpb and j < k:
                e8[j, 0] = 2048.0 * (k - 1 - j)
                m8[j, 0] = 1.0
        in_maps.append(dict(
            x_own=xf[k * TOK:(k + 1) * TOK],
            wq=wq, wv=wv, wg=wg, wo=wo, w1=w1, w2=w2, gwt=gwT,
            beta=beta, lng=lng, lnb=lnb,
            ident=ident, tri16=tri16, mask16=mask16, slt=slt, iota=iota,
            iota8=iota8, tplus1=tplus1, e8=e8, m8=m8,
        ))
    res = run_bass_kernel_spmd(nc, in_maps, core_ids=list(range(NCORES)))
    out = np.concatenate([res.results[c]["y_out"] for c in range(NCORES)], axis=0)
    return out.reshape(B, Sq, Dm)


# version 30
# speedup vs baseline: 1.1108x; 1.1108x over previous
import numpy as np
import ml_dtypes
from contextlib import ExitStack

import concourse.bass as bass
import concourse.mybir as mybir
from concourse.bass_utils import run_bass_kernel_spmd
from concourse.bass_core_rust import IndirectOffsetOnAxis

F32 = mybir.dt.float32
BF16 = mybir.dt.bfloat16
U32 = mybir.dt.uint32
AF = mybir.ActivationFunctionType
ALU = mybir.AluOpType
AX = mybir.AxisListType

NCORES = 8
TOK = 2048
D = 2048
H, DH = 16, 128
NCH = 16
KC = 16
PCAP = 384
NSLOT = 8 * PCAP
NPREV = 3
BIG = 1.0e6
EPS = 1e-5

ENG = {"pe": "tensor", "act": "scalar", "dve": "vector", "io": "sync", "gp": "gpsimd"}
DEF_SEM = {"pe": "spe", "act": "sac", "dve": "sdv", "io": None, "gp": None}


class Rec:
    __slots__ = ("writers", "readers", "name")

    def __init__(self, name, seed=()):
        self.name = name
        self.writers = list(seed)
        self.readers = []

    def toks(self):
        return self.writers + self.readers


class KB:

    def __init__(self):
        self.nc = bass.Bass("TRN2", target_bir_lowering=False, debug=False)
        self.sems = {}
        self.waited = {}
        for nm in ("spe", "sac", "sdv"):
            self.sems[nm] = [self.nc.alloc_semaphore(name=nm), 0]

    def lane(self, nm):
        if nm not in self.sems:
            self.sems[nm] = [self.nc.alloc_semaphore(name=nm), 0]
        return nm

    def _wait(self, engname, toks):
        eng = getattr(self.nc, ENG[engname])
        best = {}
        for (nm, val) in toks:
            if val > best.get(nm, 0):
                best[nm] = val
        for nm, val in best.items():
            key = (engname, nm)
            if self.waited.get(key, 0) >= val:
                continue
            eng.wait_ge(self.sems[nm][0], val)
            self.waited[key] = val

    def emit(self, engname, fns, ins=(), outs=(), sem=None, last_only=False):
        toks = []
        for r in ins:
            toks += r.writers
        for r in outs:
            toks += r.toks()
        self._wait(engname, toks)
        eng = getattr(self.nc, ENG[engname])
        instr = None
        for fn in fns:
            instr = fn(eng)
        if sem is None:
            sem = DEF_SEM[engname]
            assert sem is not None, f"DMA op on {engname} must specify a lane"
        else:
            self.lane(sem)
        k = 1 if sem in ("spe", "sac", "sdv", "cc") else 16
        rec = self.sems[sem]
        instr.then_inc(rec[0], k)
        rec[1] += k
        tok = (sem, rec[1])
        for r in ins:
            r.readers.append(tok)
        for r in outs:
            r.writers = [tok]
            r.readers = []
        return tok

    def op(self, engname, fn, ins=(), outs=(), sem=None):
        return self.emit(engname, [fn], ins=ins, outs=outs, sem=sem)


def build_program():
    kb = KB()
    nc = kb.nc
    ctx = ExitStack()
    en = ctx.enter_context
    seeds = []

    ap = lambda t: t.ap()
    x_own = ap(nc.dram_tensor("x_own", [TOK, D], F32, kind="ExternalInput"))
    Wq_d = ap(nc.dram_tensor("wq", [128, KC * D], F32, kind="ExternalInput"))
    Wv_d = ap(nc.dram_tensor("wv", [128, KC * D], F32, kind="ExternalInput"))
    Wg_d = ap(nc.dram_tensor("wg", [128, KC * D], F32, kind="ExternalInput"))
    Wo_d = ap(nc.dram_tensor("wo", [128, KC * D], F32, kind="ExternalInput"))
    w1_d = ap(nc.dram_tensor("w1", [8, 128, KC * D], F32, kind="ExternalInput"))
    w2_d = ap(nc.dram_tensor("w2", [8, 128, KC * D], F32, kind="ExternalInput"))
    gwT_d = ap(nc.dram_tensor("gwt", [128, KC * 8], F32, kind="ExternalInput"))
    beta_d = ap(nc.dram_tensor("beta", [1, H], F32, kind="ExternalInput"))
    lng_d = ap(nc.dram_tensor("lng", [1, D], F32, kind="ExternalInput"))
    lnb_d = ap(nc.dram_tensor("lnb", [1, D], F32, kind="ExternalInput"))
    ident_d = ap(nc.dram_tensor("ident", [128, 128], BF16, kind="ExternalInput"))
    tri16_d = ap(nc.dram_tensor("tri16", [128, H * 128], F32, kind="ExternalInput"))
    mask16_d = ap(nc.dram_tensor("mask16", [128, H * 128], BF16, kind="ExternalInput"))
    slt_d = ap(nc.dram_tensor("slt", [128, 128], BF16, kind="ExternalInput"))
    iota_d = ap(nc.dram_tensor("iota", [128, 1], F32, kind="ExternalInput"))
    iota8_d = ap(nc.dram_tensor("iota8", [1, 8], F32, kind="ExternalInput"))
    tplus1_d = ap(nc.dram_tensor("tplus1", [1, H * 128], F32, kind="ExternalInput"))
    e8_d = ap(nc.dram_tensor("e8", [8, 1], F32, kind="ExternalInput"))
    m8_d = ap(nc.dram_tensor("m8", [8, 1], F32, kind="ExternalInput"))
    y_out = ap(nc.dram_tensor("y_out", [TOK, D], F32, kind="ExternalOutput"))
    u_dram_t = nc.dram_tensor("u_dram", [1, D], F32, kind="Internal")
    u_all_t = nc.dram_tensor("u_all", [8, D], F32, kind="Internal")
    u_dram = u_dram_t.ap()
    u_all = u_all_t.ap()

    s_xlnT = ap(nc.dram_tensor("s_xlnt", [NCH, 128, D], BF16, kind="Internal"))
    s_v = ap(nc.dram_tensor("s_v", [NCH, 128, D], BF16, kind="Internal"))
    s_g = ap(nc.dram_tensor("s_g", [NCH, 128, D], BF16, kind="Internal"))
    s_x1 = ap(nc.dram_tensor("s_x1", [NCH, 128, D], BF16, kind="Internal"))
    s_disp = ap(nc.dram_tensor("s_disp", [NSLOT, D], BF16, kind="Internal"))

    s_eout = ap(nc.dram_tensor("s_eout", [NSLOT, D], BF16, kind="Internal"))

    r_sxlnT = [Rec(f"sxlnT{c}") for c in range(NCH)]
    r_sv = [Rec(f"sv{c}") for c in range(NCH)]
    r_sg = [Rec(f"sg{c}") for c in range(NCH)]
    r_sx1 = [Rec(f"sx1{c}") for c in range(NCH)]
    r_disp = Rec("s_disp")
    rsc = Rec("sc_chain")
    rgc = Rec("gc_chain")
    r_eout = Rec("s_eout")

    IOA = IndirectOffsetOnAxis

    def SA(stack, name, shape, dt):
        t = stack.enter_context(nc.sbuf_tensor(name, shape, dt))
        return t, Rec(name, seed=seeds)

    attn_ctx = ExitStack()
    WA, rWA = SA(ctx, "WA", [128, KC * D], BF16)
    WB, rWB = SA(ctx, "WB", [128, KC * D], BF16)
    rWAq = [Rec(f"WAq{i}") for i in range(4)]
    rWBq = [Rec(f"WBq{i}") for i in range(4)]
    ident, rident = SA(ctx, "identsb", [128, 128], BF16)
    slt, rslt = SA(ctx, "sltsb", [128, 128], BF16)
    gwT, rgwT = SA(ctx, "gwT", [128, KC * 8], BF16)
    iota, riota = SA(ctx, "iotasb", [128, 1], F32)
    iota8bc, riota8 = SA(ctx, "iota8bc", [128, 8], F32)
    ones8, rones8 = SA(ctx, "ones8", [128, 8], F32)
    onesr_f, ronesr = SA(ctx, "onesr_f", [1, 128], F32)
    onesc_bf, ronesc = SA(ctx, "onesc_bf", [128, 1], BF16)
    lnl_bc, rlnl = SA(ctx, "lnl_bc", [128, H], F32)
    wdec_all, rwdec = SA(ctx, "wdec_all", [128, NCH * H], BF16)
    u_row, ru_row = SA(ctx, "u_row", [1, D], BF16)
    e8sb, re8 = SA(ctx, "e8sb", [8, 1], F32)
    m8sb, rm8 = SA(ctx, "m8sb", [8, 1], F32)
    w8f, rw8f = SA(ctx, "w8f", [8, H], F32)
    w8b, rw8b = SA(ctx, "w8b", [8, H], BF16)
    wtok, rwtok = SA(ctx, "wtok", [128, NCH], F32)
    slots, rslots = SA(ctx, "slots", [128, NCH], U32)
    runcnt, rruncnt = SA(ctx, "runcnt", [1, 8], F32)
    cnt8, rcnt8 = SA(ctx, "cnt8", [1, 8], F32)
    xbuf0, rx0 = SA(ctx, "xbuf0", [128, D], BF16)
    c_s1, rs1 = SA(ctx, "c_s1", [128, 1], F32)
    c_s2, rs2 = SA(ctx, "c_s2", [128, 1], F32)
    c_mu, rmu = SA(ctx, "c_mu", [128, 1], F32)
    c_msq, rmsq = SA(ctx, "c_msq", [128, 1], F32)
    c_var, rvar = SA(ctx, "c_var", [128, 1], F32)
    c_std, rstd_ = SA(ctx, "c_std", [128, 1], F32)
    c_rstd, rrstd = SA(ctx, "c_rstd", [128, 1], F32)
    c_nmr, rnmr = SA(ctx, "c_nmr", [128, 1], F32)
    lg8a, rlg8a = SA(ctx, "lg8a", [128, 8], F32)
    lg8b, rlg8b = SA(ctx, "lg8b", [128, 8], F32)
    t8a, rt8a = SA(ctx, "t8a", [128, 8], F32)
    t8b, rt8b = SA(ctx, "t8b", [128, 8], F32)
    oh8, roh8 = SA(ctx, "oh8", [128, 8], F32)
    oh8b, roh8b = SA(ctx, "oh8b", [128, 8], BF16)
    pexp8, rpexp8 = SA(ctx, "pexp8", [128, 8], F32)
    c_lmax, rlmax = SA(ctx, "c_lmax", [128, 1], F32)
    c_nlmax, rnlmax = SA(ctx, "c_nlmax", [128, 1], F32)
    c_sexp, rsexp = SA(ctx, "c_sexp", [128, 1], F32)
    c_rsum, rrsum = SA(ctx, "c_rsum", [128, 1], F32)
    c_ptop, rptop = SA(ctx, "c_ptop", [128, 1], F32)
    c_w, rcw = SA(ctx, "c_w", [128, 1], F32)
    c_rk1, rrk1 = SA(ctx, "c_rk1", [128, 1], F32)
    c_rk2, rrk2 = SA(ctx, "c_rk2", [128, 1], F32)
    c_rank, rrank = SA(ctx, "c_rank", [128, 1], F32)
    c_keep, rkeep = SA(ctx, "c_keep", [128, 1], F32)
    c_et, rcet = SA(ctx, "c_et", [128, 1], F32)
    c_sf, rcsf = SA(ctx, "c_sf", [128, 1], F32)
    c_sk, rcsk = SA(ctx, "c_sk", [128, 1], F32)
    c_bigk, rbigk = SA(ctx, "c_bigk", [128, 1], F32)
    c_sm, rcsm = SA(ctx, "c_sm", [128, 1], F32)
    c_ecol, recol = SA(ctx, "c_ecol", [128, 1], F32)
    c_mubf, rmubf = SA(ctx, "c_mubf", [128, 1], BF16)

    At_all, rAt = SA(attn_ctx, "At_all", [128, H * 128], BF16)
    P_row, rProw = SA(attn_ctx, "P_row", [1, H * 128], BF16)
    lng_bc, rlng = SA(attn_ctx, "lng_bc", [128, D], BF16)
    lnb_bc, rlnb = SA(attn_ctx, "lnb_bc", [128, D], BF16)
    c_stage, rcst = SA(attn_ctx, "c_stage", [1, H * 128], BF16)
    rcst4 = [Rec(f"cst{q4}") for q4 in range(4)]

    ps_tr = en(nc.psum_tensor("ps_tr", [128, 512], F32))
    rtrp = Rec("ps_tr")
    ps_sm = en(nc.psum_tensor("ps_sm", [128, 512], F32))
    rsm = Rec("ps_sm")
    ps_a0 = en(nc.psum_tensor("ps_a0", [128, 512], F32))
    ra0 = Rec("ps_a0")
    ps_a1 = en(nc.psum_tensor("ps_a1", [128, 512], F32))
    ra1 = Rec("ps_a1")
    ps_a2 = en(nc.psum_tensor("ps_a2", [128, 512], F32))
    ra2 = Rec("ps_a2")
    ps_a3 = en(nc.psum_tensor("ps_a3", [128, 512], F32))
    ra3 = Rec("ps_a3")
    ps_a4 = en(nc.psum_tensor("ps_a4", [128, 512], F32))
    ra4 = Rec("ps_a4")
    ps_a5 = en(nc.psum_tensor("ps_a5", [128, 512], F32))
    ra5 = Rec("ps_a5")

    setup_ctx = ExitStack()
    tri16, rtri16 = SA(setup_ctx, "tri16sb", [128, H * 128], F32)
    mask16, rmask16 = SA(setup_ctx, "mask16sb", [128, H * 128], BF16)
    lnlbig, rlnlbig = SA(setup_ctx, "lnlbig", [128, H * 128], F32)
    beta_bc, rbeta = SA(setup_ctx, "beta_bc", [128, H], F32)
    tmp16a, rtmp16a = SA(setup_ctx, "tmp16a", [128, H], F32)
    tmp16b, rtmp16b = SA(setup_ctx, "tmp16b", [128, H], F32)
    tp1, rtp1 = SA(setup_ctx, "tp1", [1, H * 128], F32)
    lnlexp, rlnlexp = SA(setup_ctx, "lnlexp", [1, H * 128], F32)
    prow_f, rprowf = SA(setup_ctx, "prow_f", [1, H * 128], F32)

    kb.op("io", lambda e: e.dma_start(out=ident[:], in_=ident_d), outs=[rident, rsc], sem="sc")
    kb.op("io", lambda e: e.dma_start(out=tri16[:], in_=tri16_d), outs=[rtri16, rsc], sem="sc")
    kb.op("io", lambda e: e.dma_start(out=mask16[:], in_=mask16_d), outs=[rmask16, rsc], sem="sc")
    kb.op("io", lambda e: e.dma_start(out=slt[:], in_=slt_d), outs=[rslt, rsc], sem="sc")
    kb.op("io", lambda e: e.dma_start(out=iota[:], in_=iota_d), outs=[riota, rsc], sem="sc")
    kb.op("io", lambda e: e.dma_start(out=iota8bc[:], in_=iota8_d.broadcast_to([128, 8])),
          outs=[riota8, rsc], sem="sc")
    kb.op("io", lambda e: e.dma_start(out=tp1[:], in_=tplus1_d), outs=[rtp1, rsc], sem="sc")
    kb.op("io", lambda e: e.dma_start(out=beta_bc[:], in_=beta_d.broadcast_to([128, H])),
          outs=[rbeta, rsc], sem="sc")
    kb.op("gp", lambda e: e.dma_start(out=lng_bc[:], in_=lng_d.broadcast_to([128, D])),
          outs=[rlng, rgc], sem="gc")
    kb.op("gp", lambda e: e.dma_start(out=lnb_bc[:], in_=lnb_d.broadcast_to([128, D])),
          outs=[rlnb, rgc], sem="gc")
    kb.op("gp", lambda e: e.dma_start(out=gwT[:], in_=gwT_d), outs=[rgwT, rgc], sem="gc")
    for qt in range(4):
        kb.op("gp", lambda e, qt=qt: e.dma_start(
            out=WA[:, qt * 4 * D:(qt + 1) * 4 * D],
            in_=Wv_d[:, qt * 4 * D:(qt + 1) * 4 * D]),
            outs=[rWAq[qt]], sem="wa")
        kb.op("gp", lambda e, qt=qt: e.dma_start(
            out=WB[:, qt * 4 * D:(qt + 1) * 4 * D],
            in_=Wg_d[:, qt * 4 * D:(qt + 1) * 4 * D]),
            outs=[rWBq[qt]], sem="wb")

    kb.op("dve", lambda e: e.memset(ones8[:], 1.0), outs=[rones8])
    kb.op("dve", lambda e: e.memset(runcnt[:], 0.0), outs=[rruncnt])
    kb.op("dve", lambda e: e.memset(onesr_f[:], 1.0), outs=[ronesr])
    kb.op("dve", lambda e: e.memset(onesc_bf[:], 1.0), outs=[ronesc])

    kb.op("act", lambda e: e.activation(out=tmp16a[:], in_=beta_bc[:], func=AF.Abs),
          ins=[rbeta], outs=[rtmp16a])
    kb.op("dve", lambda e: e.tensor_scalar_mul(tmp16a[:], tmp16a[:], -1.0),
          ins=[rtmp16a], outs=[rtmp16a])
    kb.op("act", lambda e: e.activation(out=tmp16a[:], in_=tmp16a[:], func=AF.Exp),
          ins=[rtmp16a], outs=[rtmp16a])
    kb.op("act", lambda e: e.activation(out=tmp16a[:], in_=tmp16a[:], func=AF.Ln, bias=1.0),
          ins=[rtmp16a], outs=[rtmp16a])
    kb.op("dve", lambda e: e.tensor_scalar(out=tmp16b[:], in0=beta_bc[:], scalar1=-1.0,
                                           scalar2=0.0, op0=ALU.mult, op1=ALU.max),
          ins=[rbeta], outs=[rtmp16b])
    kb.op("dve", lambda e: e.tensor_add(out=tmp16b[:], in0=tmp16b[:], in1=tmp16a[:]),
          ins=[rtmp16a, rtmp16b], outs=[rtmp16b])
    kb.op("dve", lambda e: e.tensor_scalar_mul(lnl_bc[:], tmp16b[:], -1.0),
          ins=[rtmp16b], outs=[rlnl])

    kb.op("dve", lambda e: e.tensor_copy(
        out=lnlexp[:].rearrange("o (h t) -> o h t", h=H),
        in_=lnl_bc[0:1, :].unsqueeze(2).broadcast_to([1, H, 128])),
        ins=[rlnl], outs=[rlnlexp])
    for cg in range(4):
        pscg = [ps_a0, ps_a1][cg % 2]
        rcg = [ra0, ra1][cg % 2]
        kb.emit("pe", [lambda e, cg=cg, pscg=pscg: e.matmul(
            pscg[:, 0:512], onesr_f[0:1, :], lnlexp[0:1, cg * 512:(cg + 1) * 512],
            start=True, stop=True)], ins=[ronesr, rlnlexp], outs=[rcg])
        kb.op("dve", lambda e, cg=cg, pscg=pscg: e.tensor_copy(
            out=lnlbig[:, cg * 512:(cg + 1) * 512], in_=pscg[:, 0:512]),
            ins=[rcg], outs=[rlnlbig])
    kb.op("dve", lambda e: e.tensor_mul(out=lnlbig[:], in0=lnlbig[:], in1=tri16[:]),
          ins=[rtri16, rlnlbig], outs=[rlnlbig])
    kb.op("act", lambda e: e.activation(out=lnlbig[:], in_=lnlbig[:], func=AF.Exp),
          ins=[rlnlbig], outs=[rlnlbig])
    kb.op("dve", lambda e: e.tensor_tensor(out=At_all[:], in0=lnlbig[:], in1=mask16[:],
                                           op=ALU.mult),
          ins=[rlnlbig, rmask16], outs=[rAt])

    kb.op("dve", lambda e: e.tensor_mul(out=prow_f[:], in0=tp1[:], in1=lnlexp[:]),
          ins=[rtp1, rlnlexp], outs=[rprowf])
    kb.op("act", lambda e: e.activation(out=prow_f[:], in_=prow_f[:], func=AF.Exp),
          ins=[rprowf], outs=[rprowf])
    kb.op("act", lambda e: e.copy(out=P_row[:], in_=prow_f[:]), ins=[rprowf], outs=[rProw])

    kb.op("io", lambda e: e.dma_start(out=e8sb[:], in_=e8_d), outs=[re8, rsc], sem="sc")
    kb.op("io", lambda e: e.dma_start(out=m8sb[:], in_=m8_d), outs=[rm8, rsc], sem="sc")
    kb.op("dve", lambda e: e.memset(u_row[:], 0.0), outs=[ru_row])
    for c in range(NCH):
        kb.op("dve", lambda e, c=c: e.tensor_scalar(
            out=c_s1[:], in0=iota[:], scalar1=-1.0, scalar2=float(2047 - 128 * c),
            op0=ALU.mult, op1=ALU.add),
            ins=[riota], outs=[rs1])
        kb.op("dve", lambda e: e.tensor_scalar(
            out=tmp16a[:], in0=lnl_bc[:], scalar1=c_s1[:, 0:1], scalar2=None,
            op0=ALU.mult),
            ins=[rlnl, rs1], outs=[rtmp16a])
        kb.op("act", lambda e: e.activation(out=tmp16a[:], in_=tmp16a[:], func=AF.Exp),
              ins=[rtmp16a], outs=[rtmp16a])
        kb.op("act", lambda e, c=c: e.copy(out=wdec_all[:, c * H:(c + 1) * H],
                                           in_=tmp16a[:]),
              ins=[rtmp16a], outs=[rwdec])

    seeds = [t for r in (rtri16, rmask16, rlnlbig, rbeta, rtmp16a, rtmp16b, rtp1,
                         rlnlexp, rprowf) for t in r.toks()]
    setup_ctx.close()

    a12_ctx = ExitStack()
    xbuf1, rx1 = SA(a12_ctx, "xbuf1", [128, D], BF16)
    xln_t = SA(a12_ctx, "xln0", [128, D], BF16)
    xlns = [xln_t, xln_t]
    xlnTs = [SA(a12_ctx, "xlnT0", [128, D], BF16), SA(a12_ctx, "xlnT1", [128, D], BF16)]
    v_t = SA(a12_ctx, "v0", [128, D], BF16)
    vsb = [v_t, v_t]
    g_t = SA(a12_ctx, "g0", [128, D], BF16)
    gsb = [g_t, g_t]
    q_f32, rq = SA(a12_ctx, "q_f32", [128, D], F32)
    S_sb, rS = SA(a12_ctx, "S_sb", [128, D], BF16)
    ysb = [SA(a12_ctx, "y0", [128, D], BF16), SA(a12_ctx, "y1", [128, D], BF16)]
    z_bf, rz = SA(a12_ctx, "z_bf", [128, D], BF16)
    q_bf, rqb = None, None
    d_s1, qs1 = SA(a12_ctx, "d_s1", [128, 1], F32)
    d_s2, qs2 = SA(a12_ctx, "d_s2", [128, 1], F32)
    d_mu, qmu = SA(a12_ctx, "d_mu", [128, 1], F32)
    d_msq, qmsq = SA(a12_ctx, "d_msq", [128, 1], F32)
    d_var, qvar = SA(a12_ctx, "d_var", [128, 1], F32)
    d_std, qstd = SA(a12_ctx, "d_std", [128, 1], F32)
    d_rstd, qrstd = SA(a12_ctx, "d_rstd", [128, 1], F32)
    d_nmr, qnmr = SA(a12_ctx, "d_nmr", [128, 1], F32)
    s1ps = [SA(a12_ctx, "s1p0", [128, 4], F32), SA(a12_ctx, "s1p1", [128, 4], F32)]
    s2ps = [SA(a12_ctx, "s2p0", [128, 4], F32), SA(a12_ctx, "s2p1", [128, 4], F32)]

    class BCView:
        def __init__(self, t, off=0):
            self.t = t
            self.off = off
        def __getitem__(self, key):
            if key == slice(None):
                return self.t[:, self.off:self.off + D // 2].bitcast(BF16)
            _, cs = key
            return self.t[:, self.off + cs.start // 2: self.off + cs.stop // 2].bitcast(BF16)

    q_bf, rqb = BCView(q_f32, 1024), rq

    xb = [(xbuf0, rx0), (xbuf1, rx1)]
    EV = ["act", "dve", "act", "dve"]

    def pcopy(engname, dst_ap_fn, ps, rps, dst_rec):
        if engname == "act":
            kb.op("act", lambda e: e.copy(out=dst_ap_fn(), in_=ps[:, 0:512]),
                  ins=[rps], outs=[dst_rec])
        else:
            kb.op("dve", lambda e: e.tensor_copy(out=dst_ap_fn(), in_=ps[:, 0:512]),
                  ins=[rps], outs=[dst_rec])

    def ln_stats(src_rec, src_tile, sq_tile, sq_rec, cols):
        (s1_, t1), (s2_, t2), (mu_, tmu), (msq_, tmsq), (var_, tvar), (std_, tstd), \
            (rstdc, trstd), (nmrc, tnmr) = cols
        kb.op("dve", lambda e: e.reduce_sum(out=s1_[:], in_=src_tile[:], axis=AX.X),
              ins=[src_rec], outs=[t1])
        kb.op("act", lambda e: e.activation(out=sq_tile[:], in_=src_tile[:], func=AF.Square,
                                            accum_out=s2_[:]),
              ins=[src_rec], outs=[sq_rec, t2])
        kb.op("dve", lambda e: e.tensor_scalar_mul(mu_[:], s1_[:], 1.0 / D),
              ins=[t1], outs=[tmu])
        kb.op("dve", lambda e: e.tensor_scalar(out=msq_[:], in0=mu_[:],
                                               scalar1=mu_[:, 0:1], scalar2=EPS,
                                               op0=ALU.mult, op1=ALU.subtract),
              ins=[tmu], outs=[tmsq])
        kb.op("dve", lambda e: e.tensor_scalar(out=var_[:], in0=s2_[:], scalar1=1.0 / D,
                                               scalar2=msq_[:, 0:1], op0=ALU.mult,
                                               op1=ALU.subtract),
              ins=[t2, tmsq], outs=[tvar])
        kb.op("act", lambda e: e.activation(out=std_[:], in_=var_[:], func=AF.Sqrt),
              ins=[tvar], outs=[tstd])
        kb.op("dve", lambda e: e.reciprocal(out=rstdc[:], in_=std_[:]),
              ins=[tstd], outs=[trstd])
        kb.op("dve", lambda e: e.tensor_scalar(out=nmrc[:], in0=mu_[:],
                                               scalar1=rstdc[:, 0:1], scalar2=-1.0,
                                               op0=ALU.mult, op1=ALU.mult),
              ins=[tmu, trstd], outs=[tnmr])
        return (rstdc, trstd), (nmrc, tnmr)

    colsA = [(c_s1, rs1), (c_s2, rs2), (c_mu, rmu), (c_msq, rmsq), (c_var, rvar),
             (c_std, rstd_), (c_rstd, rrstd), (c_nmr, rnmr)]
    colsB = [(d_s1, qs1), (d_s2, qs2), (d_mu, qmu), (d_msq, qmsq), (d_var, qvar),
             (d_std, qstd), (d_rstd, qrstd), (d_nmr, qnmr)]

    def transpose16(src_tile, src_rec, dst_tile, dst_rec):
        for rnd in range(4):
            fns = []
            for j in range(4):
                kcx = rnd * 4 + j
                fns.append(lambda e, kcx=kcx, j=j: e.matmul(
                    ps_tr[:, j * 128:(j + 1) * 128],
                    src_tile[:, kcx * 128:(kcx + 1) * 128], ident[:],
                    start=True, stop=True))
            kb.emit("pe", fns, ins=[src_rec, rident], outs=[rtrp])
            if rnd % 2 == 0:
                kb.op("act", lambda e, rnd=rnd: e.copy(
                    out=dst_tile[:, rnd * 512:(rnd + 1) * 512], in_=ps_tr[:]),
                    ins=[rtrp], outs=[dst_rec])
            else:
                kb.op("dve", lambda e, rnd=rnd: e.tensor_copy(
                    out=dst_tile[:, rnd * 512:(rnd + 1) * 512], in_=ps_tr[:]),
                    ins=[rtrp], outs=[dst_rec])

    pss4 = [(ps_a0, ra0), (ps_a1, ra1), (ps_a0, ra0), (ps_a1, ra1)]

    def big_mm(lhsT_tile, lhsT_rec, w_tile, w_rec, evac, four_banks=True,
               banks=(0, 1)):
        if four_banks:
            for qt in range(4):
                fns = []
                for kcq in range(4):
                    kcx = qt * 4 + kcq
                    for cg in range(4):
                        ps, rps = pss4[cg]
                        fns.append(lambda e, kcx=kcx, cg=cg, ps=ps: e.matmul(
                            ps[:, 0:512],
                            lhsT_tile[:, kcx * 128:(kcx + 1) * 128],
                            w_tile[:, kcx * D + cg * 512: kcx * D + (cg + 1) * 512],
                            start=(kcx == 0), stop=(kcx == KC - 1)))
                kb.emit("pe", fns, ins=[lhsT_rec, w_rec[qt]],
                        outs=[pss4[0][1], pss4[1][1], pss4[2][1], pss4[3][1]])
            for cg in range(4):
                ps, rps = pss4[cg]
                evac(cg, ps, rps)
        else:
            for pair in range(2):
                cga, cgb = 2 * pair, 2 * pair + 1
                psa, rpsa = pss4[banks[0]]
                psb_, rpsb = pss4[banks[1]]
                for qt in range(4):
                    fns = []
                    for kcq in range(4):
                        kcx = qt * 4 + kcq
                        for cg, ps in ((cga, psa), (cgb, psb_)):
                            fns.append(lambda e, kcx=kcx, cg=cg, ps=ps: e.matmul(
                                ps[:, 0:512],
                                lhsT_tile[:, kcx * 128:(kcx + 1) * 128],
                                w_tile[:, kcx * D + cg * 512: kcx * D + (cg + 1) * 512],
                                start=(kcx == 0), stop=(kcx == KC - 1)))
                    kb.emit("pe", fns, ins=[lhsT_rec, w_rec[qt]], outs=[rpsa, rpsb])
                evac(cga, psa, rpsa)
                evac(cgb, psb_, rpsb)

    def a1_step(c):
        xt, rxt = xb[c % 2]
        xlnp, rxlnp = xlns[c % 2]
        xlnTp, rxlnTp = xlnTs[c % 2]
        vp, rvp = vsb[c % 2]
        gp_, rgp = gsb[c % 2]
        kb.op("gp", lambda e, c=c, xt=xt: e.dma_start(
            out=xt[:], in_=x_own[c * 128:(c + 1) * 128, :]), outs=[rxt],
            sem="x0")
        (rst, trst), (nmr_, tnmr) = ln_stats(rxt, xt, z_bf, rz, colsA)
        kb.op("dve", lambda e, xt=xt, xlnp=xlnp: e.tensor_scalar(
            out=xlnp[:], in0=xt[:], scalar1=rst[:, 0:1], scalar2=nmr_[:, 0:1],
            op0=ALU.mult, op1=ALU.add),
            ins=[rxt, trst, tnmr], outs=[rxlnp])
        transpose16(xlnp, rxlnp, xlnTp, rxlnTp)
        kb.op("io", lambda e, c=c, xlnTp=xlnTp: e.dma_start(out=s_xlnT[c], in_=xlnTp[:]),
              ins=[rxlnTp], outs=[r_sxlnT[c]], sem="sxt")

        def evac_v(cg, ps, rps):
            pcopy(EV[cg], lambda cg=cg, vp=vp: vp[:, cg * 512:(cg + 1) * 512], ps, rps, rvp)
        big_mm(xlnTp, rxlnTp, WA, rWAq, evac_v, four_banks=False)
        kb.op("io", lambda e, c=c, vp=vp: e.dma_start(out=s_v[c], in_=vp[:]),
              ins=[rvp], outs=[r_sv[c]], sem="sv")

        for q4 in range(4):
            bank, rbank = psU[q4]
            fns = [lambda e, h=h, vp=vp, bank=bank: e.matmul(
                bank[0:1, (h % 4) * 128:(h % 4) * 128 + 128],
                wdec_all[:, c * H + h: c * H + h + 1],
                vp[:, h * 128:(h + 1) * 128],
                start=True, stop=True) for h in range(4 * q4, 4 * q4 + 4)]
            kb.emit("pe", fns, ins=[rwdec, rvp], outs=[rbank])
            kb.op("dve", lambda e, q4=q4, bank=bank: e.tensor_tensor(
                out=u_row[0:1, q4 * 512:(q4 + 1) * 512], in0=bank[0:1, 0:512],
                in1=u_row[0:1, q4 * 512:(q4 + 1) * 512], op=ALU.add),
                ins=[rbank], outs=[ru_row])

        def evac_g(cg, ps, rps):
            kb.op("act", lambda e, cg=cg, ps=ps, gp_=gp_: e.activation(
                out=gp_[:, cg * 512:(cg + 1) * 512], in_=ps[:, 0:512], func=AF.Silu),
                ins=[rps], outs=[rgp])
        big_mm(xlnTp, rxlnTp, WB, rWBq, evac_g, four_banks=False)
        kb.op("io", lambda e, c=c, gp_=gp_: e.dma_start(out=s_g[c], in_=gp_[:]),
              ins=[rgp], outs=[r_sg[c]], sem="sg")

    pss4[:] = [(ps_a0, ra0), (ps_a1, ra1), (ps_a2, ra2), (ps_a3, ra3)]
    pssS = [(ps_a4, ra4), (ps_a5, ra5), (ps_a4, ra4), (ps_a5, ra5)]
    psU = [(ps_a2, ra2), (ps_a3, ra3), (ps_a4, ra4), (ps_a5, ra5)]
    zT = BCView(q_f32, 0)
    rzT = Rec("zT", seed=rq.toks())

    for c in range(NCH):
        a1_step(c)

    for qt in range(4):
        kb.op("gp", lambda e, qt=qt: e.dma_start(
            out=WA[:, qt * 4 * D:(qt + 1) * 4 * D],
            in_=Wq_d[:, qt * 4 * D:(qt + 1) * 4 * D]),
            outs=[rWAq[qt]], sem="wa")
        kb.op("gp", lambda e, qt=qt: e.dma_start(
            out=WB[:, qt * 4 * D:(qt + 1) * 4 * D],
            in_=Wo_d[:, qt * 4 * D:(qt + 1) * 4 * D]),
            outs=[rWBq[qt]], sem="wb")

    r_ud = Rec("u_dram")
    r_cc = Rec("u_all")
    kb.op("gp", lambda e: e.dma_start(out=u_dram, in_=u_row[0:1, :]),
          ins=[ru_row], outs=[r_ud], sem="ud")
    kb.op("gp", lambda e: e.collective_compute(
        "AllGather", ALU.bypass, replica_groups=[list(range(NCORES))],
        ins=[u_dram_t.ap().opt()], outs=[u_all_t.ap().opt()]),
        ins=[r_ud], outs=[r_cc], sem="cc")
    u8_t, r_u8 = xlnTs[0]
    kb.op("gp", lambda e: e.dma_start(out=u8_t[0:8, :], in_=u_all),
          ins=[r_cc], outs=[r_u8], sem="u8")
    kb.op("dve", lambda e: e.tensor_scalar(
        out=w8f[:], in0=lnl_bc[0:8, :], scalar1=e8sb[0:8, 0:1], scalar2=None,
        op0=ALU.mult),
        ins=[rlnl, re8], outs=[rw8f])
    kb.op("act", lambda e: e.activation(out=w8f[:], in_=w8f[:], func=AF.Exp),
          ins=[rw8f], outs=[rw8f])
    kb.op("dve", lambda e: e.tensor_scalar(
        out=w8f[:], in0=w8f[:], scalar1=m8sb[0:8, 0:1], scalar2=None, op0=ALU.mult),
        ins=[rw8f, rm8], outs=[rw8f])
    kb.op("act", lambda e: e.copy(out=w8b[:], in_=w8f[:]), ins=[rw8f], outs=[rw8b])
    for q4 in range(4):
        psb, rpbh = pssS[q4]
        fns = [lambda e, h=h, psb=psb: e.matmul(
            psb[0:1, (h % 4) * 128:(h % 4) * 128 + 128],
            w8b[0:8, h:h + 1],
            u8_t[0:8, h * 128:(h + 1) * 128],
            start=True, stop=True) for h in range(4 * q4, 4 * q4 + 4)]
        kb.emit("pe", fns, ins=[rw8b, r_u8], outs=[rpbh])
        kb.op("act", lambda e, q4=q4, psb=psb: e.copy(
            out=c_stage[0:1, q4 * 512:(q4 + 1) * 512], in_=psb[0:1, :]),
            ins=[rpbh], outs=[rcst4[q4]])

    for cc2 in range(NCH):
        r_sxlnT[cc2].writers = [("sxt", kb.sems["sxt"][1])]
        r_sv[cc2].writers = [("sv", kb.sems["sv"][1])]
        r_sg[cc2].writers = [("sg", kb.sems["sg"][1])]

    def a2_s1a(c):
        xlnTp, rxlnTp = xlnTs[c % 2]
        vp, rvp = vsb[c % 2]
        yp, ryp = ysb[c % 2]
        kb.op("io", lambda e, c=c, xlnTp=xlnTp: e.dma_start(out=xlnTp[:], in_=s_xlnT[c]),
              ins=[r_sxlnT[c]], outs=[rxlnTp], sem="lxt")
        kb.op("io", lambda e, c=c, vp=vp: e.dma_start(out=vp[:], in_=s_v[c]),
              ins=[r_sv[c]], outs=[rvp], sem="lv")

        def evac_q(cg, ps, rps):
            pcopy(EV[cg], lambda cg=cg: q_bf[:, cg * 512:(cg + 1) * 512], ps, rps, rqb)
        big_mm(xlnTp, rxlnTp, WA, rWAq, evac_q, four_banks=False, banks=(0, 1))

        for q4 in range(4):
            psb, rpbh = pssS[q4]
            fns = []
            for h in range(4 * q4, 4 * q4 + 4):
                fns.append(lambda e, h=h, vp=vp, psb=psb: e.matmul(
                    psb[:, (h % 4) * 128:(h % 4) * 128 + 128],
                    At_all[:, h * 128:(h + 1) * 128],
                    vp[:, h * 128:(h + 1) * 128],
                    start=True, stop=False))
                fns.append(lambda e, h=h, psb=psb: e.matmul(
                    psb[:, (h % 4) * 128:(h % 4) * 128 + 128],
                    P_row[0:1, h * 128:(h + 1) * 128],
                    c_stage[0:1, h * 128:(h + 1) * 128],
                    start=False, stop=True))
            kb.emit("pe", fns, ins=[rAt, rvp, rProw, rcst4[q4]], outs=[rpbh])
            if q4 % 2 == 0:
                kb.op("act", lambda e, q4=q4, psb=psb: e.copy(
                    out=S_sb[:, q4 * 512:(q4 + 1) * 512], in_=psb[:, 0:512]),
                    ins=[rpbh], outs=[rS])
            else:
                kb.op("dve", lambda e, q4=q4, psb=psb: e.tensor_copy(
                    out=S_sb[:, q4 * 512:(q4 + 1) * 512], in_=psb[:, 0:512]),
                    ins=[rpbh], outs=[rS])
            kb.op("io", lambda e, q4=q4: e.dma_start(
                out=c_stage[0:1, q4 * 512:(q4 + 1) * 512],
                in_=S_sb[127:128, q4 * 512:(q4 + 1) * 512]),
                ins=[rS], outs=[rcst4[q4]], sem="row")
            s1p, rs1p = s1ps[c % 2]
            s2p, rs2p = s2ps[c % 2]
            kb.op("dve", lambda e, q4=q4, yp=yp: e.tensor_tensor(
                out=yp[:, q4 * 512:(q4 + 1) * 512],
                in0=S_sb[:, q4 * 512:(q4 + 1) * 512],
                in1=q_bf[:, q4 * 512:(q4 + 1) * 512], op=ALU.mult),
                ins=[rqb, rS], outs=[ryp])
            kb.op("dve", lambda e, q4=q4, yp=yp, s1p=s1p: e.reduce_sum(
                out=s1p[:, q4:q4 + 1], in_=yp[:, q4 * 512:(q4 + 1) * 512], axis=AX.X),
                ins=[ryp], outs=[rs1p])
            kb.op("act", lambda e, q4=q4, yp=yp, s2p=s2p: e.activation(
                out=z_bf[:, q4 * 512:(q4 + 1) * 512],
                in_=yp[:, q4 * 512:(q4 + 1) * 512], func=AF.Square,
                accum_out=s2p[:, q4:q4 + 1]),
                ins=[ryp], outs=[rz, rs2p])

    def a2_s1b(c):
        xt, rxt = xb[c % 2]
        yp, ryp = ysb[c % 2]
        gp_, rgp = gsb[c % 2]
        kb.op("io", lambda e, c=c, gp_=gp_: e.dma_start(out=gp_[:], in_=s_g[c]),
              ins=[r_sg[c]], outs=[rgp], sem="lg")
        kb.op("gp", lambda e, c=c, xt=xt: e.dma_start(
            out=xt[:], in_=x_own[c * 128:(c + 1) * 128, :]), outs=[rxt], sem="x0")
        s1p, rs1p = s1ps[c % 2]
        s2p, rs2p = s2ps[c % 2]
        (s1_, t1), (s2_, t2), (mu_, tmu), (msq_, tmsq), (var_, tvar), \
            (std_, tstd), (rstdc, trst2), (nmrc, tnmr2) = colsA
        kb.op("dve", lambda e, s1p=s1p: e.reduce_sum(out=s1_[:], in_=s1p[:], axis=AX.X),
              ins=[rs1p], outs=[t1])
        kb.op("dve", lambda e, s2p=s2p: e.reduce_sum(out=s2_[:], in_=s2p[:], axis=AX.X),
              ins=[rs2p], outs=[t2])
        kb.op("dve", lambda e: e.tensor_scalar_mul(mu_[:], s1_[:], 1.0 / D),
              ins=[t1], outs=[tmu])
        kb.op("dve", lambda e: e.tensor_scalar(out=msq_[:], in0=mu_[:],
                                               scalar1=mu_[:, 0:1], scalar2=EPS,
                                               op0=ALU.mult, op1=ALU.subtract),
              ins=[tmu], outs=[tmsq])
        kb.op("dve", lambda e: e.tensor_scalar(out=var_[:], in0=s2_[:], scalar1=1.0 / D,
                                               scalar2=msq_[:, 0:1], op0=ALU.mult,
                                               op1=ALU.subtract),
              ins=[t2, tmsq], outs=[tvar])
        kb.op("act", lambda e: e.activation(out=std_[:], in_=var_[:], func=AF.Sqrt),
              ins=[tvar], outs=[tstd])
        kb.op("dve", lambda e: e.reciprocal(out=rstdc[:], in_=std_[:]),
              ins=[tstd], outs=[trst2])
        kb.op("dve", lambda e: e.tensor_scalar(out=nmrc[:], in0=mu_[:],
                                               scalar1=rstdc[:, 0:1], scalar2=-1.0,
                                               op0=ALU.mult, op1=ALU.mult),
              ins=[tmu, trst2], outs=[tnmr2])
        rst, trst = rstdc, trst2
        nmr_, tnmr = nmrc, tnmr2
        kb.op("act", lambda e, rst=rst, nmr_=nmr_, yp=yp: e.activation(
            out=z_bf[:], in_=yp[:], func=AF.Identity,
            scale=rst[:, 0:1], bias=nmr_[:, 0:1]),
            ins=[ryp, trst, tnmr], outs=[rz])
        kb.op("dve", lambda e, yp=yp: e.tensor_mul(out=yp[:], in0=z_bf[:], in1=lng_bc[:]),
              ins=[rz, rlng], outs=[ryp])
        kb.op("dve", lambda e, yp=yp: e.tensor_add(out=z_bf[:], in0=yp[:], in1=lnb_bc[:]),
              ins=[ryp, rlnb], outs=[rz])
        kb.op("dve", lambda e, gp_=gp_, yp=yp: e.tensor_mul(out=yp[:], in0=z_bf[:], in1=gp_[:]),
              ins=[rz, rgp], outs=[ryp])
        transpose16(yp, ryp, zT, rzT)

        def evac_o(cg, ps, rps):
            kb.op("dve", lambda e, cg=cg, ps=ps, xt=xt: e.tensor_tensor(
                out=xt[:, cg * 512:(cg + 1) * 512], in0=ps[:, 0:512],
                in1=xt[:, cg * 512:(cg + 1) * 512], op=ALU.add),
                ins=[rps, rxt], outs=[rxt])
        big_mm(zT, rzT, WB, rWBq, evac_o, four_banks=False, banks=(2, 3))
        kb.op("io", lambda e, c=c, xt=xt: e.dma_start(out=s_x1[c], in_=xt[:]),
              ins=[rxt], outs=[r_sx1[c]], sem="sx1")

    def a2_s2a(c):
        xt, rxt = xb[c % 2]
        xlnp, rxlnp = xlns[c % 2]
        xlnTp, rxlnTp = xlnTs[c % 2]
        (rst, trst), (nmr_, tnmr) = ln_stats(rxt, xt, z_bf, rz, colsB)
        kb.op("dve", lambda e, rst=rst, nmr_=nmr_, xlnp=xlnp, xt=xt: e.tensor_scalar(
            out=xlnp[:], in0=xt[:], scalar1=rst[:, 0:1], scalar2=nmr_[:, 0:1],
            op0=ALU.mult, op1=ALU.add),
            ins=[rxt, trst, tnmr], outs=[rxlnp])
        transpose16(xlnp, rxlnp, xlnTp, rxlnTp)
        fns = []
        for kcx in range(KC):
            fns.append(lambda e, kcx=kcx, xlnTp=xlnTp: e.matmul(
                ps_sm[:, 0:8], xlnTp[:, kcx * 128:(kcx + 1) * 128],
                gwT[:, kcx * 8:(kcx + 1) * 8], start=(kcx == 0), stop=(kcx == KC - 1)))
        kb.emit("pe", fns, ins=[rxlnTp, rgwT], outs=[rsm])
        lgp, rlgp = (lg8a, rlg8a) if c % 2 == 0 else (lg8b, rlg8b)
        kb.op("dve", lambda e, lgp=lgp: e.tensor_copy(out=lgp[:], in_=ps_sm[:, 0:8]),
              ins=[rsm], outs=[rlgp])
    def a2_s2b(c):
        xlnp, rxlnp = xlns[c % 2]
        lgp, rlgp = (lg8a, rlg8a) if c % 2 == 0 else (lg8b, rlg8b)
        kb.op("dve", lambda e: e.reduce_max(out=c_lmax[:], in_=lgp[:], axis=AX.X),
              ins=[rlgp], outs=[rlmax])
        kb.op("dve", lambda e: e.tensor_scalar_mul(c_nlmax[:], c_lmax[:], -1.0),
              ins=[rlmax], outs=[rnlmax])
        kb.op("act", lambda e: e.activation(out=pexp8[:], in_=lgp[:], func=AF.Exp,
                                            bias=c_nlmax[:, 0:1], scale=1.0,
                                            accum_out=c_sexp[:]),
              ins=[rlgp, rnlmax], outs=[rpexp8, rsexp])
        kb.op("dve", lambda e: e.reciprocal(out=c_rsum[:], in_=c_sexp[:]),
              ins=[rsexp], outs=[rrsum])
        kb.op("dve", lambda e: e.tensor_scalar(out=oh8[:], in0=lgp[:],
                                               scalar1=c_lmax[:, 0:1], scalar2=None,
                                               op0=ALU.is_ge),
              ins=[rlgp, rlmax], outs=[roh8])
        kb.op("dve", lambda e: e.tensor_tensor_scan(out=t8a[:], data0=ones8[:], data1=oh8[:],
                                                    initial=0.0, op0=ALU.mult, op1=ALU.add),
              ins=[rones8, roh8], outs=[rt8a])
        kb.op("dve", lambda e: e.tensor_sub(out=t8a[:], in0=t8a[:], in1=oh8[:]),
              ins=[rt8a, roh8], outs=[rt8a])
        kb.op("dve", lambda e: e.tensor_single_scalar(out=t8a[:], in_=t8a[:], scalar=0.0,
                                                      op=ALU.is_equal),
              ins=[rt8a], outs=[rt8a])
        kb.op("dve", lambda e: e.tensor_mul(out=oh8[:], in0=oh8[:], in1=t8a[:]),
              ins=[roh8, rt8a], outs=[roh8])
        kb.op("act", lambda e: e.copy(out=oh8b[:], in_=oh8[:]), ins=[roh8], outs=[roh8b])
        kb.op("dve", lambda e: e.tensor_mul(out=t8b[:], in0=oh8[:], in1=pexp8[:]),
              ins=[roh8, rpexp8], outs=[rt8b])
        kb.op("dve", lambda e: e.reduce_sum(out=c_ptop[:], in_=t8b[:], axis=AX.X),
              ins=[rt8b], outs=[rptop])
        kb.op("dve", lambda e: e.tensor_mul(out=c_w[:], in0=c_ptop[:], in1=c_rsum[:]),
              ins=[rptop, rrsum], outs=[rcw])
        kb.emit("pe", [lambda e: e.matmul(ps_sm[:, 8:16], slt[:], oh8b[:],
                                          start=True, stop=True)],
                ins=[rslt, roh8b], outs=[rsm])
        kb.emit("pe", [lambda e: e.matmul(ps_sm[:, 16:24], onesr_f[0:1, :], runcnt[0:1, :],
                                          start=True, stop=True)],
                ins=[ronesr, rruncnt], outs=[rsm])
        kb.op("dve", lambda e: e.tensor_mul(out=t8a[:], in0=oh8[:], in1=ps_sm[:, 8:16]),
              ins=[roh8, rsm], outs=[rt8a])
        kb.op("dve", lambda e: e.reduce_sum(out=c_rk1[:], in_=t8a[:], axis=AX.X),
              ins=[rt8a], outs=[rrk1])
        kb.op("dve", lambda e: e.tensor_mul(out=t8b[:], in0=oh8[:], in1=ps_sm[:, 16:24]),
              ins=[roh8, rsm], outs=[rt8b])
        kb.op("dve", lambda e: e.reduce_sum(out=c_rk2[:], in_=t8b[:], axis=AX.X),
              ins=[rt8b], outs=[rrk2])
        kb.op("dve", lambda e: e.tensor_add(out=c_rank[:], in0=c_rk1[:], in1=c_rk2[:]),
              ins=[rrk1, rrk2], outs=[rrank])
        kb.op("dve", lambda e: e.tensor_mul(out=t8a[:], in0=oh8[:], in1=iota8bc[:]),
              ins=[roh8, riota8], outs=[rt8a])
        kb.op("dve", lambda e: e.reduce_sum(out=c_et[:], in_=t8a[:], axis=AX.X),
              ins=[rt8a], outs=[rcet])
        kb.op("dve", lambda e: e.tensor_single_scalar(out=c_keep[:], in_=c_rank[:],
                                                      scalar=float(PCAP), op=ALU.is_lt),
              ins=[rrank], outs=[rkeep])
        kb.op("dve", lambda e, c=c: e.tensor_scalar(out=wtok[:, c:c + 1], in0=c_w[:],
                                                    scalar1=c_keep[:, 0:1], scalar2=None,
                                                    op0=ALU.mult),
              ins=[rcw, rkeep], outs=[rwtok])
        kb.op("dve", lambda e: e.tensor_scalar(out=c_sf[:], in0=c_et[:], scalar1=float(PCAP),
                                               scalar2=c_rank[:, 0:1], op0=ALU.mult,
                                               op1=ALU.add),
              ins=[rcet, rrank], outs=[rcsf])
        kb.op("dve", lambda e: e.tensor_scalar(out=c_sk[:], in0=c_sf[:],
                                               scalar1=c_keep[:, 0:1], scalar2=None,
                                               op0=ALU.mult),
              ins=[rcsf, rkeep], outs=[rcsk])
        kb.op("dve", lambda e: e.tensor_scalar(out=c_bigk[:], in0=c_keep[:], scalar1=-BIG,
                                               scalar2=BIG, op0=ALU.mult, op1=ALU.add),
              ins=[rkeep], outs=[rbigk])
        kb.op("dve", lambda e: e.tensor_add(out=c_sm[:], in0=c_sk[:], in1=c_bigk[:]),
              ins=[rcsk, rbigk], outs=[rcsm])
        kb.op("dve", lambda e, c=c: e.tensor_copy(out=slots[:, c:c + 1], in_=c_sm[:]),
              ins=[rcsm], outs=[rslots])
        kb.emit("pe", [lambda e: e.matmul(ps_sm[0:1, 24:32], onesc_bf[:], oh8b[:],
                                          start=True, stop=True)],
                ins=[ronesc, roh8b], outs=[rsm])
        kb.op("dve", lambda e: e.tensor_copy(out=cnt8[:], in_=ps_sm[0:1, 24:32]),
              ins=[rsm], outs=[rcnt8])
        kb.op("dve", lambda e: e.tensor_add(out=runcnt[:], in0=runcnt[:], in1=cnt8[:]),
              ins=[rcnt8, rruncnt], outs=[rruncnt])
        tok = kb.op("gp", lambda e, c=c, xlnp=xlnp: e.indirect_dma_start(
            out=s_disp, out_offset=IOA(ap=slots[:, c:c + 1], axis=0),
            in_=xlnp[:], in_offset=None,
            bounds_check=NSLOT - 1, oob_is_err=False),
            ins=[rxlnp, rslots], sem="ind")
        r_disp.writers.append(tok)

    a2_s1a(0)
    a2_s1a(1)
    a2_s1b(0)
    for c in range(2, NCH):
        a2_s1a(c)
        a2_s2a(c - 2)
        a2_s1b(c - 1)
        a2_s2b(c - 2)
    a2_s2a(NCH - 2)
    a2_s1b(NCH - 1)
    a2_s2b(NCH - 2)
    a2_s2a(NCH - 1)
    a2_s2b(NCH - 1)

    seeds = [t for rr in (xlns + xlnTs + vsb + gsb + ysb) for t in rr[1].toks()] + \
        [t for r in (rq, rx1, rS, rz, rzT, rAt, rProw, rlng, rlnb, rcst, *rcst4)
         for t in r.toks()]
    a12_ctx.close()
    attn_ctx.close()

    a3_ctx = ExitStack()
    DeTs = [SA(a3_ctx, "DeT0", [128, KC * PCAP], BF16),
            SA(a3_ctx, "DeT1", [128, KC * PCAP], BF16)]
    rDeTq = [[Rec(f"DeT{b}q{qt}") for qt in range(4)] for b in range(2)]
    hmTs = [SA(a3_ctx, "hmT0", [128, KC * PCAP], BF16),
            SA(a3_ctx, "hmT1", [128, KC * PCAP], BF16)]
    eo_tm, reotm = SA(a3_ctx, "eo_tm", [128, D], BF16)
    ffnb = [SA(a3_ctx, "ffn0", [128, D], BF16), SA(a3_ctx, "ffn1", [128, D], BF16)]
    xb4 = [SA(a3_ctx, "x4a", [128, D], BF16), SA(a3_ctx, "x4b", [128, D], BF16)]

    class FView:
        def __init__(self, t):
            self.t = t
        def __getitem__(self, key):
            assert key == slice(None)
            return self.t[:, 0:2 * D].bitcast(F32)

    def a3_deT_loads(ei):
        DeT_bf, _ = DeTs[ei % 2]
        for kcx in range(KC):
            kb.op("io", lambda e, ei=ei, kcx=kcx, DeT_bf=DeT_bf: e.dma_start(
                out=DeT_bf[:, kcx * PCAP:(kcx + 1) * PCAP],
                in_=s_disp[ei * PCAP:(ei + 1) * PCAP, kcx * 128:(kcx + 1) * 128],
                transpose=True),
                ins=[r_disp], outs=[rDeTq[ei % 2][kcx // 4]], sem="de")

    def a3_mm1(ei):
        DeT_bf, _ = DeTs[ei % 2]
        hmT, rhmT = hmTs[ei % 2]
        for ic in range(KC):
            ps, rps = pss4[ic % 4]
            for qt in range(4):
                fns = []
                for kcq in range(4):
                    kcx = qt * 4 + kcq
                    fns.append(lambda e, ic=ic, kcx=kcx, ps=ps, DeT_bf=DeT_bf: e.matmul(
                        ps[:, 0:PCAP],
                        WA[:, kcx * D + ic * 128: kcx * D + (ic + 1) * 128],
                        DeT_bf[:, kcx * PCAP:(kcx + 1) * PCAP],
                        start=(kcx == 0), stop=(kcx == KC - 1)))
                kb.emit("pe", fns, ins=[rDeTq[ei % 2][qt], rWAq[qt]], outs=[rps])
            kb.op("act", lambda e, ic=ic, ps=ps, hmT=hmT: e.activation(
                out=hmT[:, ic * PCAP:(ic + 1) * PCAP],
                in_=ps[:, 0:PCAP], func=AF.Gelu),
                ins=[rps], outs=[rhmT])

    def a3_mm2(ei):
        hmT, rhmT = hmTs[ei % 2]
        for sg in range(3):
            for qt in range(4):
                fns = []
                for kcq in range(4):
                    kcx = qt * 4 + kcq
                    for cg in range(4):
                        ps, rps = pss4[cg]
                        fns.append(lambda e, sg=sg, kcx=kcx, cg=cg, ps=ps, hmT=hmT: e.matmul(
                            ps[:, 0:512],
                            hmT[:, kcx * PCAP + sg * 128: kcx * PCAP + (sg + 1) * 128],
                            WB[:, kcx * D + cg * 512: kcx * D + (cg + 1) * 512],
                            start=(kcx == 0), stop=(kcx == KC - 1)))
                kb.emit("pe", fns, ins=[rhmT, rWBq[qt]],
                        outs=[pss4[0][1], pss4[1][1], pss4[2][1], pss4[3][1]])
            for cg in range(4):
                ps, rps = pss4[cg]
                kb.op("dve", lambda e, cg=cg, ps=ps: e.tensor_copy(
                    out=eo_tm[:, cg * 512:(cg + 1) * 512], in_=ps[:, 0:512]),
                    ins=[rps], outs=[reotm])
            kb.op("act", lambda e, ei=ei, sg=sg: e.dma_start(
                out=s_eout[ei * PCAP + sg * 128: ei * PCAP + (sg + 1) * 128, :],
                in_=eo_tm[:]),
                ins=[reotm], outs=[r_eout], sem="seo")

    for qt in range(4):
        kb.op("gp", lambda e, qt=qt: e.dma_start(
            out=WA[:, qt * 4 * D:(qt + 1) * 4 * D],
            in_=w1_d[0][:, qt * 4 * D:(qt + 1) * 4 * D]),
            outs=[rWAq[qt]], sem="wa")
        kb.op("gp", lambda e, qt=qt: e.dma_start(
            out=WB[:, qt * 4 * D:(qt + 1) * 4 * D],
            in_=w2_d[0][:, qt * 4 * D:(qt + 1) * 4 * D]),
            outs=[rWBq[qt]], sem="wb")
    a3_deT_loads(0)
    for ei in range(8):
        a3_mm1(ei)
        if ei < 7:
            a3_deT_loads(ei + 1)
            for qt in range(4):
                kb.op("gp", lambda e, ei=ei, qt=qt: e.dma_start(
                    out=WA[:, qt * 4 * D:(qt + 1) * 4 * D],
                    in_=w1_d[ei + 1][:, qt * 4 * D:(qt + 1) * 4 * D]),
                    outs=[rWAq[qt]], sem="wa")
        if ei >= 1:
            a3_mm2(ei - 1)
            for qt in range(4):
                kb.op("gp", lambda e, ei=ei, qt=qt: e.dma_start(
                    out=WB[:, qt * 4 * D:(qt + 1) * 4 * D],
                    in_=w2_d[ei][:, qt * 4 * D:(qt + 1) * 4 * D]),
                    outs=[rWBq[qt]], sem="wb")
    a3_mm2(7)

    ofb = [(FView(hmTs[0][0]), Rec("of0", seed=hmTs[0][1].toks())),
           (FView(hmTs[1][0]), Rec("of1", seed=hmTs[1][1].toks()))]
    kb.op("dve", lambda e: e.memset(ffnb[0][0][:], 0.0), outs=[ffnb[0][1]])
    kb.op("dve", lambda e: e.memset(ffnb[1][0][:], 0.0), outs=[ffnb[1][1]])
    for cc2 in range(NCH):
        r_sx1[cc2].writers = [("sx1", kb.sems["sx1"][1])]
    r_eout.writers = [("seo", kb.sems["seo"][1])]
    for c in range(NCH):
        xt, rxt = xb4[c % 2]
        of, rof = ofb[c % 2]
        ffn_bf, rffn = ffnb[c % 2]
        kb.op("io", lambda e, c=c, xt=xt: e.dma_start(out=xt[:], in_=s_x1[c]),
              ins=[r_sx1[c]], outs=[rxt], sem="x0")
        kb.op("gp", lambda e, c=c, ffn_bf=ffn_bf: e.indirect_dma_start(
            out=ffn_bf[:], out_offset=None,
            in_=s_eout, in_offset=IOA(ap=slots[:, c:c + 1], axis=0),
            bounds_check=NSLOT - 1, oob_is_err=False),
            ins=[r_eout, rslots], outs=[rffn], sem="ga")
        kb.op("dve", lambda e, c=c, xt=xt, of=of, ffn_bf=ffn_bf: e.scalar_tensor_tensor(
            out=of[:], in0=ffn_bf[:], scalar=wtok[:, c:c + 1], in1=xt[:],
            op0=ALU.mult, op1=ALU.add),
            ins=[rffn, rwtok, rxt], outs=[rof])
        kb.op("io", lambda e, c=c, of=of: e.dma_start(
            out=y_out[c * 128:(c + 1) * 128, :], in_=of[:]),
            ins=[rof], outs=[], sem=("out0" if c % 2 == 0 else "out1"))
    seeds = [t for rr in (DeTs + hmTs) for t in rr[1].toks()] + \
        [t for r in (reotm,) for t in r.toks()]
    a3_ctx.close()

    nc.sync.wait_ge(kb.sems["out0"][0], kb.sems["out0"][1])
    nc.sync.wait_ge(kb.sems["out1"][0], kb.sems["out1"][1])
    ctx.close()
    return nc


_PROGRAM_CACHE = {}


def _enable_ldw_opt():
    try:
        from concourse.compiler_utils import get_compiler_flags, set_compiler_flags
        flags = [f.replace("--enable-ldw-opt=false", "--enable-ldw-opt=true")
                 for f in get_compiler_flags()]
        set_compiler_flags(flags)
    except Exception:
        pass


def _get_program():
    if "nc" not in _PROGRAM_CACHE:
        _PROGRAM_CACHE["nc"] = build_program()
    return _PROGRAM_CACHE["nc"]


def _host_w_layout(w):
    n = w.shape[1]
    return np.ascontiguousarray(
        w.reshape(KC, 128, n).transpose(1, 0, 2).reshape(128, KC * n))


def kernel(L0_attn_x, L0_attn_Wq, L0_attn_Wv, L0_attn_Wg, L0_attn_Wo, L0_attn_beta,
           L0_attn_ln_g, L0_attn_ln_b, L0_ffn_gate_weight, L0_ffn_w1, L0_ffn_w2):
    B, Sq, Dm = L0_attn_x.shape
    T = B * Sq
    xf = np.ascontiguousarray(np.asarray(L0_attn_x, dtype=np.float32).reshape(T, Dm))

    wq = _host_w_layout(np.asarray(L0_attn_Wq, dtype=np.float32))
    wv = _host_w_layout(np.asarray(L0_attn_Wv, dtype=np.float32))
    wg = _host_w_layout(np.asarray(L0_attn_Wg, dtype=np.float32))
    wo = _host_w_layout(np.asarray(L0_attn_Wo, dtype=np.float32))
    w1 = np.stack([_host_w_layout(np.asarray(L0_ffn_w1[e], dtype=np.float32))
                   for e in range(8)])
    w2 = np.stack([_host_w_layout(np.asarray(L0_ffn_w2[e], dtype=np.float32))
                   for e in range(8)])
    gw = np.asarray(L0_ffn_gate_weight, dtype=np.float32)
    gwT = np.ascontiguousarray(
        np.ascontiguousarray(gw.T).reshape(KC, 128, 8).transpose(1, 0, 2).reshape(128, KC * 8))

    beta = np.asarray(L0_attn_beta, dtype=np.float32).reshape(1, H)
    lng = np.asarray(L0_attn_ln_g, dtype=np.float32).reshape(1, Dm)
    lnb = np.asarray(L0_attn_ln_b, dtype=np.float32).reshape(1, Dm)

    ident = np.eye(128, dtype=np.float32).astype(ml_dtypes.bfloat16)
    sidx = np.arange(128)[:, None]
    tidx = np.arange(128)[None, :]
    tri1 = np.where(sidx <= tidx, (tidx - sidx).astype(np.float32), 0.0).astype(np.float32)
    tri16 = np.tile(tri1, (1, H))
    mask16 = np.tile((sidx <= tidx).astype(np.float32), (1, H)).astype(ml_dtypes.bfloat16)
    slt = (sidx < tidx).astype(np.float32).astype(ml_dtypes.bfloat16)
    iota = np.arange(128, dtype=np.float32).reshape(128, 1)
    iota8 = np.arange(8, dtype=np.float32).reshape(1, 8)
    tplus1 = np.tile(np.arange(1, 129, dtype=np.float32), H).reshape(1, H * 128)

    nc = _get_program()
    in_maps = []
    cpb = NCORES // B
    for k in range(NCORES):
        e8 = np.zeros((8, 1), dtype=np.float32)
        m8 = np.zeros((8, 1), dtype=np.float32)
        for j in range(NCORES):
            if j // cpb == k // cpb and j < k:
                e8[j, 0] = 2048.0 * (k - 1 - j)
                m8[j, 0] = 1.0
        in_maps.append(dict(
            x_own=xf[k * TOK:(k + 1) * TOK],
            wq=wq, wv=wv, wg=wg, wo=wo, w1=w1, w2=w2, gwt=gwT,
            beta=beta, lng=lng, lnb=lnb,
            ident=ident, tri16=tri16, mask16=mask16, slt=slt, iota=iota,
            iota8=iota8, tplus1=tplus1, e8=e8, m8=m8,
        ))
    res = run_bass_kernel_spmd(nc, in_maps, core_ids=list(range(NCORES)))
    out = np.concatenate([res.results[c]["y_out"] for c in range(NCORES)], axis=0)
    return out.reshape(B, Sq, Dm)


# version 31
# speedup vs baseline: 1.1181x; 1.0066x over previous
import numpy as np
import ml_dtypes
from contextlib import ExitStack

import concourse.bass as bass
import concourse.mybir as mybir
from concourse.bass_utils import run_bass_kernel_spmd
from concourse.bass_core_rust import IndirectOffsetOnAxis

F32 = mybir.dt.float32
BF16 = mybir.dt.bfloat16
U32 = mybir.dt.uint32
AF = mybir.ActivationFunctionType
ALU = mybir.AluOpType
AX = mybir.AxisListType

NCORES = 8
TOK = 2048
D = 2048
H, DH = 16, 128
NCH = 16
KC = 16
PCAP = 384
NSLOT = 8 * PCAP
NPREV = 3
BIG = 1.0e6
EPS = 1e-5

ENG = {"pe": "tensor", "act": "scalar", "dve": "vector", "io": "sync", "gp": "gpsimd"}
DEF_SEM = {"pe": "spe", "act": "sac", "dve": "sdv", "io": None, "gp": None}


class Rec:
    __slots__ = ("writers", "readers", "name")

    def __init__(self, name, seed=()):
        self.name = name
        self.writers = list(seed)
        self.readers = []

    def toks(self):
        return self.writers + self.readers


class KB:

    def __init__(self):
        self.nc = bass.Bass("TRN2", target_bir_lowering=False, debug=False)
        self.sems = {}
        self.waited = {}
        for nm in ("spe", "sac", "sdv"):
            self.sems[nm] = [self.nc.alloc_semaphore(name=nm), 0]

    def lane(self, nm):
        if nm not in self.sems:
            self.sems[nm] = [self.nc.alloc_semaphore(name=nm), 0]
        return nm

    def _wait(self, engname, toks):
        eng = getattr(self.nc, ENG[engname])
        best = {}
        for (nm, val) in toks:
            if engname == "pe" and nm == "spe":
                continue
            if val > best.get(nm, 0):
                best[nm] = val
        for nm, val in best.items():
            key = (engname, nm)
            if self.waited.get(key, 0) >= val:
                continue
            eng.wait_ge(self.sems[nm][0], val)
            self.waited[key] = val

    def emit(self, engname, fns, ins=(), outs=(), sem=None, last_only=False):
        toks = []
        for r in ins:
            toks += r.writers
        for r in outs:
            toks += r.toks()
        self._wait(engname, toks)
        eng = getattr(self.nc, ENG[engname])
        instr = None
        for fn in fns:
            instr = fn(eng)
        if sem is None:
            sem = DEF_SEM[engname]
            assert sem is not None, f"DMA op on {engname} must specify a lane"
        else:
            self.lane(sem)
        k = 1 if sem in ("spe", "sac", "sdv", "cc") else 16
        rec = self.sems[sem]
        instr.then_inc(rec[0], k)
        rec[1] += k
        tok = (sem, rec[1])
        for r in ins:
            r.readers.append(tok)
        for r in outs:
            r.writers = [tok]
            r.readers = []
        return tok

    def op(self, engname, fn, ins=(), outs=(), sem=None):
        return self.emit(engname, [fn], ins=ins, outs=outs, sem=sem)


def build_program():
    kb = KB()
    nc = kb.nc
    ctx = ExitStack()
    en = ctx.enter_context
    seeds = []

    ap = lambda t: t.ap()
    x_own = ap(nc.dram_tensor("x_own", [TOK, D], F32, kind="ExternalInput"))
    Wq_d = ap(nc.dram_tensor("wq", [128, KC * D], F32, kind="ExternalInput"))
    Wv_d = ap(nc.dram_tensor("wv", [128, KC * D], F32, kind="ExternalInput"))
    Wg_d = ap(nc.dram_tensor("wg", [128, KC * D], F32, kind="ExternalInput"))
    Wo_d = ap(nc.dram_tensor("wo", [128, KC * D], F32, kind="ExternalInput"))
    w1_d = ap(nc.dram_tensor("w1", [8, 128, KC * D], F32, kind="ExternalInput"))
    w2_d = ap(nc.dram_tensor("w2", [8, 128, KC * D], F32, kind="ExternalInput"))
    gwT_d = ap(nc.dram_tensor("gwt", [128, KC * 8], F32, kind="ExternalInput"))
    beta_d = ap(nc.dram_tensor("beta", [1, H], F32, kind="ExternalInput"))
    lng_d = ap(nc.dram_tensor("lng", [1, D], F32, kind="ExternalInput"))
    lnb_d = ap(nc.dram_tensor("lnb", [1, D], F32, kind="ExternalInput"))
    ident_d = ap(nc.dram_tensor("ident", [128, 128], BF16, kind="ExternalInput"))
    tri16_d = ap(nc.dram_tensor("tri16", [128, H * 128], F32, kind="ExternalInput"))
    mask16_d = ap(nc.dram_tensor("mask16", [128, H * 128], BF16, kind="ExternalInput"))
    slt_d = ap(nc.dram_tensor("slt", [128, 128], BF16, kind="ExternalInput"))
    iota_d = ap(nc.dram_tensor("iota", [128, 1], F32, kind="ExternalInput"))
    iota8_d = ap(nc.dram_tensor("iota8", [1, 8], F32, kind="ExternalInput"))
    tplus1_d = ap(nc.dram_tensor("tplus1", [1, H * 128], F32, kind="ExternalInput"))
    e8_d = ap(nc.dram_tensor("e8", [8, 1], F32, kind="ExternalInput"))
    m8_d = ap(nc.dram_tensor("m8", [8, 1], F32, kind="ExternalInput"))
    y_out = ap(nc.dram_tensor("y_out", [TOK, D], F32, kind="ExternalOutput"))
    u_dram_t = nc.dram_tensor("u_dram", [1, D], F32, kind="Internal")
    u_all_t = nc.dram_tensor("u_all", [8, D], F32, kind="Internal")
    u_dram = u_dram_t.ap()
    u_all = u_all_t.ap()

    s_xlnT = ap(nc.dram_tensor("s_xlnt", [NCH, 128, D], BF16, kind="Internal"))
    s_v = ap(nc.dram_tensor("s_v", [NCH, 128, D], BF16, kind="Internal"))
    s_g = ap(nc.dram_tensor("s_g", [NCH, 128, D], BF16, kind="Internal"))
    s_x1 = ap(nc.dram_tensor("s_x1", [NCH, 128, D], BF16, kind="Internal"))
    s_disp = ap(nc.dram_tensor("s_disp", [NSLOT, D], BF16, kind="Internal"))

    s_eout = ap(nc.dram_tensor("s_eout", [NSLOT, D], BF16, kind="Internal"))

    r_sxlnT = [Rec(f"sxlnT{c}") for c in range(NCH)]
    r_sv = [Rec(f"sv{c}") for c in range(NCH)]
    r_sg = [Rec(f"sg{c}") for c in range(NCH)]
    r_sx1 = [Rec(f"sx1{c}") for c in range(NCH)]
    r_disp = Rec("s_disp")
    rsc = Rec("sc_chain")
    rgc = Rec("gc_chain")
    r_eout = Rec("s_eout")

    IOA = IndirectOffsetOnAxis

    def SA(stack, name, shape, dt):
        t = stack.enter_context(nc.sbuf_tensor(name, shape, dt))
        return t, Rec(name, seed=seeds)

    attn_ctx = ExitStack()
    WA, rWA = SA(ctx, "WA", [128, KC * D], BF16)
    WB, rWB = SA(ctx, "WB", [128, KC * D], BF16)
    rWAq = [Rec(f"WAq{i}") for i in range(4)]
    rWBq = [Rec(f"WBq{i}") for i in range(4)]
    ident, rident = SA(ctx, "identsb", [128, 128], BF16)
    slt, rslt = SA(ctx, "sltsb", [128, 128], BF16)
    gwT, rgwT = SA(ctx, "gwT", [128, KC * 8], BF16)
    iota, riota = SA(ctx, "iotasb", [128, 1], F32)
    iota8bc, riota8 = SA(ctx, "iota8bc", [128, 8], F32)
    ones8, rones8 = SA(ctx, "ones8", [128, 8], F32)
    onesr_f, ronesr = SA(ctx, "onesr_f", [1, 128], F32)
    onesc_bf, ronesc = SA(ctx, "onesc_bf", [128, 1], BF16)
    lnl_bc, rlnl = SA(ctx, "lnl_bc", [128, H], F32)
    wdec_all, rwdec = SA(ctx, "wdec_all", [128, NCH * H], BF16)
    u_row, ru_row = SA(ctx, "u_row", [1, D], BF16)
    e8sb, re8 = SA(ctx, "e8sb", [8, 1], F32)
    m8sb, rm8 = SA(ctx, "m8sb", [8, 1], F32)
    w8f, rw8f = SA(ctx, "w8f", [8, H], F32)
    w8b, rw8b = SA(ctx, "w8b", [8, H], BF16)
    wtok, rwtok = SA(ctx, "wtok", [128, NCH], F32)
    slots, rslots = SA(ctx, "slots", [128, NCH], U32)
    runcnt, rruncnt = SA(ctx, "runcnt", [1, 8], F32)
    cnt8, rcnt8 = SA(ctx, "cnt8", [1, 8], F32)
    xbuf0, rx0 = SA(ctx, "xbuf0", [128, D], BF16)
    c_s1, rs1 = SA(ctx, "c_s1", [128, 1], F32)
    c_s2, rs2 = SA(ctx, "c_s2", [128, 1], F32)
    c_mu, rmu = SA(ctx, "c_mu", [128, 1], F32)
    c_msq, rmsq = SA(ctx, "c_msq", [128, 1], F32)
    c_var, rvar = SA(ctx, "c_var", [128, 1], F32)
    c_std, rstd_ = SA(ctx, "c_std", [128, 1], F32)
    c_rstd, rrstd = SA(ctx, "c_rstd", [128, 1], F32)
    c_nmr, rnmr = SA(ctx, "c_nmr", [128, 1], F32)
    lg8a, rlg8a = SA(ctx, "lg8a", [128, 8], F32)
    lg8b, rlg8b = SA(ctx, "lg8b", [128, 8], F32)
    t8a, rt8a = SA(ctx, "t8a", [128, 8], F32)
    t8b, rt8b = SA(ctx, "t8b", [128, 8], F32)
    oh8, roh8 = SA(ctx, "oh8", [128, 8], F32)
    oh8b, roh8b = SA(ctx, "oh8b", [128, 8], BF16)
    pexp8, rpexp8 = SA(ctx, "pexp8", [128, 8], F32)
    c_lmax, rlmax = SA(ctx, "c_lmax", [128, 1], F32)
    c_nlmax, rnlmax = SA(ctx, "c_nlmax", [128, 1], F32)
    c_sexp, rsexp = SA(ctx, "c_sexp", [128, 1], F32)
    c_rsum, rrsum = SA(ctx, "c_rsum", [128, 1], F32)
    c_ptop, rptop = SA(ctx, "c_ptop", [128, 1], F32)
    c_w, rcw = SA(ctx, "c_w", [128, 1], F32)
    c_rk1, rrk1 = SA(ctx, "c_rk1", [128, 1], F32)
    c_rk2, rrk2 = SA(ctx, "c_rk2", [128, 1], F32)
    c_rank, rrank = SA(ctx, "c_rank", [128, 1], F32)
    c_keep, rkeep = SA(ctx, "c_keep", [128, 1], F32)
    c_et, rcet = SA(ctx, "c_et", [128, 1], F32)
    c_sf, rcsf = SA(ctx, "c_sf", [128, 1], F32)
    c_sk, rcsk = SA(ctx, "c_sk", [128, 1], F32)
    c_bigk, rbigk = SA(ctx, "c_bigk", [128, 1], F32)
    c_sm, rcsm = SA(ctx, "c_sm", [128, 1], F32)
    c_ecol, recol = SA(ctx, "c_ecol", [128, 1], F32)
    c_mubf, rmubf = SA(ctx, "c_mubf", [128, 1], BF16)

    At_all, rAt = SA(attn_ctx, "At_all", [128, H * 128], BF16)
    P_row, rProw = SA(attn_ctx, "P_row", [1, H * 128], BF16)
    lng_bc, rlng = SA(attn_ctx, "lng_bc", [128, D], BF16)
    lnb_bc, rlnb = SA(attn_ctx, "lnb_bc", [128, D], BF16)
    c_stage, rcst = SA(attn_ctx, "c_stage", [1, H * 128], BF16)
    rcst4 = [Rec(f"cst{q4}") for q4 in range(4)]

    ps_tr = en(nc.psum_tensor("ps_tr", [128, 512], F32))
    rtrp = Rec("ps_tr")
    ps_sm = en(nc.psum_tensor("ps_sm", [128, 512], F32))
    rsm = Rec("ps_sm")
    ps_a0 = en(nc.psum_tensor("ps_a0", [128, 512], F32))
    ra0 = Rec("ps_a0")
    ps_a1 = en(nc.psum_tensor("ps_a1", [128, 512], F32))
    ra1 = Rec("ps_a1")
    ps_a2 = en(nc.psum_tensor("ps_a2", [128, 512], F32))
    ra2 = Rec("ps_a2")
    ps_a3 = en(nc.psum_tensor("ps_a3", [128, 512], F32))
    ra3 = Rec("ps_a3")
    ps_a4 = en(nc.psum_tensor("ps_a4", [128, 512], F32))
    ra4 = Rec("ps_a4")
    ps_a5 = en(nc.psum_tensor("ps_a5", [128, 512], F32))
    ra5 = Rec("ps_a5")

    setup_ctx = ExitStack()
    tri16, rtri16 = SA(setup_ctx, "tri16sb", [128, H * 128], F32)
    mask16, rmask16 = SA(setup_ctx, "mask16sb", [128, H * 128], BF16)
    lnlbig, rlnlbig = SA(setup_ctx, "lnlbig", [128, H * 128], F32)
    beta_bc, rbeta = SA(setup_ctx, "beta_bc", [128, H], F32)
    tmp16a, rtmp16a = SA(setup_ctx, "tmp16a", [128, H], F32)
    tmp16b, rtmp16b = SA(setup_ctx, "tmp16b", [128, H], F32)
    tp1, rtp1 = SA(setup_ctx, "tp1", [1, H * 128], F32)
    lnlexp, rlnlexp = SA(setup_ctx, "lnlexp", [1, H * 128], F32)
    prow_f, rprowf = SA(setup_ctx, "prow_f", [1, H * 128], F32)

    kb.op("io", lambda e: e.dma_start(out=ident[:], in_=ident_d), outs=[rident, rsc], sem="sc")
    kb.op("io", lambda e: e.dma_start(out=tri16[:], in_=tri16_d), outs=[rtri16, rsc], sem="sc")
    kb.op("io", lambda e: e.dma_start(out=mask16[:], in_=mask16_d), outs=[rmask16, rsc], sem="sc")
    kb.op("io", lambda e: e.dma_start(out=slt[:], in_=slt_d), outs=[rslt, rsc], sem="sc")
    kb.op("io", lambda e: e.dma_start(out=iota[:], in_=iota_d), outs=[riota, rsc], sem="sc")
    kb.op("io", lambda e: e.dma_start(out=iota8bc[:], in_=iota8_d.broadcast_to([128, 8])),
          outs=[riota8, rsc], sem="sc")
    kb.op("io", lambda e: e.dma_start(out=tp1[:], in_=tplus1_d), outs=[rtp1, rsc], sem="sc")
    kb.op("io", lambda e: e.dma_start(out=beta_bc[:], in_=beta_d.broadcast_to([128, H])),
          outs=[rbeta, rsc], sem="sc")
    kb.op("gp", lambda e: e.dma_start(out=lng_bc[:], in_=lng_d.broadcast_to([128, D])),
          outs=[rlng, rgc], sem="gc")
    kb.op("gp", lambda e: e.dma_start(out=lnb_bc[:], in_=lnb_d.broadcast_to([128, D])),
          outs=[rlnb, rgc], sem="gc")
    kb.op("gp", lambda e: e.dma_start(out=gwT[:], in_=gwT_d), outs=[rgwT, rgc], sem="gc")
    for qt in range(4):
        kb.op("gp", lambda e, qt=qt: e.dma_start(
            out=WA[:, qt * 4 * D:(qt + 1) * 4 * D],
            in_=Wv_d[:, qt * 4 * D:(qt + 1) * 4 * D]),
            outs=[rWAq[qt]], sem="wa")
        kb.op("gp", lambda e, qt=qt: e.dma_start(
            out=WB[:, qt * 4 * D:(qt + 1) * 4 * D],
            in_=Wg_d[:, qt * 4 * D:(qt + 1) * 4 * D]),
            outs=[rWBq[qt]], sem="wb")

    kb.op("dve", lambda e: e.memset(ones8[:], 1.0), outs=[rones8])
    kb.op("dve", lambda e: e.memset(runcnt[:], 0.0), outs=[rruncnt])
    kb.op("dve", lambda e: e.memset(onesr_f[:], 1.0), outs=[ronesr])
    kb.op("dve", lambda e: e.memset(onesc_bf[:], 1.0), outs=[ronesc])

    kb.op("act", lambda e: e.activation(out=tmp16a[:], in_=beta_bc[:], func=AF.Abs),
          ins=[rbeta], outs=[rtmp16a])
    kb.op("dve", lambda e: e.tensor_scalar_mul(tmp16a[:], tmp16a[:], -1.0),
          ins=[rtmp16a], outs=[rtmp16a])
    kb.op("act", lambda e: e.activation(out=tmp16a[:], in_=tmp16a[:], func=AF.Exp),
          ins=[rtmp16a], outs=[rtmp16a])
    kb.op("act", lambda e: e.activation(out=tmp16a[:], in_=tmp16a[:], func=AF.Ln, bias=1.0),
          ins=[rtmp16a], outs=[rtmp16a])
    kb.op("dve", lambda e: e.tensor_scalar(out=tmp16b[:], in0=beta_bc[:], scalar1=-1.0,
                                           scalar2=0.0, op0=ALU.mult, op1=ALU.max),
          ins=[rbeta], outs=[rtmp16b])
    kb.op("dve", lambda e: e.tensor_add(out=tmp16b[:], in0=tmp16b[:], in1=tmp16a[:]),
          ins=[rtmp16a, rtmp16b], outs=[rtmp16b])
    kb.op("dve", lambda e: e.tensor_scalar_mul(lnl_bc[:], tmp16b[:], -1.0),
          ins=[rtmp16b], outs=[rlnl])

    kb.op("dve", lambda e: e.tensor_copy(
        out=lnlexp[:].rearrange("o (h t) -> o h t", h=H),
        in_=lnl_bc[0:1, :].unsqueeze(2).broadcast_to([1, H, 128])),
        ins=[rlnl], outs=[rlnlexp])
    for cg in range(4):
        pscg = [ps_a0, ps_a1][cg % 2]
        rcg = [ra0, ra1][cg % 2]
        kb.emit("pe", [lambda e, cg=cg, pscg=pscg: e.matmul(
            pscg[:, 0:512], onesr_f[0:1, :], lnlexp[0:1, cg * 512:(cg + 1) * 512],
            start=True, stop=True)], ins=[ronesr, rlnlexp], outs=[rcg])
        kb.op("dve", lambda e, cg=cg, pscg=pscg: e.tensor_copy(
            out=lnlbig[:, cg * 512:(cg + 1) * 512], in_=pscg[:, 0:512]),
            ins=[rcg], outs=[rlnlbig])
    kb.op("dve", lambda e: e.tensor_mul(out=lnlbig[:], in0=lnlbig[:], in1=tri16[:]),
          ins=[rtri16, rlnlbig], outs=[rlnlbig])
    kb.op("act", lambda e: e.activation(out=lnlbig[:], in_=lnlbig[:], func=AF.Exp),
          ins=[rlnlbig], outs=[rlnlbig])
    kb.op("dve", lambda e: e.tensor_tensor(out=At_all[:], in0=lnlbig[:], in1=mask16[:],
                                           op=ALU.mult),
          ins=[rlnlbig, rmask16], outs=[rAt])

    kb.op("dve", lambda e: e.tensor_mul(out=prow_f[:], in0=tp1[:], in1=lnlexp[:]),
          ins=[rtp1, rlnlexp], outs=[rprowf])
    kb.op("act", lambda e: e.activation(out=prow_f[:], in_=prow_f[:], func=AF.Exp),
          ins=[rprowf], outs=[rprowf])
    kb.op("act", lambda e: e.copy(out=P_row[:], in_=prow_f[:]), ins=[rprowf], outs=[rProw])

    kb.op("io", lambda e: e.dma_start(out=e8sb[:], in_=e8_d), outs=[re8, rsc], sem="sc")
    kb.op("io", lambda e: e.dma_start(out=m8sb[:], in_=m8_d), outs=[rm8, rsc], sem="sc")
    kb.op("dve", lambda e: e.memset(u_row[:], 0.0), outs=[ru_row])
    for c in range(NCH):
        kb.op("dve", lambda e, c=c: e.tensor_scalar(
            out=c_s1[:], in0=iota[:], scalar1=-1.0, scalar2=float(2047 - 128 * c),
            op0=ALU.mult, op1=ALU.add),
            ins=[riota], outs=[rs1])
        kb.op("dve", lambda e: e.tensor_scalar(
            out=tmp16a[:], in0=lnl_bc[:], scalar1=c_s1[:, 0:1], scalar2=None,
            op0=ALU.mult),
            ins=[rlnl, rs1], outs=[rtmp16a])
        kb.op("act", lambda e: e.activation(out=tmp16a[:], in_=tmp16a[:], func=AF.Exp),
              ins=[rtmp16a], outs=[rtmp16a])
        kb.op("act", lambda e, c=c: e.copy(out=wdec_all[:, c * H:(c + 1) * H],
                                           in_=tmp16a[:]),
              ins=[rtmp16a], outs=[rwdec])

    seeds = [t for r in (rtri16, rmask16, rlnlbig, rbeta, rtmp16a, rtmp16b, rtp1,
                         rlnlexp, rprowf) for t in r.toks()]
    setup_ctx.close()

    a12_ctx = ExitStack()
    xbuf1, rx1 = SA(a12_ctx, "xbuf1", [128, D], BF16)
    xln_t = SA(a12_ctx, "xln0", [128, D], BF16)
    xlns = [xln_t, xln_t]
    xlnTs = [SA(a12_ctx, "xlnT0", [128, D], BF16), SA(a12_ctx, "xlnT1", [128, D], BF16)]
    v_t = SA(a12_ctx, "v0", [128, D], BF16)
    vsb = [v_t, v_t]
    g_t = SA(a12_ctx, "g0", [128, D], BF16)
    gsb = [g_t, g_t]
    q_f32, rq = SA(a12_ctx, "q_f32", [128, D], F32)
    S_sb, rS = SA(a12_ctx, "S_sb", [128, D], BF16)
    ysb = [SA(a12_ctx, "y0", [128, D], BF16), SA(a12_ctx, "y1", [128, D], BF16)]
    z_bf, rz = SA(a12_ctx, "z_bf", [128, D], BF16)
    q_bf, rqb = None, None
    d_s1, qs1 = SA(a12_ctx, "d_s1", [128, 1], F32)
    d_s2, qs2 = SA(a12_ctx, "d_s2", [128, 1], F32)
    d_mu, qmu = SA(a12_ctx, "d_mu", [128, 1], F32)
    d_msq, qmsq = SA(a12_ctx, "d_msq", [128, 1], F32)
    d_var, qvar = SA(a12_ctx, "d_var", [128, 1], F32)
    d_std, qstd = SA(a12_ctx, "d_std", [128, 1], F32)
    d_rstd, qrstd = SA(a12_ctx, "d_rstd", [128, 1], F32)
    d_nmr, qnmr = SA(a12_ctx, "d_nmr", [128, 1], F32)

    class BCView:
        def __init__(self, t, off=0):
            self.t = t
            self.off = off
        def __getitem__(self, key):
            if key == slice(None):
                return self.t[:, self.off:self.off + D // 2].bitcast(BF16)
            _, cs = key
            return self.t[:, self.off + cs.start // 2: self.off + cs.stop // 2].bitcast(BF16)

    q_bf, rqb = BCView(q_f32, 1024), rq

    xb = [(xbuf0, rx0), (xbuf1, rx1)]
    EV = ["act", "dve", "act", "dve"]

    def pcopy(engname, dst_ap_fn, ps, rps, dst_rec):
        if engname == "act":
            kb.op("act", lambda e: e.copy(out=dst_ap_fn(), in_=ps[:, 0:512]),
                  ins=[rps], outs=[dst_rec])
        else:
            kb.op("dve", lambda e: e.tensor_copy(out=dst_ap_fn(), in_=ps[:, 0:512]),
                  ins=[rps], outs=[dst_rec])

    def ln_stats(src_rec, src_tile, sq_tile, sq_rec, cols):
        (s1_, t1), (s2_, t2), (mu_, tmu), (msq_, tmsq), (var_, tvar), (std_, tstd), \
            (rstdc, trstd), (nmrc, tnmr) = cols
        kb.op("dve", lambda e: e.reduce_sum(out=s1_[:], in_=src_tile[:], axis=AX.X),
              ins=[src_rec], outs=[t1])
        kb.op("act", lambda e: e.activation(out=sq_tile[:], in_=src_tile[:], func=AF.Square,
                                            accum_out=s2_[:]),
              ins=[src_rec], outs=[sq_rec, t2])
        kb.op("dve", lambda e: e.tensor_scalar_mul(mu_[:], s1_[:], 1.0 / D),
              ins=[t1], outs=[tmu])
        kb.op("dve", lambda e: e.tensor_scalar(out=msq_[:], in0=mu_[:],
                                               scalar1=mu_[:, 0:1], scalar2=EPS,
                                               op0=ALU.mult, op1=ALU.subtract),
              ins=[tmu], outs=[tmsq])
        kb.op("dve", lambda e: e.tensor_scalar(out=var_[:], in0=s2_[:], scalar1=1.0 / D,
                                               scalar2=msq_[:, 0:1], op0=ALU.mult,
                                               op1=ALU.subtract),
              ins=[t2, tmsq], outs=[tvar])
        kb.op("act", lambda e: e.activation(out=std_[:], in_=var_[:], func=AF.Sqrt),
              ins=[tvar], outs=[tstd])
        kb.op("dve", lambda e: e.reciprocal(out=rstdc[:], in_=std_[:]),
              ins=[tstd], outs=[trstd])
        kb.op("dve", lambda e: e.tensor_scalar(out=nmrc[:], in0=mu_[:],
                                               scalar1=rstdc[:, 0:1], scalar2=-1.0,
                                               op0=ALU.mult, op1=ALU.mult),
              ins=[tmu, trstd], outs=[tnmr])
        return (rstdc, trstd), (nmrc, tnmr)

    colsA = [(c_s1, rs1), (c_s2, rs2), (c_mu, rmu), (c_msq, rmsq), (c_var, rvar),
             (c_std, rstd_), (c_rstd, rrstd), (c_nmr, rnmr)]
    colsB = [(d_s1, qs1), (d_s2, qs2), (d_mu, qmu), (d_msq, qmsq), (d_var, qvar),
             (d_std, qstd), (d_rstd, qrstd), (d_nmr, qnmr)]

    def transpose16(src_tile, src_rec, dst_tile, dst_rec):
        for rnd in range(4):
            fns = []
            for j in range(4):
                kcx = rnd * 4 + j
                fns.append(lambda e, kcx=kcx, j=j: e.matmul(
                    ps_tr[:, j * 128:(j + 1) * 128],
                    src_tile[:, kcx * 128:(kcx + 1) * 128], ident[:],
                    start=True, stop=True))
            kb.emit("pe", fns, ins=[src_rec, rident], outs=[rtrp])
            if rnd % 2 == 0:
                kb.op("act", lambda e, rnd=rnd: e.copy(
                    out=dst_tile[:, rnd * 512:(rnd + 1) * 512], in_=ps_tr[:]),
                    ins=[rtrp], outs=[dst_rec])
            else:
                kb.op("dve", lambda e, rnd=rnd: e.tensor_copy(
                    out=dst_tile[:, rnd * 512:(rnd + 1) * 512], in_=ps_tr[:]),
                    ins=[rtrp], outs=[dst_rec])

    pss4 = [(ps_a0, ra0), (ps_a1, ra1), (ps_a0, ra0), (ps_a1, ra1)]

    def big_mm(lhsT_tile, lhsT_rec, w_tile, w_rec, evac, four_banks=True,
               banks=(0, 1)):
        if four_banks:
            for qt in range(4):
                fns = []
                for kcq in range(4):
                    kcx = qt * 4 + kcq
                    for cg in range(4):
                        ps, rps = pss4[cg]
                        fns.append(lambda e, kcx=kcx, cg=cg, ps=ps: e.matmul(
                            ps[:, 0:512],
                            lhsT_tile[:, kcx * 128:(kcx + 1) * 128],
                            w_tile[:, kcx * D + cg * 512: kcx * D + (cg + 1) * 512],
                            start=(kcx == 0), stop=(kcx == KC - 1)))
                kb.emit("pe", fns, ins=[lhsT_rec, w_rec[qt]],
                        outs=[pss4[0][1], pss4[1][1], pss4[2][1], pss4[3][1]])
            for cg in range(4):
                ps, rps = pss4[cg]
                evac(cg, ps, rps)
        else:
            for pair in range(2):
                cga, cgb = 2 * pair, 2 * pair + 1
                psa, rpsa = pss4[banks[0]]
                psb_, rpsb = pss4[banks[1]]
                for qt in range(4):
                    fns = []
                    for kcq in range(4):
                        kcx = qt * 4 + kcq
                        for cg, ps in ((cga, psa), (cgb, psb_)):
                            fns.append(lambda e, kcx=kcx, cg=cg, ps=ps: e.matmul(
                                ps[:, 0:512],
                                lhsT_tile[:, kcx * 128:(kcx + 1) * 128],
                                w_tile[:, kcx * D + cg * 512: kcx * D + (cg + 1) * 512],
                                start=(kcx == 0), stop=(kcx == KC - 1)))
                    kb.emit("pe", fns, ins=[lhsT_rec, w_rec[qt]], outs=[rpsa, rpsb])
                evac(cga, psa, rpsa)
                evac(cgb, psb_, rpsb)

    def a1_step(c):
        xt, rxt = xb[c % 2]
        xlnp, rxlnp = xlns[c % 2]
        xlnTp, rxlnTp = xlnTs[c % 2]
        vp, rvp = vsb[c % 2]
        gp_, rgp = gsb[c % 2]
        kb.op("gp", lambda e, c=c, xt=xt: e.dma_start(
            out=xt[:], in_=x_own[c * 128:(c + 1) * 128, :]), outs=[rxt],
            sem="x0")
        (rst, trst), (nmr_, tnmr) = ln_stats(rxt, xt, z_bf, rz, colsA)
        kb.op("dve", lambda e, xt=xt, xlnp=xlnp: e.tensor_scalar(
            out=xlnp[:], in0=xt[:], scalar1=rst[:, 0:1], scalar2=nmr_[:, 0:1],
            op0=ALU.mult, op1=ALU.add),
            ins=[rxt, trst, tnmr], outs=[rxlnp])
        transpose16(xlnp, rxlnp, xlnTp, rxlnTp)
        kb.op("io", lambda e, c=c, xlnTp=xlnTp: e.dma_start(out=s_xlnT[c], in_=xlnTp[:]),
              ins=[rxlnTp], outs=[r_sxlnT[c]], sem="sxt")

        def evac_v(cg, ps, rps):
            pcopy(EV[cg], lambda cg=cg, vp=vp: vp[:, cg * 512:(cg + 1) * 512], ps, rps, rvp)
        big_mm(xlnTp, rxlnTp, WA, rWAq, evac_v, four_banks=False)
        kb.op("io", lambda e, c=c, vp=vp: e.dma_start(out=s_v[c], in_=vp[:]),
              ins=[rvp], outs=[r_sv[c]], sem="sv")

        for q4 in range(4):
            bank, rbank = psU[q4]
            fns = [lambda e, h=h, vp=vp, bank=bank: e.matmul(
                bank[0:1, (h % 4) * 128:(h % 4) * 128 + 128],
                wdec_all[:, c * H + h: c * H + h + 1],
                vp[:, h * 128:(h + 1) * 128],
                start=True, stop=True) for h in range(4 * q4, 4 * q4 + 4)]
            kb.emit("pe", fns, ins=[rwdec, rvp], outs=[rbank])
            kb.op("dve", lambda e, q4=q4, bank=bank: e.tensor_tensor(
                out=u_row[0:1, q4 * 512:(q4 + 1) * 512], in0=bank[0:1, 0:512],
                in1=u_row[0:1, q4 * 512:(q4 + 1) * 512], op=ALU.add),
                ins=[rbank], outs=[ru_row])

        def evac_g(cg, ps, rps):
            kb.op("act", lambda e, cg=cg, ps=ps, gp_=gp_: e.activation(
                out=gp_[:, cg * 512:(cg + 1) * 512], in_=ps[:, 0:512], func=AF.Silu),
                ins=[rps], outs=[rgp])
        big_mm(xlnTp, rxlnTp, WB, rWBq, evac_g, four_banks=False)
        kb.op("io", lambda e, c=c, gp_=gp_: e.dma_start(out=s_g[c], in_=gp_[:]),
              ins=[rgp], outs=[r_sg[c]], sem="sg")

    pss4[:] = [(ps_a0, ra0), (ps_a1, ra1), (ps_a2, ra2), (ps_a3, ra3)]
    pssS = [(ps_a4, ra4), (ps_a5, ra5), (ps_a4, ra4), (ps_a5, ra5)]
    psU = [(ps_a2, ra2), (ps_a3, ra3), (ps_a4, ra4), (ps_a5, ra5)]
    zT = BCView(q_f32, 0)
    rzT = Rec("zT", seed=rq.toks())

    for c in range(NCH):
        a1_step(c)

    for qt in range(4):
        kb.op("gp", lambda e, qt=qt: e.dma_start(
            out=WA[:, qt * 4 * D:(qt + 1) * 4 * D],
            in_=Wq_d[:, qt * 4 * D:(qt + 1) * 4 * D]),
            outs=[rWAq[qt]], sem="wa")
        kb.op("gp", lambda e, qt=qt: e.dma_start(
            out=WB[:, qt * 4 * D:(qt + 1) * 4 * D],
            in_=Wo_d[:, qt * 4 * D:(qt + 1) * 4 * D]),
            outs=[rWBq[qt]], sem="wb")

    r_ud = Rec("u_dram")
    r_cc = Rec("u_all")
    kb.op("gp", lambda e: e.dma_start(out=u_dram, in_=u_row[0:1, :]),
          ins=[ru_row], outs=[r_ud], sem="ud")
    kb.op("gp", lambda e: e.collective_compute(
        "AllGather", ALU.bypass, replica_groups=[list(range(NCORES))],
        ins=[u_dram_t.ap().opt()], outs=[u_all_t.ap().opt()]),
        ins=[r_ud], outs=[r_cc], sem="cc")
    u8_t, r_u8 = xlnTs[0]
    kb.op("gp", lambda e: e.dma_start(out=u8_t[0:8, :], in_=u_all),
          ins=[r_cc], outs=[r_u8], sem="u8")
    kb.op("dve", lambda e: e.tensor_scalar(
        out=w8f[:], in0=lnl_bc[0:8, :], scalar1=e8sb[0:8, 0:1], scalar2=None,
        op0=ALU.mult),
        ins=[rlnl, re8], outs=[rw8f])
    kb.op("act", lambda e: e.activation(out=w8f[:], in_=w8f[:], func=AF.Exp),
          ins=[rw8f], outs=[rw8f])
    kb.op("dve", lambda e: e.tensor_scalar(
        out=w8f[:], in0=w8f[:], scalar1=m8sb[0:8, 0:1], scalar2=None, op0=ALU.mult),
        ins=[rw8f, rm8], outs=[rw8f])
    kb.op("act", lambda e: e.copy(out=w8b[:], in_=w8f[:]), ins=[rw8f], outs=[rw8b])
    for q4 in range(4):
        psb, rpbh = pssS[q4]
        fns = [lambda e, h=h, psb=psb: e.matmul(
            psb[0:1, (h % 4) * 128:(h % 4) * 128 + 128],
            w8b[0:8, h:h + 1],
            u8_t[0:8, h * 128:(h + 1) * 128],
            start=True, stop=True) for h in range(4 * q4, 4 * q4 + 4)]
        kb.emit("pe", fns, ins=[rw8b, r_u8], outs=[rpbh])
        kb.op("act", lambda e, q4=q4, psb=psb: e.copy(
            out=c_stage[0:1, q4 * 512:(q4 + 1) * 512], in_=psb[0:1, :]),
            ins=[rpbh], outs=[rcst4[q4]])

    for cc2 in range(NCH):
        r_sxlnT[cc2].writers = [("sxt", kb.sems["sxt"][1])]
        r_sv[cc2].writers = [("sv", kb.sems["sv"][1])]
        r_sg[cc2].writers = [("sg", kb.sems["sg"][1])]

    def a2_s1a(c):
        xlnTp, rxlnTp = xlnTs[c % 2]
        vp, rvp = vsb[c % 2]
        yp, ryp = ysb[c % 2]
        kb.op("io", lambda e, c=c, xlnTp=xlnTp: e.dma_start(out=xlnTp[:], in_=s_xlnT[c]),
              ins=[r_sxlnT[c]], outs=[rxlnTp], sem="lxt")
        kb.op("io", lambda e, c=c, vp=vp: e.dma_start(out=vp[:], in_=s_v[c]),
              ins=[r_sv[c]], outs=[rvp], sem="lv")

        def evac_q(cg, ps, rps):
            pcopy(EV[cg], lambda cg=cg: q_bf[:, cg * 512:(cg + 1) * 512], ps, rps, rqb)
        big_mm(xlnTp, rxlnTp, WA, rWAq, evac_q, four_banks=False, banks=(0, 1))

        for q4 in range(4):
            psb, rpbh = pssS[q4]
            fns = []
            for h in range(4 * q4, 4 * q4 + 4):
                fns.append(lambda e, h=h, vp=vp, psb=psb: e.matmul(
                    psb[:, (h % 4) * 128:(h % 4) * 128 + 128],
                    At_all[:, h * 128:(h + 1) * 128],
                    vp[:, h * 128:(h + 1) * 128],
                    start=True, stop=False))
                fns.append(lambda e, h=h, psb=psb: e.matmul(
                    psb[:, (h % 4) * 128:(h % 4) * 128 + 128],
                    P_row[0:1, h * 128:(h + 1) * 128],
                    c_stage[0:1, h * 128:(h + 1) * 128],
                    start=False, stop=True))
            kb.emit("pe", fns, ins=[rAt, rvp, rProw, rcst4[q4]], outs=[rpbh])
            if q4 % 2 == 0:
                kb.op("act", lambda e, q4=q4, psb=psb: e.copy(
                    out=S_sb[:, q4 * 512:(q4 + 1) * 512], in_=psb[:, 0:512]),
                    ins=[rpbh], outs=[rS])
            else:
                kb.op("dve", lambda e, q4=q4, psb=psb: e.tensor_copy(
                    out=S_sb[:, q4 * 512:(q4 + 1) * 512], in_=psb[:, 0:512]),
                    ins=[rpbh], outs=[rS])
            kb.op("io", lambda e, q4=q4: e.dma_start(
                out=c_stage[0:1, q4 * 512:(q4 + 1) * 512],
                in_=S_sb[127:128, q4 * 512:(q4 + 1) * 512]),
                ins=[rS], outs=[rcst4[q4]], sem="row")
        kb.op("dve", lambda e, yp=yp: e.tensor_mul(out=yp[:], in0=q_bf[:], in1=S_sb[:]),
              ins=[rqb, rS], outs=[ryp])

    def a2_s1b(c):
        xt, rxt = xb[c % 2]
        yp, ryp = ysb[c % 2]
        gp_, rgp = gsb[c % 2]
        kb.op("io", lambda e, c=c, gp_=gp_: e.dma_start(out=gp_[:], in_=s_g[c]),
              ins=[r_sg[c]], outs=[rgp], sem="lg")
        kb.op("gp", lambda e, c=c, xt=xt: e.dma_start(
            out=xt[:], in_=x_own[c * 128:(c + 1) * 128, :]), outs=[rxt], sem="x0")
        (rst, trst), (nmr_, tnmr) = ln_stats(ryp, yp, z_bf, rz, colsA)
        kb.op("act", lambda e, rst=rst, nmr_=nmr_, yp=yp: e.activation(
            out=z_bf[:], in_=yp[:], func=AF.Identity,
            scale=rst[:, 0:1], bias=nmr_[:, 0:1]),
            ins=[ryp, trst, tnmr], outs=[rz])
        kb.op("dve", lambda e, yp=yp: e.tensor_mul(out=yp[:], in0=z_bf[:], in1=lng_bc[:]),
              ins=[rz, rlng], outs=[ryp])
        kb.op("dve", lambda e, yp=yp: e.tensor_add(out=z_bf[:], in0=yp[:], in1=lnb_bc[:]),
              ins=[ryp, rlnb], outs=[rz])
        kb.op("dve", lambda e, gp_=gp_, yp=yp: e.tensor_mul(out=yp[:], in0=z_bf[:], in1=gp_[:]),
              ins=[rz, rgp], outs=[ryp])
        transpose16(yp, ryp, zT, rzT)

        def evac_o(cg, ps, rps):
            kb.op("dve", lambda e, cg=cg, ps=ps, xt=xt: e.tensor_tensor(
                out=xt[:, cg * 512:(cg + 1) * 512], in0=ps[:, 0:512],
                in1=xt[:, cg * 512:(cg + 1) * 512], op=ALU.add),
                ins=[rps, rxt], outs=[rxt])
        big_mm(zT, rzT, WB, rWBq, evac_o, four_banks=False, banks=(2, 3))
        kb.op("io", lambda e, c=c, xt=xt: e.dma_start(out=s_x1[c], in_=xt[:]),
              ins=[rxt], outs=[r_sx1[c]], sem="sx1")

    def a2_s2a(c):
        xt, rxt = xb[c % 2]
        xlnp, rxlnp = xlns[c % 2]
        xlnTp, rxlnTp = xlnTs[c % 2]
        (rst, trst), (nmr_, tnmr) = ln_stats(rxt, xt, z_bf, rz, colsB)
        kb.op("dve", lambda e, rst=rst, nmr_=nmr_, xlnp=xlnp, xt=xt: e.tensor_scalar(
            out=xlnp[:], in0=xt[:], scalar1=rst[:, 0:1], scalar2=nmr_[:, 0:1],
            op0=ALU.mult, op1=ALU.add),
            ins=[rxt, trst, tnmr], outs=[rxlnp])
        transpose16(xlnp, rxlnp, xlnTp, rxlnTp)
        fns = []
        for kcx in range(KC):
            fns.append(lambda e, kcx=kcx, xlnTp=xlnTp: e.matmul(
                ps_sm[:, 0:8], xlnTp[:, kcx * 128:(kcx + 1) * 128],
                gwT[:, kcx * 8:(kcx + 1) * 8], start=(kcx == 0), stop=(kcx == KC - 1)))
        kb.emit("pe", fns, ins=[rxlnTp, rgwT], outs=[rsm])
        lgp, rlgp = (lg8a, rlg8a) if c % 2 == 0 else (lg8b, rlg8b)
        kb.op("dve", lambda e, lgp=lgp: e.tensor_copy(out=lgp[:], in_=ps_sm[:, 0:8]),
              ins=[rsm], outs=[rlgp])
    def a2_s2b(c):
        xlnp, rxlnp = xlns[c % 2]
        lgp, rlgp = (lg8a, rlg8a) if c % 2 == 0 else (lg8b, rlg8b)
        kb.op("dve", lambda e: e.reduce_max(out=c_lmax[:], in_=lgp[:], axis=AX.X),
              ins=[rlgp], outs=[rlmax])
        kb.op("dve", lambda e: e.tensor_scalar_mul(c_nlmax[:], c_lmax[:], -1.0),
              ins=[rlmax], outs=[rnlmax])
        kb.op("act", lambda e: e.activation(out=pexp8[:], in_=lgp[:], func=AF.Exp,
                                            bias=c_nlmax[:, 0:1], scale=1.0,
                                            accum_out=c_sexp[:]),
              ins=[rlgp, rnlmax], outs=[rpexp8, rsexp])
        kb.op("dve", lambda e: e.reciprocal(out=c_rsum[:], in_=c_sexp[:]),
              ins=[rsexp], outs=[rrsum])
        kb.op("dve", lambda e: e.tensor_scalar(out=oh8[:], in0=lgp[:],
                                               scalar1=c_lmax[:, 0:1], scalar2=None,
                                               op0=ALU.is_ge),
              ins=[rlgp, rlmax], outs=[roh8])
        kb.op("dve", lambda e: e.tensor_tensor_scan(out=t8a[:], data0=ones8[:], data1=oh8[:],
                                                    initial=0.0, op0=ALU.mult, op1=ALU.add),
              ins=[rones8, roh8], outs=[rt8a])
        kb.op("dve", lambda e: e.tensor_sub(out=t8a[:], in0=t8a[:], in1=oh8[:]),
              ins=[rt8a, roh8], outs=[rt8a])
        kb.op("dve", lambda e: e.tensor_single_scalar(out=t8a[:], in_=t8a[:], scalar=0.0,
                                                      op=ALU.is_equal),
              ins=[rt8a], outs=[rt8a])
        kb.op("dve", lambda e: e.tensor_mul(out=oh8[:], in0=oh8[:], in1=t8a[:]),
              ins=[roh8, rt8a], outs=[roh8])
        kb.op("act", lambda e: e.copy(out=oh8b[:], in_=oh8[:]), ins=[roh8], outs=[roh8b])
        kb.op("dve", lambda e: e.tensor_mul(out=t8b[:], in0=oh8[:], in1=pexp8[:]),
              ins=[roh8, rpexp8], outs=[rt8b])
        kb.op("dve", lambda e: e.reduce_sum(out=c_ptop[:], in_=t8b[:], axis=AX.X),
              ins=[rt8b], outs=[rptop])
        kb.op("dve", lambda e: e.tensor_mul(out=c_w[:], in0=c_ptop[:], in1=c_rsum[:]),
              ins=[rptop, rrsum], outs=[rcw])
        kb.emit("pe", [lambda e: e.matmul(ps_sm[:, 8:16], slt[:], oh8b[:],
                                          start=True, stop=True)],
                ins=[rslt, roh8b], outs=[rsm])
        kb.emit("pe", [lambda e: e.matmul(ps_sm[:, 16:24], onesr_f[0:1, :], runcnt[0:1, :],
                                          start=True, stop=True)],
                ins=[ronesr, rruncnt], outs=[rsm])
        kb.op("dve", lambda e: e.tensor_mul(out=t8a[:], in0=oh8[:], in1=ps_sm[:, 8:16]),
              ins=[roh8, rsm], outs=[rt8a])
        kb.op("dve", lambda e: e.reduce_sum(out=c_rk1[:], in_=t8a[:], axis=AX.X),
              ins=[rt8a], outs=[rrk1])
        kb.op("dve", lambda e: e.tensor_mul(out=t8b[:], in0=oh8[:], in1=ps_sm[:, 16:24]),
              ins=[roh8, rsm], outs=[rt8b])
        kb.op("dve", lambda e: e.reduce_sum(out=c_rk2[:], in_=t8b[:], axis=AX.X),
              ins=[rt8b], outs=[rrk2])
        kb.op("dve", lambda e: e.tensor_add(out=c_rank[:], in0=c_rk1[:], in1=c_rk2[:]),
              ins=[rrk1, rrk2], outs=[rrank])
        kb.op("dve", lambda e: e.tensor_mul(out=t8a[:], in0=oh8[:], in1=iota8bc[:]),
              ins=[roh8, riota8], outs=[rt8a])
        kb.op("dve", lambda e: e.reduce_sum(out=c_et[:], in_=t8a[:], axis=AX.X),
              ins=[rt8a], outs=[rcet])
        kb.op("dve", lambda e: e.tensor_single_scalar(out=c_keep[:], in_=c_rank[:],
                                                      scalar=float(PCAP), op=ALU.is_lt),
              ins=[rrank], outs=[rkeep])
        kb.op("dve", lambda e, c=c: e.tensor_scalar(out=wtok[:, c:c + 1], in0=c_w[:],
                                                    scalar1=c_keep[:, 0:1], scalar2=None,
                                                    op0=ALU.mult),
              ins=[rcw, rkeep], outs=[rwtok])
        kb.op("dve", lambda e: e.tensor_scalar(out=c_sf[:], in0=c_et[:], scalar1=float(PCAP),
                                               scalar2=c_rank[:, 0:1], op0=ALU.mult,
                                               op1=ALU.add),
              ins=[rcet, rrank], outs=[rcsf])
        kb.op("dve", lambda e: e.tensor_scalar(out=c_sk[:], in0=c_sf[:],
                                               scalar1=c_keep[:, 0:1], scalar2=None,
                                               op0=ALU.mult),
              ins=[rcsf, rkeep], outs=[rcsk])
        kb.op("dve", lambda e: e.tensor_scalar(out=c_bigk[:], in0=c_keep[:], scalar1=-BIG,
                                               scalar2=BIG, op0=ALU.mult, op1=ALU.add),
              ins=[rkeep], outs=[rbigk])
        kb.op("dve", lambda e: e.tensor_add(out=c_sm[:], in0=c_sk[:], in1=c_bigk[:]),
              ins=[rcsk, rbigk], outs=[rcsm])
        kb.op("dve", lambda e, c=c: e.tensor_copy(out=slots[:, c:c + 1], in_=c_sm[:]),
              ins=[rcsm], outs=[rslots])
        kb.emit("pe", [lambda e: e.matmul(ps_sm[0:1, 24:32], onesc_bf[:], oh8b[:],
                                          start=True, stop=True)],
                ins=[ronesc, roh8b], outs=[rsm])
        kb.op("dve", lambda e: e.tensor_copy(out=cnt8[:], in_=ps_sm[0:1, 24:32]),
              ins=[rsm], outs=[rcnt8])
        kb.op("dve", lambda e: e.tensor_add(out=runcnt[:], in0=runcnt[:], in1=cnt8[:]),
              ins=[rcnt8, rruncnt], outs=[rruncnt])
        tok = kb.op("gp", lambda e, c=c, xlnp=xlnp: e.indirect_dma_start(
            out=s_disp, out_offset=IOA(ap=slots[:, c:c + 1], axis=0),
            in_=xlnp[:], in_offset=None,
            bounds_check=NSLOT - 1, oob_is_err=False),
            ins=[rxlnp, rslots], sem="ind")
        r_disp.writers.append(tok)

    a2_s1a(0)
    a2_s1a(1)
    a2_s1b(0)
    for c in range(2, NCH):
        a2_s1a(c)
        a2_s2a(c - 2)
        a2_s1b(c - 1)
        a2_s2b(c - 2)
    a2_s2a(NCH - 2)
    a2_s1b(NCH - 1)
    a2_s2b(NCH - 2)
    a2_s2a(NCH - 1)
    a2_s2b(NCH - 1)

    seeds = [t for rr in (xlns + xlnTs + vsb + gsb + ysb) for t in rr[1].toks()] + \
        [t for r in (rq, rx1, rS, rz, rzT, rAt, rProw, rlng, rlnb, rcst, *rcst4)
         for t in r.toks()]
    a12_ctx.close()
    attn_ctx.close()

    a3_ctx = ExitStack()
    DeTs = [SA(a3_ctx, "DeT0", [128, KC * PCAP], BF16),
            SA(a3_ctx, "DeT1", [128, KC * PCAP], BF16)]
    rDeTq = [[Rec(f"DeT{b}q{qt}") for qt in range(4)] for b in range(2)]
    hmTs = [SA(a3_ctx, "hmT0", [128, KC * PCAP], BF16),
            SA(a3_ctx, "hmT1", [128, KC * PCAP], BF16)]
    eo_tm, reotm = SA(a3_ctx, "eo_tm", [128, D], BF16)
    ffnb = [SA(a3_ctx, "ffn0", [128, D], BF16), SA(a3_ctx, "ffn1", [128, D], BF16)]
    xb4 = [SA(a3_ctx, "x4a", [128, D], BF16), SA(a3_ctx, "x4b", [128, D], BF16)]

    class FView:
        def __init__(self, t):
            self.t = t
        def __getitem__(self, key):
            assert key == slice(None)
            return self.t[:, 0:2 * D].bitcast(F32)

    def a3_deT_loads(ei):
        DeT_bf, _ = DeTs[ei % 2]
        for kcx in range(KC):
            kb.op("io", lambda e, ei=ei, kcx=kcx, DeT_bf=DeT_bf: e.dma_start(
                out=DeT_bf[:, kcx * PCAP:(kcx + 1) * PCAP],
                in_=s_disp[ei * PCAP:(ei + 1) * PCAP, kcx * 128:(kcx + 1) * 128],
                transpose=True),
                ins=[r_disp], outs=[rDeTq[ei % 2][kcx // 4]], sem="de")

    def a3_mm1(ei):
        DeT_bf, _ = DeTs[ei % 2]
        hmT, rhmT = hmTs[ei % 2]
        for ic in range(KC):
            ps, rps = pss4[ic % 4]
            for qt in range(4):
                fns = []
                for kcq in range(4):
                    kcx = qt * 4 + kcq
                    fns.append(lambda e, ic=ic, kcx=kcx, ps=ps, DeT_bf=DeT_bf: e.matmul(
                        ps[:, 0:PCAP],
                        WA[:, kcx * D + ic * 128: kcx * D + (ic + 1) * 128],
                        DeT_bf[:, kcx * PCAP:(kcx + 1) * PCAP],
                        start=(kcx == 0), stop=(kcx == KC - 1)))
                kb.emit("pe", fns, ins=[rDeTq[ei % 2][qt], rWAq[qt]], outs=[rps])
            kb.op("act", lambda e, ic=ic, ps=ps, hmT=hmT: e.activation(
                out=hmT[:, ic * PCAP:(ic + 1) * PCAP],
                in_=ps[:, 0:PCAP], func=AF.Gelu),
                ins=[rps], outs=[rhmT])

    def a3_mm2(ei):
        hmT, rhmT = hmTs[ei % 2]
        for sg in range(3):
            for qt in range(4):
                fns = []
                for kcq in range(4):
                    kcx = qt * 4 + kcq
                    for cg in range(4):
                        ps, rps = pss4[cg]
                        fns.append(lambda e, sg=sg, kcx=kcx, cg=cg, ps=ps, hmT=hmT: e.matmul(
                            ps[:, 0:512],
                            hmT[:, kcx * PCAP + sg * 128: kcx * PCAP + (sg + 1) * 128],
                            WB[:, kcx * D + cg * 512: kcx * D + (cg + 1) * 512],
                            start=(kcx == 0), stop=(kcx == KC - 1)))
                kb.emit("pe", fns, ins=[rhmT, rWBq[qt]],
                        outs=[pss4[0][1], pss4[1][1], pss4[2][1], pss4[3][1]])
            for cg in range(4):
                ps, rps = pss4[cg]
                kb.op("dve", lambda e, cg=cg, ps=ps: e.tensor_copy(
                    out=eo_tm[:, cg * 512:(cg + 1) * 512], in_=ps[:, 0:512]),
                    ins=[rps], outs=[reotm])
            kb.op("act", lambda e, ei=ei, sg=sg: e.dma_start(
                out=s_eout[ei * PCAP + sg * 128: ei * PCAP + (sg + 1) * 128, :],
                in_=eo_tm[:]),
                ins=[reotm], outs=[r_eout], sem="seo")

    for qt in range(4):
        kb.op("gp", lambda e, qt=qt: e.dma_start(
            out=WA[:, qt * 4 * D:(qt + 1) * 4 * D],
            in_=w1_d[0][:, qt * 4 * D:(qt + 1) * 4 * D]),
            outs=[rWAq[qt]], sem="wa")
        kb.op("gp", lambda e, qt=qt: e.dma_start(
            out=WB[:, qt * 4 * D:(qt + 1) * 4 * D],
            in_=w2_d[0][:, qt * 4 * D:(qt + 1) * 4 * D]),
            outs=[rWBq[qt]], sem="wb")
    a3_deT_loads(0)
    for ei in range(8):
        a3_mm1(ei)
        if ei < 7:
            a3_deT_loads(ei + 1)
            for qt in range(4):
                kb.op("gp", lambda e, ei=ei, qt=qt: e.dma_start(
                    out=WA[:, qt * 4 * D:(qt + 1) * 4 * D],
                    in_=w1_d[ei + 1][:, qt * 4 * D:(qt + 1) * 4 * D]),
                    outs=[rWAq[qt]], sem="wa")
        if ei >= 1:
            a3_mm2(ei - 1)
            for qt in range(4):
                kb.op("gp", lambda e, ei=ei, qt=qt: e.dma_start(
                    out=WB[:, qt * 4 * D:(qt + 1) * 4 * D],
                    in_=w2_d[ei][:, qt * 4 * D:(qt + 1) * 4 * D]),
                    outs=[rWBq[qt]], sem="wb")
    a3_mm2(7)

    ofb = [(FView(hmTs[0][0]), Rec("of0", seed=hmTs[0][1].toks())),
           (FView(hmTs[1][0]), Rec("of1", seed=hmTs[1][1].toks()))]
    kb.op("dve", lambda e: e.memset(ffnb[0][0][:], 0.0), outs=[ffnb[0][1]])
    kb.op("dve", lambda e: e.memset(ffnb[1][0][:], 0.0), outs=[ffnb[1][1]])
    for cc2 in range(NCH):
        r_sx1[cc2].writers = [("sx1", kb.sems["sx1"][1])]
    r_eout.writers = [("seo", kb.sems["seo"][1])]
    for c in range(NCH):
        xt, rxt = xb4[c % 2]
        of, rof = ofb[c % 2]
        ffn_bf, rffn = ffnb[c % 2]
        kb.op("io", lambda e, c=c, xt=xt: e.dma_start(out=xt[:], in_=s_x1[c]),
              ins=[r_sx1[c]], outs=[rxt], sem="x0")
        kb.op("gp", lambda e, c=c, ffn_bf=ffn_bf: e.indirect_dma_start(
            out=ffn_bf[:], out_offset=None,
            in_=s_eout, in_offset=IOA(ap=slots[:, c:c + 1], axis=0),
            bounds_check=NSLOT - 1, oob_is_err=False),
            ins=[r_eout, rslots], outs=[rffn], sem="ga")
        kb.op("dve", lambda e, c=c, xt=xt, of=of, ffn_bf=ffn_bf: e.scalar_tensor_tensor(
            out=of[:], in0=ffn_bf[:], scalar=wtok[:, c:c + 1], in1=xt[:],
            op0=ALU.mult, op1=ALU.add),
            ins=[rffn, rwtok, rxt], outs=[rof])
        kb.op("io", lambda e, c=c, of=of: e.dma_start(
            out=y_out[c * 128:(c + 1) * 128, :], in_=of[:]),
            ins=[rof], outs=[], sem=("out0" if c % 2 == 0 else "out1"))
    seeds = [t for rr in (DeTs + hmTs) for t in rr[1].toks()] + \
        [t for r in (reotm,) for t in r.toks()]
    a3_ctx.close()

    nc.sync.wait_ge(kb.sems["out0"][0], kb.sems["out0"][1])
    nc.sync.wait_ge(kb.sems["out1"][0], kb.sems["out1"][1])
    ctx.close()
    return nc


_PROGRAM_CACHE = {}


def _enable_ldw_opt():
    try:
        from concourse.compiler_utils import get_compiler_flags, set_compiler_flags
        flags = [f.replace("--enable-ldw-opt=false", "--enable-ldw-opt=true")
                 for f in get_compiler_flags()]
        set_compiler_flags(flags)
    except Exception:
        pass


def _get_program():
    if "nc" not in _PROGRAM_CACHE:
        _PROGRAM_CACHE["nc"] = build_program()
    return _PROGRAM_CACHE["nc"]


def _host_w_layout(w):
    n = w.shape[1]
    return np.ascontiguousarray(
        w.reshape(KC, 128, n).transpose(1, 0, 2).reshape(128, KC * n))


def kernel(L0_attn_x, L0_attn_Wq, L0_attn_Wv, L0_attn_Wg, L0_attn_Wo, L0_attn_beta,
           L0_attn_ln_g, L0_attn_ln_b, L0_ffn_gate_weight, L0_ffn_w1, L0_ffn_w2):
    B, Sq, Dm = L0_attn_x.shape
    T = B * Sq
    xf = np.ascontiguousarray(np.asarray(L0_attn_x, dtype=np.float32).reshape(T, Dm))

    wq = _host_w_layout(np.asarray(L0_attn_Wq, dtype=np.float32))
    wv = _host_w_layout(np.asarray(L0_attn_Wv, dtype=np.float32))
    wg = _host_w_layout(np.asarray(L0_attn_Wg, dtype=np.float32))
    wo = _host_w_layout(np.asarray(L0_attn_Wo, dtype=np.float32))
    w1 = np.stack([_host_w_layout(np.asarray(L0_ffn_w1[e], dtype=np.float32))
                   for e in range(8)])
    w2 = np.stack([_host_w_layout(np.asarray(L0_ffn_w2[e], dtype=np.float32))
                   for e in range(8)])
    gw = np.asarray(L0_ffn_gate_weight, dtype=np.float32)
    gwT = np.ascontiguousarray(
        np.ascontiguousarray(gw.T).reshape(KC, 128, 8).transpose(1, 0, 2).reshape(128, KC * 8))

    beta = np.asarray(L0_attn_beta, dtype=np.float32).reshape(1, H)
    lng = np.asarray(L0_attn_ln_g, dtype=np.float32).reshape(1, Dm)
    lnb = np.asarray(L0_attn_ln_b, dtype=np.float32).reshape(1, Dm)

    ident = np.eye(128, dtype=np.float32).astype(ml_dtypes.bfloat16)
    sidx = np.arange(128)[:, None]
    tidx = np.arange(128)[None, :]
    tri1 = np.where(sidx <= tidx, (tidx - sidx).astype(np.float32), 0.0).astype(np.float32)
    tri16 = np.tile(tri1, (1, H))
    mask16 = np.tile((sidx <= tidx).astype(np.float32), (1, H)).astype(ml_dtypes.bfloat16)
    slt = (sidx < tidx).astype(np.float32).astype(ml_dtypes.bfloat16)
    iota = np.arange(128, dtype=np.float32).reshape(128, 1)
    iota8 = np.arange(8, dtype=np.float32).reshape(1, 8)
    tplus1 = np.tile(np.arange(1, 129, dtype=np.float32), H).reshape(1, H * 128)

    nc = _get_program()
    in_maps = []
    cpb = NCORES // B
    for k in range(NCORES):
        e8 = np.zeros((8, 1), dtype=np.float32)
        m8 = np.zeros((8, 1), dtype=np.float32)
        for j in range(NCORES):
            if j // cpb == k // cpb and j < k:
                e8[j, 0] = 2048.0 * (k - 1 - j)
                m8[j, 0] = 1.0
        in_maps.append(dict(
            x_own=xf[k * TOK:(k + 1) * TOK],
            wq=wq, wv=wv, wg=wg, wo=wo, w1=w1, w2=w2, gwt=gwT,
            beta=beta, lng=lng, lnb=lnb,
            ident=ident, tri16=tri16, mask16=mask16, slt=slt, iota=iota,
            iota8=iota8, tplus1=tplus1, e8=e8, m8=m8,
        ))
    res = run_bass_kernel_spmd(nc, in_maps, core_ids=list(range(NCORES)))
    out = np.concatenate([res.results[c]["y_out"] for c in range(NCORES)], axis=0)
    return out.reshape(B, Sq, Dm)


# version 32
# speedup vs baseline: 1.1330x; 1.0133x over previous
import numpy as np
import ml_dtypes
from contextlib import ExitStack

import concourse.bass as bass
import concourse.mybir as mybir
from concourse.bass_utils import run_bass_kernel_spmd
from concourse.bass_core_rust import IndirectOffsetOnAxis

F32 = mybir.dt.float32
BF16 = mybir.dt.bfloat16
U32 = mybir.dt.uint32
AF = mybir.ActivationFunctionType
ALU = mybir.AluOpType
AX = mybir.AxisListType

NCORES = 8
TOK = 2048
D = 2048
H, DH = 16, 128
NCH = 16
KC = 16
PCAP = 384
NSLOT = 8 * PCAP
NPREV = 3
BIG = 1.0e6
EPS = 1e-5

ENG = {"pe": "tensor", "act": "scalar", "dve": "vector", "io": "sync", "gp": "gpsimd"}
DEF_SEM = {"pe": "spe", "act": "sac", "dve": "sdv", "io": None, "gp": None}


class Rec:
    __slots__ = ("writers", "readers", "name")

    def __init__(self, name, seed=()):
        self.name = name
        self.writers = list(seed)
        self.readers = []

    def toks(self):
        return self.writers + self.readers


class KB:

    def __init__(self):
        self.nc = bass.Bass("TRN2", target_bir_lowering=False, debug=False)
        self.sems = {}
        self.waited = {}
        for nm in ("spe", "sac", "sdv"):
            self.sems[nm] = [self.nc.alloc_semaphore(name=nm), 0]

    def lane(self, nm):
        if nm not in self.sems:
            self.sems[nm] = [self.nc.alloc_semaphore(name=nm), 0]
        return nm

    def _wait(self, engname, toks):
        eng = getattr(self.nc, ENG[engname])
        best = {}
        for (nm, val) in toks:
            if val > best.get(nm, 0):
                best[nm] = val
        for nm, val in best.items():
            key = (engname, nm)
            if self.waited.get(key, 0) >= val:
                continue
            eng.wait_ge(self.sems[nm][0], val)
            self.waited[key] = val

    def emit(self, engname, fns, ins=(), outs=(), sem=None, last_only=False):
        toks = []
        for r in ins:
            toks += r.writers
        for r in outs:
            toks += r.toks()
        self._wait(engname, toks)
        eng = getattr(self.nc, ENG[engname])
        instr = None
        for fn in fns:
            instr = fn(eng)
        if sem is None:
            sem = DEF_SEM[engname]
            assert sem is not None, f"DMA op on {engname} must specify a lane"
        else:
            self.lane(sem)
        k = 1 if sem in ("spe", "sac", "sdv", "cc") else 16
        rec = self.sems[sem]
        instr.then_inc(rec[0], k)
        rec[1] += k
        tok = (sem, rec[1])
        for r in ins:
            r.readers.append(tok)
        for r in outs:
            r.writers = [tok]
            r.readers = []
        return tok

    def op(self, engname, fn, ins=(), outs=(), sem=None):
        return self.emit(engname, [fn], ins=ins, outs=outs, sem=sem)


def build_program():
    kb = KB()
    nc = kb.nc
    ctx = ExitStack()
    en = ctx.enter_context
    seeds = []

    ap = lambda t: t.ap()
    x_own = ap(nc.dram_tensor("x_own", [TOK, D], F32, kind="ExternalInput"))
    Wq_d = ap(nc.dram_tensor("wq", [128, KC * D], F32, kind="ExternalInput"))
    Wv_d = ap(nc.dram_tensor("wv", [128, KC * D], F32, kind="ExternalInput"))
    Wg_d = ap(nc.dram_tensor("wg", [128, KC * D], F32, kind="ExternalInput"))
    Wo_d = ap(nc.dram_tensor("wo", [128, KC * D], F32, kind="ExternalInput"))
    w1_d = ap(nc.dram_tensor("w1", [8, 128, KC * D], F32, kind="ExternalInput"))
    w2_d = ap(nc.dram_tensor("w2", [8, 128, KC * D], F32, kind="ExternalInput"))
    gwT_d = ap(nc.dram_tensor("gwt", [128, KC * 8], F32, kind="ExternalInput"))
    beta_d = ap(nc.dram_tensor("beta", [1, H], F32, kind="ExternalInput"))
    lng_d = ap(nc.dram_tensor("lng", [1, D], F32, kind="ExternalInput"))
    lnb_d = ap(nc.dram_tensor("lnb", [1, D], F32, kind="ExternalInput"))
    ident_d = ap(nc.dram_tensor("ident", [128, 128], BF16, kind="ExternalInput"))
    tri16_d = ap(nc.dram_tensor("tri16", [128, H * 128], F32, kind="ExternalInput"))
    mask16_d = ap(nc.dram_tensor("mask16", [128, H * 128], BF16, kind="ExternalInput"))
    slt_d = ap(nc.dram_tensor("slt", [128, 128], BF16, kind="ExternalInput"))
    iota_d = ap(nc.dram_tensor("iota", [128, 1], F32, kind="ExternalInput"))
    iota8_d = ap(nc.dram_tensor("iota8", [1, 8], F32, kind="ExternalInput"))
    tplus1_d = ap(nc.dram_tensor("tplus1", [1, H * 128], F32, kind="ExternalInput"))
    e8_d = ap(nc.dram_tensor("e8", [8, 1], F32, kind="ExternalInput"))
    m8_d = ap(nc.dram_tensor("m8", [8, 1], F32, kind="ExternalInput"))
    y_out = ap(nc.dram_tensor("y_out", [TOK, D], F32, kind="ExternalOutput"))
    u_dram_t = nc.dram_tensor("u_dram", [1, D], BF16, kind="Internal")
    u_all_t = nc.dram_tensor("u_all", [8, D], BF16, kind="Internal")
    u_dram = u_dram_t.ap()
    u_all = u_all_t.ap()

    s_xlnT = ap(nc.dram_tensor("s_xlnt", [NCH, 128, D], BF16, kind="Internal"))
    s_v = ap(nc.dram_tensor("s_v", [NCH, 128, D], BF16, kind="Internal"))
    s_g = ap(nc.dram_tensor("s_g", [NCH, 128, D], BF16, kind="Internal"))
    s_x1 = ap(nc.dram_tensor("s_x1", [NCH, 128, D], BF16, kind="Internal"))
    s_disp = ap(nc.dram_tensor("s_disp", [NSLOT, D], BF16, kind="Internal"))

    s_eout = ap(nc.dram_tensor("s_eout", [NSLOT, D], BF16, kind="Internal"))

    r_sxlnT = [Rec(f"sxlnT{c}") for c in range(NCH)]
    r_sv = [Rec(f"sv{c}") for c in range(NCH)]
    r_sg = [Rec(f"sg{c}") for c in range(NCH)]
    r_sx1 = [Rec(f"sx1{c}") for c in range(NCH)]
    r_disp = Rec("s_disp")
    rsc = Rec("sc_chain")
    rgc = Rec("gc_chain")
    r_eout = Rec("s_eout")

    IOA = IndirectOffsetOnAxis

    def SA(stack, name, shape, dt):
        t = stack.enter_context(nc.sbuf_tensor(name, shape, dt))
        return t, Rec(name, seed=seeds)

    attn_ctx = ExitStack()
    WA, rWA = SA(ctx, "WA", [128, KC * D], BF16)
    WB, rWB = SA(ctx, "WB", [128, KC * D], BF16)
    rWAq = [Rec(f"WAq{i}") for i in range(4)]
    rWBq = [Rec(f"WBq{i}") for i in range(4)]
    ident, rident = SA(ctx, "identsb", [128, 128], BF16)
    slt, rslt = SA(ctx, "sltsb", [128, 128], BF16)
    gwT, rgwT = SA(ctx, "gwT", [128, KC * 8], BF16)
    iota, riota = SA(ctx, "iotasb", [128, 1], F32)
    iota8bc, riota8 = SA(ctx, "iota8bc", [128, 8], F32)
    ones8, rones8 = SA(ctx, "ones8", [128, 8], F32)
    onesr_f, ronesr = SA(ctx, "onesr_f", [1, 128], F32)
    onesc_bf, ronesc = SA(ctx, "onesc_bf", [128, 1], BF16)
    lnl_bc, rlnl = SA(ctx, "lnl_bc", [128, H], F32)
    wdec_all, rwdec = SA(ctx, "wdec_all", [128, NCH * H], BF16)
    u_row, ru_row = SA(ctx, "u_row", [1, D], BF16)
    e8sb, re8 = SA(ctx, "e8sb", [8, 1], F32)
    m8sb, rm8 = SA(ctx, "m8sb", [8, 1], F32)
    w8f, rw8f = SA(ctx, "w8f", [8, H], F32)
    w8b, rw8b = SA(ctx, "w8b", [8, H], BF16)
    wtok, rwtok = SA(ctx, "wtok", [128, NCH], F32)
    slots, rslots = SA(ctx, "slots", [128, NCH], U32)
    runcnt, rruncnt = SA(ctx, "runcnt", [1, 8], F32)
    cnt8, rcnt8 = SA(ctx, "cnt8", [1, 8], F32)
    xbuf0, rx0 = SA(ctx, "xbuf0", [128, D], BF16)
    c_s1, rs1 = SA(ctx, "c_s1", [128, 1], F32)
    c_s2, rs2 = SA(ctx, "c_s2", [128, 1], F32)
    c_mu, rmu = SA(ctx, "c_mu", [128, 1], F32)
    c_msq, rmsq = SA(ctx, "c_msq", [128, 1], F32)
    c_var, rvar = SA(ctx, "c_var", [128, 1], F32)
    c_std, rstd_ = SA(ctx, "c_std", [128, 1], F32)
    c_rstd, rrstd = SA(ctx, "c_rstd", [128, 1], F32)
    c_nmr, rnmr = SA(ctx, "c_nmr", [128, 1], F32)
    lg8a, rlg8a = SA(ctx, "lg8a", [128, 8], F32)
    lg8b, rlg8b = SA(ctx, "lg8b", [128, 8], F32)
    t8a, rt8a = SA(ctx, "t8a", [128, 8], F32)
    t8b, rt8b = SA(ctx, "t8b", [128, 8], F32)
    oh8, roh8 = SA(ctx, "oh8", [128, 8], F32)
    oh8b, roh8b = SA(ctx, "oh8b", [128, 8], BF16)
    pexp8, rpexp8 = SA(ctx, "pexp8", [128, 8], F32)
    c_lmax, rlmax = SA(ctx, "c_lmax", [128, 1], F32)
    c_nlmax, rnlmax = SA(ctx, "c_nlmax", [128, 1], F32)
    c_sexp, rsexp = SA(ctx, "c_sexp", [128, 1], F32)
    c_rsum, rrsum = SA(ctx, "c_rsum", [128, 1], F32)
    c_ptop, rptop = SA(ctx, "c_ptop", [128, 1], F32)
    c_w, rcw = SA(ctx, "c_w", [128, 1], F32)
    c_rk1, rrk1 = SA(ctx, "c_rk1", [128, 1], F32)
    c_rk2, rrk2 = SA(ctx, "c_rk2", [128, 1], F32)
    c_rank, rrank = SA(ctx, "c_rank", [128, 1], F32)
    c_keep, rkeep = SA(ctx, "c_keep", [128, 1], F32)
    c_et, rcet = SA(ctx, "c_et", [128, 1], F32)
    c_sf, rcsf = SA(ctx, "c_sf", [128, 1], F32)
    c_sk, rcsk = SA(ctx, "c_sk", [128, 1], F32)
    c_bigk, rbigk = SA(ctx, "c_bigk", [128, 1], F32)
    c_sm, rcsm = SA(ctx, "c_sm", [128, 1], F32)
    c_ecol, recol = SA(ctx, "c_ecol", [128, 1], F32)
    c_mubf, rmubf = SA(ctx, "c_mubf", [128, 1], BF16)

    At_all, rAt = SA(attn_ctx, "At_all", [128, H * 128], BF16)
    P_row, rProw = SA(attn_ctx, "P_row", [1, H * 128], BF16)
    lng_bc, rlng = SA(attn_ctx, "lng_bc", [128, D], BF16)
    lnb_bc, rlnb = SA(attn_ctx, "lnb_bc", [128, D], BF16)
    c_stage, rcst = SA(attn_ctx, "c_stage", [1, H * 128], BF16)
    rcst4 = [Rec(f"cst{q4}") for q4 in range(4)]

    ps_tr = en(nc.psum_tensor("ps_tr", [128, 512], F32))
    rtrp = Rec("ps_tr")
    ps_sm = en(nc.psum_tensor("ps_sm", [128, 512], F32))
    rsm = Rec("ps_sm")
    ps_a0 = en(nc.psum_tensor("ps_a0", [128, 512], F32))
    ra0 = Rec("ps_a0")
    ps_a1 = en(nc.psum_tensor("ps_a1", [128, 512], F32))
    ra1 = Rec("ps_a1")
    ps_a2 = en(nc.psum_tensor("ps_a2", [128, 512], F32))
    ra2 = Rec("ps_a2")
    ps_a3 = en(nc.psum_tensor("ps_a3", [128, 512], F32))
    ra3 = Rec("ps_a3")
    ps_a4 = en(nc.psum_tensor("ps_a4", [128, 512], F32))
    ra4 = Rec("ps_a4")
    ps_a5 = en(nc.psum_tensor("ps_a5", [128, 512], F32))
    ra5 = Rec("ps_a5")

    setup_ctx = ExitStack()
    tri16, rtri16 = SA(setup_ctx, "tri16sb", [128, H * 128], F32)
    mask16, rmask16 = SA(setup_ctx, "mask16sb", [128, H * 128], BF16)
    lnlbig, rlnlbig = SA(setup_ctx, "lnlbig", [128, H * 128], F32)
    beta_bc, rbeta = SA(setup_ctx, "beta_bc", [128, H], F32)
    tmp16a, rtmp16a = SA(setup_ctx, "tmp16a", [128, H], F32)
    tmp16b, rtmp16b = SA(setup_ctx, "tmp16b", [128, H], F32)
    tp1, rtp1 = SA(setup_ctx, "tp1", [1, H * 128], F32)
    lnlexp, rlnlexp = SA(setup_ctx, "lnlexp", [1, H * 128], F32)
    prow_f, rprowf = SA(setup_ctx, "prow_f", [1, H * 128], F32)

    kb.op("io", lambda e: e.dma_start(out=ident[:], in_=ident_d), outs=[rident, rsc], sem="sc")
    kb.op("io", lambda e: e.dma_start(out=tri16[:], in_=tri16_d), outs=[rtri16, rsc], sem="sc")
    kb.op("io", lambda e: e.dma_start(out=mask16[:], in_=mask16_d), outs=[rmask16, rsc], sem="sc")
    kb.op("io", lambda e: e.dma_start(out=slt[:], in_=slt_d), outs=[rslt, rsc], sem="sc")
    kb.op("io", lambda e: e.dma_start(out=iota[:], in_=iota_d), outs=[riota, rsc], sem="sc")
    kb.op("io", lambda e: e.dma_start(out=iota8bc[:], in_=iota8_d.broadcast_to([128, 8])),
          outs=[riota8, rsc], sem="sc")
    kb.op("io", lambda e: e.dma_start(out=tp1[:], in_=tplus1_d), outs=[rtp1, rsc], sem="sc")
    kb.op("io", lambda e: e.dma_start(out=beta_bc[:], in_=beta_d.broadcast_to([128, H])),
          outs=[rbeta, rsc], sem="sc")
    kb.op("gp", lambda e: e.dma_start(out=lng_bc[:], in_=lng_d.broadcast_to([128, D])),
          outs=[rlng, rgc], sem="gc")
    kb.op("gp", lambda e: e.dma_start(out=lnb_bc[:], in_=lnb_d.broadcast_to([128, D])),
          outs=[rlnb, rgc], sem="gc")
    kb.op("gp", lambda e: e.dma_start(out=gwT[:], in_=gwT_d), outs=[rgwT, rgc], sem="gc")
    kb.op("dve", lambda e: e.memset(ones8[:], 1.0), outs=[rones8])
    kb.op("dve", lambda e: e.memset(runcnt[:], 0.0), outs=[rruncnt])
    kb.op("dve", lambda e: e.memset(onesr_f[:], 1.0), outs=[ronesr])
    kb.op("dve", lambda e: e.memset(onesc_bf[:], 1.0), outs=[ronesc])

    kb.op("act", lambda e: e.activation(out=tmp16a[:], in_=beta_bc[:], func=AF.Abs),
          ins=[rbeta], outs=[rtmp16a])
    kb.op("dve", lambda e: e.tensor_scalar_mul(tmp16a[:], tmp16a[:], -1.0),
          ins=[rtmp16a], outs=[rtmp16a])
    kb.op("act", lambda e: e.activation(out=tmp16a[:], in_=tmp16a[:], func=AF.Exp),
          ins=[rtmp16a], outs=[rtmp16a])
    kb.op("act", lambda e: e.activation(out=tmp16a[:], in_=tmp16a[:], func=AF.Ln, bias=1.0),
          ins=[rtmp16a], outs=[rtmp16a])
    kb.op("dve", lambda e: e.tensor_scalar(out=tmp16b[:], in0=beta_bc[:], scalar1=-1.0,
                                           scalar2=0.0, op0=ALU.mult, op1=ALU.max),
          ins=[rbeta], outs=[rtmp16b])
    kb.op("dve", lambda e: e.tensor_add(out=tmp16b[:], in0=tmp16b[:], in1=tmp16a[:]),
          ins=[rtmp16a, rtmp16b], outs=[rtmp16b])
    kb.op("dve", lambda e: e.tensor_scalar_mul(lnl_bc[:], tmp16b[:], -1.0),
          ins=[rtmp16b], outs=[rlnl])

    kb.op("dve", lambda e: e.tensor_copy(
        out=lnlexp[:].rearrange("o (h t) -> o h t", h=H),
        in_=lnl_bc[0:1, :].unsqueeze(2).broadcast_to([1, H, 128])),
        ins=[rlnl], outs=[rlnlexp])
    for cg in range(4):
        pscg = [ps_a0, ps_a1][cg % 2]
        rcg = [ra0, ra1][cg % 2]
        kb.emit("pe", [lambda e, cg=cg, pscg=pscg: e.matmul(
            pscg[:, 0:512], onesr_f[0:1, :], lnlexp[0:1, cg * 512:(cg + 1) * 512],
            start=True, stop=True)], ins=[ronesr, rlnlexp], outs=[rcg])
        kb.op("dve", lambda e, cg=cg, pscg=pscg: e.tensor_copy(
            out=lnlbig[:, cg * 512:(cg + 1) * 512], in_=pscg[:, 0:512]),
            ins=[rcg], outs=[rlnlbig])
    kb.op("dve", lambda e: e.tensor_mul(out=lnlbig[:], in0=lnlbig[:], in1=tri16[:]),
          ins=[rtri16, rlnlbig], outs=[rlnlbig])
    kb.op("act", lambda e: e.activation(out=lnlbig[:], in_=lnlbig[:], func=AF.Exp),
          ins=[rlnlbig], outs=[rlnlbig])
    kb.op("dve", lambda e: e.tensor_tensor(out=At_all[:], in0=lnlbig[:], in1=mask16[:],
                                           op=ALU.mult),
          ins=[rlnlbig, rmask16], outs=[rAt])

    kb.op("dve", lambda e: e.tensor_mul(out=prow_f[:], in0=tp1[:], in1=lnlexp[:]),
          ins=[rtp1, rlnlexp], outs=[rprowf])
    kb.op("act", lambda e: e.activation(out=prow_f[:], in_=prow_f[:], func=AF.Exp),
          ins=[rprowf], outs=[rprowf])
    kb.op("act", lambda e: e.copy(out=P_row[:], in_=prow_f[:]), ins=[rprowf], outs=[rProw])

    kb.op("io", lambda e: e.dma_start(out=e8sb[:], in_=e8_d), outs=[re8, rsc], sem="sc")
    kb.op("io", lambda e: e.dma_start(out=m8sb[:], in_=m8_d), outs=[rm8, rsc], sem="sc")
    kb.op("dve", lambda e: e.memset(u_row[:], 0.0), outs=[ru_row])
    for c in range(NCH):
        kb.op("dve", lambda e, c=c: e.tensor_scalar(
            out=c_s1[:], in0=iota[:], scalar1=-1.0, scalar2=float(2047 - 128 * c),
            op0=ALU.mult, op1=ALU.add),
            ins=[riota], outs=[rs1])
        kb.op("dve", lambda e: e.tensor_scalar(
            out=tmp16a[:], in0=lnl_bc[:], scalar1=c_s1[:, 0:1], scalar2=None,
            op0=ALU.mult),
            ins=[rlnl, rs1], outs=[rtmp16a])
        kb.op("act", lambda e: e.activation(out=tmp16a[:], in_=tmp16a[:], func=AF.Exp),
              ins=[rtmp16a], outs=[rtmp16a])
        kb.op("act", lambda e, c=c: e.copy(out=wdec_all[:, c * H:(c + 1) * H],
                                           in_=tmp16a[:]),
              ins=[rtmp16a], outs=[rwdec])

    seeds = [t for r in (rtri16, rmask16, rlnlbig, rbeta, rtmp16a, rtmp16b, rtp1,
                         rlnlexp, rprowf) for t in r.toks()]
    setup_ctx.close()

    a12_ctx = ExitStack()
    xbuf1, rx1 = SA(a12_ctx, "xbuf1", [128, D], BF16)
    xln_t = SA(a12_ctx, "xln0", [128, D], BF16)
    xlns = [xln_t, xln_t]
    xlnTs = [SA(a12_ctx, "xlnT0", [128, D], BF16), SA(a12_ctx, "xlnT1", [128, D], BF16)]
    v_t = SA(a12_ctx, "v0", [128, D], BF16)
    vsb = [v_t, v_t]
    g_t = SA(a12_ctx, "g0", [128, D], BF16)
    gsb = [g_t, g_t]
    q_f32, rq = SA(a12_ctx, "q_f32", [128, D], F32)
    S_sb, rS = SA(a12_ctx, "S_sb", [128, D], BF16)
    ysb = [SA(a12_ctx, "y0", [128, D], BF16), SA(a12_ctx, "y1", [128, D], BF16)]
    z_bf, rz = SA(a12_ctx, "z_bf", [128, D], BF16)
    q_bf, rqb = None, None
    d_s1, qs1 = SA(a12_ctx, "d_s1", [128, 1], F32)
    d_s2, qs2 = SA(a12_ctx, "d_s2", [128, 1], F32)
    d_mu, qmu = SA(a12_ctx, "d_mu", [128, 1], F32)
    d_msq, qmsq = SA(a12_ctx, "d_msq", [128, 1], F32)
    d_var, qvar = SA(a12_ctx, "d_var", [128, 1], F32)
    d_std, qstd = SA(a12_ctx, "d_std", [128, 1], F32)
    d_rstd, qrstd = SA(a12_ctx, "d_rstd", [128, 1], F32)
    d_nmr, qnmr = SA(a12_ctx, "d_nmr", [128, 1], F32)

    class BCView:
        def __init__(self, t, off=0):
            self.t = t
            self.off = off
        def __getitem__(self, key):
            if key == slice(None):
                return self.t[:, self.off:self.off + D // 2].bitcast(BF16)
            _, cs = key
            return self.t[:, self.off + cs.start // 2: self.off + cs.stop // 2].bitcast(BF16)

    q_bf, rqb = BCView(q_f32, 1024), rq

    xb = [(xbuf0, rx0), (xbuf1, rx1)]
    EV = ["act", "dve", "act", "dve"]

    def pcopy(engname, dst_ap_fn, ps, rps, dst_rec):
        if engname == "act":
            kb.op("act", lambda e: e.copy(out=dst_ap_fn(), in_=ps[:, 0:512]),
                  ins=[rps], outs=[dst_rec])
        else:
            kb.op("dve", lambda e: e.tensor_copy(out=dst_ap_fn(), in_=ps[:, 0:512]),
                  ins=[rps], outs=[dst_rec])

    def ln_stats(src_rec, src_tile, sq_tile, sq_rec, cols):
        (s1_, t1), (s2_, t2), (mu_, tmu), (msq_, tmsq), (var_, tvar), (std_, tstd), \
            (rstdc, trstd), (nmrc, tnmr) = cols
        kb.op("dve", lambda e: e.reduce_sum(out=s1_[:], in_=src_tile[:], axis=AX.X),
              ins=[src_rec], outs=[t1])
        kb.op("act", lambda e: e.activation(out=sq_tile[:], in_=src_tile[:], func=AF.Square,
                                            accum_out=s2_[:]),
              ins=[src_rec], outs=[sq_rec, t2])
        kb.op("dve", lambda e: e.tensor_scalar_mul(mu_[:], s1_[:], 1.0 / D),
              ins=[t1], outs=[tmu])
        kb.op("dve", lambda e: e.tensor_scalar(out=msq_[:], in0=mu_[:],
                                               scalar1=mu_[:, 0:1], scalar2=EPS,
                                               op0=ALU.mult, op1=ALU.subtract),
              ins=[tmu], outs=[tmsq])
        kb.op("dve", lambda e: e.tensor_scalar(out=var_[:], in0=s2_[:], scalar1=1.0 / D,
                                               scalar2=msq_[:, 0:1], op0=ALU.mult,
                                               op1=ALU.subtract),
              ins=[t2, tmsq], outs=[tvar])
        kb.op("act", lambda e: e.activation(out=std_[:], in_=var_[:], func=AF.Sqrt),
              ins=[tvar], outs=[tstd])
        kb.op("dve", lambda e: e.reciprocal(out=rstdc[:], in_=std_[:]),
              ins=[tstd], outs=[trstd])
        kb.op("dve", lambda e: e.tensor_scalar(out=nmrc[:], in0=mu_[:],
                                               scalar1=rstdc[:, 0:1], scalar2=-1.0,
                                               op0=ALU.mult, op1=ALU.mult),
              ins=[tmu, trstd], outs=[tnmr])
        return (rstdc, trstd), (nmrc, tnmr)

    colsA = [(c_s1, rs1), (c_s2, rs2), (c_mu, rmu), (c_msq, rmsq), (c_var, rvar),
             (c_std, rstd_), (c_rstd, rrstd), (c_nmr, rnmr)]
    colsB = [(d_s1, qs1), (d_s2, qs2), (d_mu, qmu), (d_msq, qmsq), (d_var, qvar),
             (d_std, qstd), (d_rstd, qrstd), (d_nmr, qnmr)]

    def transpose16(src_tile, src_rec, dst_tile, dst_rec):
        for rnd in range(4):
            fns = []
            for j in range(4):
                kcx = rnd * 4 + j
                fns.append(lambda e, kcx=kcx, j=j: e.matmul(
                    ps_tr[:, j * 128:(j + 1) * 128],
                    src_tile[:, kcx * 128:(kcx + 1) * 128], ident[:],
                    start=True, stop=True))
            kb.emit("pe", fns, ins=[src_rec, rident], outs=[rtrp])
            if rnd % 2 == 0:
                kb.op("act", lambda e, rnd=rnd: e.copy(
                    out=dst_tile[:, rnd * 512:(rnd + 1) * 512], in_=ps_tr[:]),
                    ins=[rtrp], outs=[dst_rec])
            else:
                kb.op("dve", lambda e, rnd=rnd: e.tensor_copy(
                    out=dst_tile[:, rnd * 512:(rnd + 1) * 512], in_=ps_tr[:]),
                    ins=[rtrp], outs=[dst_rec])

    pss4 = [(ps_a0, ra0), (ps_a1, ra1), (ps_a0, ra0), (ps_a1, ra1)]

    def big_mm(lhsT_tile, lhsT_rec, w_tile, w_rec, evac, four_banks=True,
               banks=(0, 1)):
        if four_banks:
            for qt in range(4):
                fns = []
                for kcq in range(4):
                    kcx = qt * 4 + kcq
                    for cg in range(4):
                        ps, rps = pss4[cg]
                        fns.append(lambda e, kcx=kcx, cg=cg, ps=ps: e.matmul(
                            ps[:, 0:512],
                            lhsT_tile[:, kcx * 128:(kcx + 1) * 128],
                            w_tile[:, kcx * D + cg * 512: kcx * D + (cg + 1) * 512],
                            start=(kcx == 0), stop=(kcx == KC - 1)))
                kb.emit("pe", fns, ins=[lhsT_rec, w_rec[qt]],
                        outs=[pss4[0][1], pss4[1][1], pss4[2][1], pss4[3][1]])
            for cg in range(4):
                ps, rps = pss4[cg]
                evac(cg, ps, rps)
        else:
            for pair in range(2):
                cga, cgb = 2 * pair, 2 * pair + 1
                psa, rpsa = pss4[banks[0]]
                psb_, rpsb = pss4[banks[1]]
                for qt in range(4):
                    fns = []
                    for kcq in range(4):
                        kcx = qt * 4 + kcq
                        for cg, ps in ((cga, psa), (cgb, psb_)):
                            fns.append(lambda e, kcx=kcx, cg=cg, ps=ps: e.matmul(
                                ps[:, 0:512],
                                lhsT_tile[:, kcx * 128:(kcx + 1) * 128],
                                w_tile[:, kcx * D + cg * 512: kcx * D + (cg + 1) * 512],
                                start=(kcx == 0), stop=(kcx == KC - 1)))
                    kb.emit("pe", fns, ins=[lhsT_rec, w_rec[qt]], outs=[rpsa, rpsb])
                evac(cga, psa, rpsa)
                evac(cgb, psb_, rpsb)

    def a1_step(c):
        xt, rxt = xb[c % 2]
        xlnp, rxlnp = xlns[c % 2]
        xlnTp, rxlnTp = xlnTs[c % 2]
        vp, rvp = vsb[c % 2]
        gp_, rgp = gsb[c % 2]
        if c >= 2:
            kb.op("gp", lambda e, c=c, xt=xt: e.dma_start(
                out=xt[:], in_=x_own[c * 128:(c + 1) * 128, :]), outs=[rxt],
                sem="x0")
        (rst, trst), (nmr_, tnmr) = ln_stats(rxt, xt, z_bf, rz, colsA)
        kb.op("dve", lambda e, xt=xt, xlnp=xlnp: e.tensor_scalar(
            out=xlnp[:], in0=xt[:], scalar1=rst[:, 0:1], scalar2=nmr_[:, 0:1],
            op0=ALU.mult, op1=ALU.add),
            ins=[rxt, trst, tnmr], outs=[rxlnp])
        transpose16(xlnp, rxlnp, xlnTp, rxlnTp)
        kb.op("io", lambda e, c=c, xlnTp=xlnTp: e.dma_start(out=s_xlnT[c], in_=xlnTp[:]),
              ins=[rxlnTp], outs=[r_sxlnT[c]], sem="sxt")

        def evac_v(cg, ps, rps):
            pcopy(EV[cg], lambda cg=cg, vp=vp: vp[:, cg * 512:(cg + 1) * 512], ps, rps, rvp)
        big_mm(xlnTp, rxlnTp, WA, rWAq, evac_v, four_banks=False)
        kb.op("io", lambda e, c=c, vp=vp: e.dma_start(out=s_v[c], in_=vp[:]),
              ins=[rvp], outs=[r_sv[c]], sem="sv")

        for q4 in range(4):
            bank, rbank = psU[q4]
            fns = [lambda e, h=h, vp=vp, bank=bank: e.matmul(
                bank[0:1, (h % 4) * 128:(h % 4) * 128 + 128],
                wdec_all[:, c * H + h: c * H + h + 1],
                vp[:, h * 128:(h + 1) * 128],
                start=True, stop=True) for h in range(4 * q4, 4 * q4 + 4)]
            kb.emit("pe", fns, ins=[rwdec, rvp], outs=[rbank])
            kb.op("dve", lambda e, q4=q4, bank=bank: e.tensor_tensor(
                out=u_row[0:1, q4 * 512:(q4 + 1) * 512], in0=bank[0:1, 0:512],
                in1=u_row[0:1, q4 * 512:(q4 + 1) * 512], op=ALU.add),
                ins=[rbank], outs=[ru_row])

        def evac_g(cg, ps, rps):
            kb.op("act", lambda e, cg=cg, ps=ps, gp_=gp_: e.activation(
                out=gp_[:, cg * 512:(cg + 1) * 512], in_=ps[:, 0:512], func=AF.Silu),
                ins=[rps], outs=[rgp])
        big_mm(xlnTp, rxlnTp, WB, rWBq, evac_g, four_banks=False)
        kb.op("io", lambda e, c=c, gp_=gp_: e.dma_start(out=s_g[c], in_=gp_[:]),
              ins=[rgp], outs=[r_sg[c]], sem="sg")

    pss4[:] = [(ps_a0, ra0), (ps_a1, ra1), (ps_a2, ra2), (ps_a3, ra3)]
    pssS = [(ps_a4, ra4), (ps_a5, ra5), (ps_a4, ra4), (ps_a5, ra5)]
    psU = [(ps_a2, ra2), (ps_a3, ra3), (ps_a4, ra4), (ps_a5, ra5)]
    zT = BCView(q_f32, 0)
    rzT = Rec("zT", seed=rq.toks())

    kb.op("gp", lambda e: e.dma_start(out=xbuf0[:], in_=x_own[0:128, :]),
          outs=[rx0], sem="x0")
    kb.op("gp", lambda e: e.dma_start(
        out=WA[:, 0:4 * D], in_=Wv_d[:, 0:4 * D]), outs=[rWAq[0]], sem="wa")
    kb.op("gp", lambda e: e.dma_start(out=xbuf1[:], in_=x_own[128:256, :]),
          outs=[rx1], sem="x0")
    for qt in range(1, 4):
        kb.op("gp", lambda e, qt=qt: e.dma_start(
            out=WA[:, qt * 4 * D:(qt + 1) * 4 * D],
            in_=Wv_d[:, qt * 4 * D:(qt + 1) * 4 * D]),
            outs=[rWAq[qt]], sem="wa")
    for qt in range(4):
        kb.op("gp", lambda e, qt=qt: e.dma_start(
            out=WB[:, qt * 4 * D:(qt + 1) * 4 * D],
            in_=Wg_d[:, qt * 4 * D:(qt + 1) * 4 * D]),
            outs=[rWBq[qt]], sem="wb")

    for c in range(NCH):
        a1_step(c)

    for qt in range(4):
        kb.op("gp", lambda e, qt=qt: e.dma_start(
            out=WA[:, qt * 4 * D:(qt + 1) * 4 * D],
            in_=Wq_d[:, qt * 4 * D:(qt + 1) * 4 * D]),
            outs=[rWAq[qt]], sem="wa")
        kb.op("gp", lambda e, qt=qt: e.dma_start(
            out=WB[:, qt * 4 * D:(qt + 1) * 4 * D],
            in_=Wo_d[:, qt * 4 * D:(qt + 1) * 4 * D]),
            outs=[rWBq[qt]], sem="wb")

    r_ud = Rec("u_dram")
    r_cc = Rec("u_all")
    kb.op("io", lambda e: e.dma_start(out=u_dram, in_=u_row[0:1, :]),
          ins=[ru_row], outs=[r_ud], sem="ud")
    kb.op("gp", lambda e: e.collective_compute(
        "AllGather", ALU.bypass, replica_groups=[list(range(NCORES))],
        ins=[u_dram_t.ap().opt()], outs=[u_all_t.ap().opt()]),
        ins=[r_ud], outs=[r_cc], sem="cc")
    u8_t, r_u8 = xlnTs[0]
    kb.op("io", lambda e: e.dma_start(out=u8_t[0:8, :], in_=u_all),
          ins=[r_cc], outs=[r_u8], sem="u8")
    kb.op("dve", lambda e: e.tensor_scalar(
        out=w8f[:], in0=lnl_bc[0:8, :], scalar1=e8sb[0:8, 0:1], scalar2=None,
        op0=ALU.mult),
        ins=[rlnl, re8], outs=[rw8f])
    kb.op("act", lambda e: e.activation(out=w8f[:], in_=w8f[:], func=AF.Exp),
          ins=[rw8f], outs=[rw8f])
    kb.op("dve", lambda e: e.tensor_scalar(
        out=w8f[:], in0=w8f[:], scalar1=m8sb[0:8, 0:1], scalar2=None, op0=ALU.mult),
        ins=[rw8f, rm8], outs=[rw8f])
    kb.op("act", lambda e: e.copy(out=w8b[:], in_=w8f[:]), ins=[rw8f], outs=[rw8b])
    for q4 in range(4):
        psb, rpbh = pssS[q4]
        fns = [lambda e, h=h, psb=psb: e.matmul(
            psb[0:1, (h % 4) * 128:(h % 4) * 128 + 128],
            w8b[0:8, h:h + 1],
            u8_t[0:8, h * 128:(h + 1) * 128],
            start=True, stop=True) for h in range(4 * q4, 4 * q4 + 4)]
        kb.emit("pe", fns, ins=[rw8b, r_u8], outs=[rpbh])
        kb.op("act", lambda e, q4=q4, psb=psb: e.copy(
            out=c_stage[0:1, q4 * 512:(q4 + 1) * 512], in_=psb[0:1, :]),
            ins=[rpbh], outs=[rcst4[q4]])

    for cc2 in range(NCH):
        r_sxlnT[cc2].writers = [("sxt", kb.sems["sxt"][1])]
        r_sv[cc2].writers = [("sv", kb.sems["sv"][1])]
        r_sg[cc2].writers = [("sg", kb.sems["sg"][1])]

    def a2_s1a(c):
        xlnTp, rxlnTp = xlnTs[c % 2]
        vp, rvp = vsb[c % 2]
        yp, ryp = ysb[c % 2]
        kb.op("io", lambda e, c=c, xlnTp=xlnTp: e.dma_start(out=xlnTp[:], in_=s_xlnT[c]),
              ins=[r_sxlnT[c]], outs=[rxlnTp], sem="lxt")
        kb.op("io", lambda e, c=c, vp=vp: e.dma_start(out=vp[:], in_=s_v[c]),
              ins=[r_sv[c]], outs=[rvp], sem="lv")

        def evac_q(cg, ps, rps):
            pcopy(EV[cg], lambda cg=cg: q_bf[:, cg * 512:(cg + 1) * 512], ps, rps, rqb)
        big_mm(xlnTp, rxlnTp, WA, rWAq, evac_q, four_banks=False, banks=(0, 1))

        for q4 in range(4):
            psb, rpbh = pssS[q4]
            fns = []
            for h in range(4 * q4, 4 * q4 + 4):
                fns.append(lambda e, h=h, vp=vp, psb=psb: e.matmul(
                    psb[:, (h % 4) * 128:(h % 4) * 128 + 128],
                    At_all[:, h * 128:(h + 1) * 128],
                    vp[:, h * 128:(h + 1) * 128],
                    start=True, stop=False))
                fns.append(lambda e, h=h, psb=psb: e.matmul(
                    psb[:, (h % 4) * 128:(h % 4) * 128 + 128],
                    P_row[0:1, h * 128:(h + 1) * 128],
                    c_stage[0:1, h * 128:(h + 1) * 128],
                    start=False, stop=True))
            kb.emit("pe", fns, ins=[rAt, rvp, rProw, rcst4[q4]], outs=[rpbh])
            if q4 % 2 == 0:
                kb.op("act", lambda e, q4=q4, psb=psb: e.copy(
                    out=S_sb[:, q4 * 512:(q4 + 1) * 512], in_=psb[:, 0:512]),
                    ins=[rpbh], outs=[rS])
            else:
                kb.op("dve", lambda e, q4=q4, psb=psb: e.tensor_copy(
                    out=S_sb[:, q4 * 512:(q4 + 1) * 512], in_=psb[:, 0:512]),
                    ins=[rpbh], outs=[rS])
            kb.op("io", lambda e, q4=q4: e.dma_start(
                out=c_stage[0:1, q4 * 512:(q4 + 1) * 512],
                in_=S_sb[127:128, q4 * 512:(q4 + 1) * 512]),
                ins=[rS], outs=[rcst4[q4]], sem="row")
        kb.op("dve", lambda e, yp=yp: e.tensor_mul(out=yp[:], in0=q_bf[:], in1=S_sb[:]),
              ins=[rqb, rS], outs=[ryp])

    def a2_s1b(c):
        xt, rxt = xb[c % 2]
        yp, ryp = ysb[c % 2]
        gp_, rgp = gsb[c % 2]
        kb.op("io", lambda e, c=c, gp_=gp_: e.dma_start(out=gp_[:], in_=s_g[c]),
              ins=[r_sg[c]], outs=[rgp], sem="lg")
        kb.op("gp", lambda e, c=c, xt=xt: e.dma_start(
            out=xt[:], in_=x_own[c * 128:(c + 1) * 128, :]), outs=[rxt], sem="x0")
        (rst, trst), (nmr_, tnmr) = ln_stats(ryp, yp, z_bf, rz, colsA)
        kb.op("act", lambda e, rst=rst, nmr_=nmr_, yp=yp: e.activation(
            out=z_bf[:], in_=yp[:], func=AF.Identity,
            scale=rst[:, 0:1], bias=nmr_[:, 0:1]),
            ins=[ryp, trst, tnmr], outs=[rz])
        kb.op("dve", lambda e, yp=yp: e.tensor_mul(out=yp[:], in0=z_bf[:], in1=lng_bc[:]),
              ins=[rz, rlng], outs=[ryp])
        kb.op("dve", lambda e, yp=yp: e.tensor_add(out=z_bf[:], in0=yp[:], in1=lnb_bc[:]),
              ins=[ryp, rlnb], outs=[rz])
        kb.op("dve", lambda e, gp_=gp_, yp=yp: e.tensor_mul(out=yp[:], in0=z_bf[:], in1=gp_[:]),
              ins=[rz, rgp], outs=[ryp])
        transpose16(yp, ryp, zT, rzT)

        def evac_o(cg, ps, rps):
            kb.op("dve", lambda e, cg=cg, ps=ps, xt=xt: e.tensor_tensor(
                out=xt[:, cg * 512:(cg + 1) * 512], in0=ps[:, 0:512],
                in1=xt[:, cg * 512:(cg + 1) * 512], op=ALU.add),
                ins=[rps, rxt], outs=[rxt])
        big_mm(zT, rzT, WB, rWBq, evac_o, four_banks=False, banks=(2, 3))
        kb.op("io", lambda e, c=c, xt=xt: e.dma_start(out=s_x1[c], in_=xt[:]),
              ins=[rxt], outs=[r_sx1[c]], sem="sx1")

    def a2_s2a(c):
        xt, rxt = xb[c % 2]
        xlnp, rxlnp = xlns[c % 2]
        xlnTp, rxlnTp = xlnTs[c % 2]
        (rst, trst), (nmr_, tnmr) = ln_stats(rxt, xt, z_bf, rz, colsB)
        kb.op("dve", lambda e, rst=rst, nmr_=nmr_, xlnp=xlnp, xt=xt: e.tensor_scalar(
            out=xlnp[:], in0=xt[:], scalar1=rst[:, 0:1], scalar2=nmr_[:, 0:1],
            op0=ALU.mult, op1=ALU.add),
            ins=[rxt, trst, tnmr], outs=[rxlnp])
        transpose16(xlnp, rxlnp, xlnTp, rxlnTp)
        fns = []
        for kcx in range(KC):
            fns.append(lambda e, kcx=kcx, xlnTp=xlnTp: e.matmul(
                ps_sm[:, 0:8], xlnTp[:, kcx * 128:(kcx + 1) * 128],
                gwT[:, kcx * 8:(kcx + 1) * 8], start=(kcx == 0), stop=(kcx == KC - 1)))
        kb.emit("pe", fns, ins=[rxlnTp, rgwT], outs=[rsm])
        lgp, rlgp = (lg8a, rlg8a) if c % 2 == 0 else (lg8b, rlg8b)
        kb.op("dve", lambda e, lgp=lgp: e.tensor_copy(out=lgp[:], in_=ps_sm[:, 0:8]),
              ins=[rsm], outs=[rlgp])
    def a2_s2b(c):
        xlnp, rxlnp = xlns[c % 2]
        lgp, rlgp = (lg8a, rlg8a) if c % 2 == 0 else (lg8b, rlg8b)
        kb.op("dve", lambda e: e.reduce_max(out=c_lmax[:], in_=lgp[:], axis=AX.X),
              ins=[rlgp], outs=[rlmax])
        kb.op("dve", lambda e: e.tensor_scalar_mul(c_nlmax[:], c_lmax[:], -1.0),
              ins=[rlmax], outs=[rnlmax])
        kb.op("act", lambda e: e.activation(out=pexp8[:], in_=lgp[:], func=AF.Exp,
                                            bias=c_nlmax[:, 0:1], scale=1.0,
                                            accum_out=c_sexp[:]),
              ins=[rlgp, rnlmax], outs=[rpexp8, rsexp])
        kb.op("dve", lambda e: e.reciprocal(out=c_rsum[:], in_=c_sexp[:]),
              ins=[rsexp], outs=[rrsum])
        kb.op("dve", lambda e: e.tensor_scalar(out=oh8[:], in0=lgp[:],
                                               scalar1=c_lmax[:, 0:1], scalar2=None,
                                               op0=ALU.is_ge),
              ins=[rlgp, rlmax], outs=[roh8])
        kb.op("dve", lambda e: e.tensor_tensor_scan(out=t8a[:], data0=ones8[:], data1=oh8[:],
                                                    initial=0.0, op0=ALU.mult, op1=ALU.add),
              ins=[rones8, roh8], outs=[rt8a])
        kb.op("dve", lambda e: e.tensor_sub(out=t8a[:], in0=t8a[:], in1=oh8[:]),
              ins=[rt8a, roh8], outs=[rt8a])
        kb.op("dve", lambda e: e.tensor_single_scalar(out=t8a[:], in_=t8a[:], scalar=0.0,
                                                      op=ALU.is_equal),
              ins=[rt8a], outs=[rt8a])
        kb.op("dve", lambda e: e.tensor_mul(out=oh8[:], in0=oh8[:], in1=t8a[:]),
              ins=[roh8, rt8a], outs=[roh8])
        kb.op("act", lambda e: e.copy(out=oh8b[:], in_=oh8[:]), ins=[roh8], outs=[roh8b])
        kb.op("dve", lambda e: e.tensor_mul(out=t8b[:], in0=oh8[:], in1=pexp8[:]),
              ins=[roh8, rpexp8], outs=[rt8b])
        kb.op("dve", lambda e: e.reduce_sum(out=c_ptop[:], in_=t8b[:], axis=AX.X),
              ins=[rt8b], outs=[rptop])
        kb.op("dve", lambda e: e.tensor_mul(out=c_w[:], in0=c_ptop[:], in1=c_rsum[:]),
              ins=[rptop, rrsum], outs=[rcw])
        kb.emit("pe", [lambda e: e.matmul(ps_sm[:, 8:16], slt[:], oh8b[:],
                                          start=True, stop=True)],
                ins=[rslt, roh8b], outs=[rsm])
        kb.emit("pe", [lambda e: e.matmul(ps_sm[:, 16:24], onesr_f[0:1, :], runcnt[0:1, :],
                                          start=True, stop=True)],
                ins=[ronesr, rruncnt], outs=[rsm])
        kb.op("dve", lambda e: e.tensor_mul(out=t8a[:], in0=oh8[:], in1=ps_sm[:, 8:16]),
              ins=[roh8, rsm], outs=[rt8a])
        kb.op("dve", lambda e: e.reduce_sum(out=c_rk1[:], in_=t8a[:], axis=AX.X),
              ins=[rt8a], outs=[rrk1])
        kb.op("dve", lambda e: e.tensor_mul(out=t8b[:], in0=oh8[:], in1=ps_sm[:, 16:24]),
              ins=[roh8, rsm], outs=[rt8b])
        kb.op("dve", lambda e: e.reduce_sum(out=c_rk2[:], in_=t8b[:], axis=AX.X),
              ins=[rt8b], outs=[rrk2])
        kb.op("dve", lambda e: e.tensor_add(out=c_rank[:], in0=c_rk1[:], in1=c_rk2[:]),
              ins=[rrk1, rrk2], outs=[rrank])
        kb.op("dve", lambda e: e.tensor_mul(out=t8a[:], in0=oh8[:], in1=iota8bc[:]),
              ins=[roh8, riota8], outs=[rt8a])
        kb.op("dve", lambda e: e.reduce_sum(out=c_et[:], in_=t8a[:], axis=AX.X),
              ins=[rt8a], outs=[rcet])
        kb.op("dve", lambda e: e.tensor_single_scalar(out=c_keep[:], in_=c_rank[:],
                                                      scalar=float(PCAP), op=ALU.is_lt),
              ins=[rrank], outs=[rkeep])
        kb.op("dve", lambda e, c=c: e.tensor_scalar(out=wtok[:, c:c + 1], in0=c_w[:],
                                                    scalar1=c_keep[:, 0:1], scalar2=None,
                                                    op0=ALU.mult),
              ins=[rcw, rkeep], outs=[rwtok])
        kb.op("dve", lambda e: e.tensor_scalar(out=c_sf[:], in0=c_et[:], scalar1=float(PCAP),
                                               scalar2=c_rank[:, 0:1], op0=ALU.mult,
                                               op1=ALU.add),
              ins=[rcet, rrank], outs=[rcsf])
        kb.op("dve", lambda e: e.tensor_scalar(out=c_sk[:], in0=c_sf[:],
                                               scalar1=c_keep[:, 0:1], scalar2=None,
                                               op0=ALU.mult),
              ins=[rcsf, rkeep], outs=[rcsk])
        kb.op("dve", lambda e: e.tensor_scalar(out=c_bigk[:], in0=c_keep[:], scalar1=-BIG,
                                               scalar2=BIG, op0=ALU.mult, op1=ALU.add),
              ins=[rkeep], outs=[rbigk])
        kb.op("dve", lambda e: e.tensor_add(out=c_sm[:], in0=c_sk[:], in1=c_bigk[:]),
              ins=[rcsk, rbigk], outs=[rcsm])
        kb.op("dve", lambda e, c=c: e.tensor_copy(out=slots[:, c:c + 1], in_=c_sm[:]),
              ins=[rcsm], outs=[rslots])
        kb.emit("pe", [lambda e: e.matmul(ps_sm[0:1, 24:32], onesc_bf[:], oh8b[:],
                                          start=True, stop=True)],
                ins=[ronesc, roh8b], outs=[rsm])
        kb.op("dve", lambda e: e.tensor_copy(out=cnt8[:], in_=ps_sm[0:1, 24:32]),
              ins=[rsm], outs=[rcnt8])
        kb.op("dve", lambda e: e.tensor_add(out=runcnt[:], in0=runcnt[:], in1=cnt8[:]),
              ins=[rcnt8, rruncnt], outs=[rruncnt])
        tok = kb.op("gp", lambda e, c=c, xlnp=xlnp: e.indirect_dma_start(
            out=s_disp, out_offset=IOA(ap=slots[:, c:c + 1], axis=0),
            in_=xlnp[:], in_offset=None,
            bounds_check=NSLOT - 1, oob_is_err=False),
            ins=[rxlnp, rslots], sem="ind")
        r_disp.writers.append(tok)

    a2_s1a(0)
    a2_s1a(1)
    a2_s1b(0)
    for c in range(2, NCH):
        a2_s1a(c)
        a2_s2a(c - 2)
        a2_s1b(c - 1)
        a2_s2b(c - 2)
    a2_s2a(NCH - 2)
    a2_s1b(NCH - 1)
    a2_s2b(NCH - 2)
    a2_s2a(NCH - 1)
    a2_s2b(NCH - 1)

    seeds = [t for rr in (xlns + xlnTs + vsb + gsb + ysb) for t in rr[1].toks()] + \
        [t for r in (rq, rx1, rS, rz, rzT, rAt, rProw, rlng, rlnb, rcst, *rcst4)
         for t in r.toks()]
    a12_ctx.close()
    attn_ctx.close()

    a3_ctx = ExitStack()
    DeTs = [SA(a3_ctx, "DeT0", [128, KC * PCAP], BF16),
            SA(a3_ctx, "DeT1", [128, KC * PCAP], BF16)]
    rDeTq = [[Rec(f"DeT{b}q{qt}") for qt in range(4)] for b in range(2)]
    hmTs = [SA(a3_ctx, "hmT0", [128, KC * PCAP], BF16),
            SA(a3_ctx, "hmT1", [128, KC * PCAP], BF16)]
    eo_tm, reotm = SA(a3_ctx, "eo_tm", [128, D], BF16)
    ffnb = [SA(a3_ctx, "ffn0", [128, D], BF16), SA(a3_ctx, "ffn1", [128, D], BF16)]
    xb4 = [SA(a3_ctx, "x4a", [128, D], BF16), SA(a3_ctx, "x4b", [128, D], BF16)]

    class FView:
        def __init__(self, t):
            self.t = t
        def __getitem__(self, key):
            assert key == slice(None)
            return self.t[:, 0:2 * D].bitcast(F32)

    def a3_deT_loads(ei):
        DeT_bf, _ = DeTs[ei % 2]
        for kcx in range(KC):
            kb.op("io", lambda e, ei=ei, kcx=kcx, DeT_bf=DeT_bf: e.dma_start(
                out=DeT_bf[:, kcx * PCAP:(kcx + 1) * PCAP],
                in_=s_disp[ei * PCAP:(ei + 1) * PCAP, kcx * 128:(kcx + 1) * 128],
                transpose=True),
                ins=[r_disp], outs=[rDeTq[ei % 2][kcx // 4]], sem="de")

    def a3_mm1(ei):
        DeT_bf, _ = DeTs[ei % 2]
        hmT, rhmT = hmTs[ei % 2]
        for ic in range(KC):
            ps, rps = pss4[ic % 4]
            for qt in range(4):
                fns = []
                for kcq in range(4):
                    kcx = qt * 4 + kcq
                    fns.append(lambda e, ic=ic, kcx=kcx, ps=ps, DeT_bf=DeT_bf: e.matmul(
                        ps[:, 0:PCAP],
                        WA[:, kcx * D + ic * 128: kcx * D + (ic + 1) * 128],
                        DeT_bf[:, kcx * PCAP:(kcx + 1) * PCAP],
                        start=(kcx == 0), stop=(kcx == KC - 1)))
                kb.emit("pe", fns, ins=[rDeTq[ei % 2][qt], rWAq[qt]], outs=[rps])
            kb.op("act", lambda e, ic=ic, ps=ps, hmT=hmT: e.activation(
                out=hmT[:, ic * PCAP:(ic + 1) * PCAP],
                in_=ps[:, 0:PCAP], func=AF.Gelu),
                ins=[rps], outs=[rhmT])

    def a3_mm2(ei):
        hmT, rhmT = hmTs[ei % 2]
        for sg in range(3):
            for qt in range(4):
                fns = []
                for kcq in range(4):
                    kcx = qt * 4 + kcq
                    for cg in range(4):
                        ps, rps = pss4[cg]
                        fns.append(lambda e, sg=sg, kcx=kcx, cg=cg, ps=ps, hmT=hmT: e.matmul(
                            ps[:, 0:512],
                            hmT[:, kcx * PCAP + sg * 128: kcx * PCAP + (sg + 1) * 128],
                            WB[:, kcx * D + cg * 512: kcx * D + (cg + 1) * 512],
                            start=(kcx == 0), stop=(kcx == KC - 1)))
                kb.emit("pe", fns, ins=[rhmT, rWBq[qt]],
                        outs=[pss4[0][1], pss4[1][1], pss4[2][1], pss4[3][1]])
            for cg in range(4):
                ps, rps = pss4[cg]
                kb.op("dve", lambda e, cg=cg, ps=ps: e.tensor_copy(
                    out=eo_tm[:, cg * 512:(cg + 1) * 512], in_=ps[:, 0:512]),
                    ins=[rps], outs=[reotm])
            kb.op("act", lambda e, ei=ei, sg=sg: e.dma_start(
                out=s_eout[ei * PCAP + sg * 128: ei * PCAP + (sg + 1) * 128, :],
                in_=eo_tm[:]),
                ins=[reotm], outs=[r_eout], sem="seo")

    for qt in range(4):
        kb.op("gp", lambda e, qt=qt: e.dma_start(
            out=WA[:, qt * 4 * D:(qt + 1) * 4 * D],
            in_=w1_d[0][:, qt * 4 * D:(qt + 1) * 4 * D]),
            outs=[rWAq[qt]], sem="wa")
        kb.op("gp", lambda e, qt=qt: e.dma_start(
            out=WB[:, qt * 4 * D:(qt + 1) * 4 * D],
            in_=w2_d[0][:, qt * 4 * D:(qt + 1) * 4 * D]),
            outs=[rWBq[qt]], sem="wb")
    a3_deT_loads(0)
    for ei in range(8):
        a3_mm1(ei)
        if ei < 7:
            a3_deT_loads(ei + 1)
            for qt in range(4):
                kb.op("gp", lambda e, ei=ei, qt=qt: e.dma_start(
                    out=WA[:, qt * 4 * D:(qt + 1) * 4 * D],
                    in_=w1_d[ei + 1][:, qt * 4 * D:(qt + 1) * 4 * D]),
                    outs=[rWAq[qt]], sem="wa")
        if ei >= 1:
            a3_mm2(ei - 1)
            for qt in range(4):
                kb.op("gp", lambda e, ei=ei, qt=qt: e.dma_start(
                    out=WB[:, qt * 4 * D:(qt + 1) * 4 * D],
                    in_=w2_d[ei][:, qt * 4 * D:(qt + 1) * 4 * D]),
                    outs=[rWBq[qt]], sem="wb")
    a3_mm2(7)

    ofb = [(FView(hmTs[0][0]), Rec("of0", seed=hmTs[0][1].toks())),
           (FView(hmTs[1][0]), Rec("of1", seed=hmTs[1][1].toks()))]
    kb.op("dve", lambda e: e.memset(ffnb[0][0][:], 0.0), outs=[ffnb[0][1]])
    kb.op("dve", lambda e: e.memset(ffnb[1][0][:], 0.0), outs=[ffnb[1][1]])
    for cc2 in range(NCH):
        r_sx1[cc2].writers = [("sx1", kb.sems["sx1"][1])]
    r_eout.writers = [("seo", kb.sems["seo"][1])]
    for c in range(NCH):
        xt, rxt = xb4[c % 2]
        of, rof = ofb[c % 2]
        ffn_bf, rffn = ffnb[c % 2]
        kb.op("io", lambda e, c=c, xt=xt: e.dma_start(out=xt[:], in_=s_x1[c]),
              ins=[r_sx1[c]], outs=[rxt], sem="x0")
        kb.op("gp", lambda e, c=c, ffn_bf=ffn_bf: e.indirect_dma_start(
            out=ffn_bf[:], out_offset=None,
            in_=s_eout, in_offset=IOA(ap=slots[:, c:c + 1], axis=0),
            bounds_check=NSLOT - 1, oob_is_err=False),
            ins=[r_eout, rslots], outs=[rffn], sem="ga")
        kb.op("dve", lambda e, c=c, xt=xt, of=of, ffn_bf=ffn_bf: e.scalar_tensor_tensor(
            out=of[:], in0=ffn_bf[:], scalar=wtok[:, c:c + 1], in1=xt[:],
            op0=ALU.mult, op1=ALU.add),
            ins=[rffn, rwtok, rxt], outs=[rof])
        kb.op("io", lambda e, c=c, of=of: e.dma_start(
            out=y_out[c * 128:(c + 1) * 128, :], in_=of[:]),
            ins=[rof], outs=[], sem=("out0" if c % 2 == 0 else "out1"))
    seeds = [t for rr in (DeTs + hmTs) for t in rr[1].toks()] + \
        [t for r in (reotm,) for t in r.toks()]
    a3_ctx.close()

    nc.sync.wait_ge(kb.sems["out0"][0], kb.sems["out0"][1])
    nc.sync.wait_ge(kb.sems["out1"][0], kb.sems["out1"][1])
    ctx.close()
    return nc


_PROGRAM_CACHE = {}


def _enable_ldw_opt():
    try:
        from concourse.compiler_utils import get_compiler_flags, set_compiler_flags
        flags = [f.replace("--enable-ldw-opt=false", "--enable-ldw-opt=true")
                 for f in get_compiler_flags()]
        set_compiler_flags(flags)
    except Exception:
        pass


def _get_program():
    if "nc" not in _PROGRAM_CACHE:
        _PROGRAM_CACHE["nc"] = build_program()
    return _PROGRAM_CACHE["nc"]


def _host_w_layout(w):
    n = w.shape[1]
    return np.ascontiguousarray(
        w.reshape(KC, 128, n).transpose(1, 0, 2).reshape(128, KC * n))


def kernel(L0_attn_x, L0_attn_Wq, L0_attn_Wv, L0_attn_Wg, L0_attn_Wo, L0_attn_beta,
           L0_attn_ln_g, L0_attn_ln_b, L0_ffn_gate_weight, L0_ffn_w1, L0_ffn_w2):
    B, Sq, Dm = L0_attn_x.shape
    T = B * Sq
    xf = np.ascontiguousarray(np.asarray(L0_attn_x, dtype=np.float32).reshape(T, Dm))

    wq = _host_w_layout(np.asarray(L0_attn_Wq, dtype=np.float32))
    wv = _host_w_layout(np.asarray(L0_attn_Wv, dtype=np.float32))
    wg = _host_w_layout(np.asarray(L0_attn_Wg, dtype=np.float32))
    wo = _host_w_layout(np.asarray(L0_attn_Wo, dtype=np.float32))
    w1 = np.stack([_host_w_layout(np.asarray(L0_ffn_w1[e], dtype=np.float32))
                   for e in range(8)])
    w2 = np.stack([_host_w_layout(np.asarray(L0_ffn_w2[e], dtype=np.float32))
                   for e in range(8)])
    gw = np.asarray(L0_ffn_gate_weight, dtype=np.float32)
    gwT = np.ascontiguousarray(
        np.ascontiguousarray(gw.T).reshape(KC, 128, 8).transpose(1, 0, 2).reshape(128, KC * 8))

    beta = np.asarray(L0_attn_beta, dtype=np.float32).reshape(1, H)
    lng = np.asarray(L0_attn_ln_g, dtype=np.float32).reshape(1, Dm)
    lnb = np.asarray(L0_attn_ln_b, dtype=np.float32).reshape(1, Dm)

    ident = np.eye(128, dtype=np.float32).astype(ml_dtypes.bfloat16)
    sidx = np.arange(128)[:, None]
    tidx = np.arange(128)[None, :]
    tri1 = np.where(sidx <= tidx, (tidx - sidx).astype(np.float32), 0.0).astype(np.float32)
    tri16 = np.tile(tri1, (1, H))
    mask16 = np.tile((sidx <= tidx).astype(np.float32), (1, H)).astype(ml_dtypes.bfloat16)
    slt = (sidx < tidx).astype(np.float32).astype(ml_dtypes.bfloat16)
    iota = np.arange(128, dtype=np.float32).reshape(128, 1)
    iota8 = np.arange(8, dtype=np.float32).reshape(1, 8)
    tplus1 = np.tile(np.arange(1, 129, dtype=np.float32), H).reshape(1, H * 128)

    nc = _get_program()
    in_maps = []
    cpb = NCORES // B
    for k in range(NCORES):
        e8 = np.zeros((8, 1), dtype=np.float32)
        m8 = np.zeros((8, 1), dtype=np.float32)
        for j in range(NCORES):
            if j // cpb == k // cpb and j < k:
                e8[j, 0] = 2048.0 * (k - 1 - j)
                m8[j, 0] = 1.0
        in_maps.append(dict(
            x_own=xf[k * TOK:(k + 1) * TOK],
            wq=wq, wv=wv, wg=wg, wo=wo, w1=w1, w2=w2, gwt=gwT,
            beta=beta, lng=lng, lnb=lnb,
            ident=ident, tri16=tri16, mask16=mask16, slt=slt, iota=iota,
            iota8=iota8, tplus1=tplus1, e8=e8, m8=m8,
        ))
    res = run_bass_kernel_spmd(nc, in_maps, core_ids=list(range(NCORES)))
    out = np.concatenate([res.results[c]["y_out"] for c in range(NCORES)], axis=0)
    return out.reshape(B, Sq, Dm)
